# version 22
# baseline (speedup 1.0000x reference)
import contextlib
import numpy as np
import ml_dtypes
import concourse.bass as bass
import concourse.mybir as mybir
from concourse.bass_utils import run_bass_kernel_spmd
from concourse.bass import IndirectOffsetOnAxis

F32 = mybir.dt.float32
BF16 = mybir.dt.bfloat16
U32 = mybir.dt.uint32
U8 = mybir.dt.uint8
AF = mybir.ActivationFunctionType
ALU = mybir.AluOpType
AX = mybir.AxisListType.X

NCORE = 8
TOK = 4096
NG = 32
NQ = 4
D = 1024
L = 256
E = 8
FF = 2048
CAP = 2048
PACKW = 16 + E * 8192
NROWX = 32768 + 32
ZROW = 32768.0
BIG = float(2 ** 24)


class Sem:
    def __init__(self, h):
        self.h = h
        self.v = 0


class StubInst:
    def then_inc(self, *a, **k):
        return self


class StubEng:
    def __getattr__(self, name):
        def f(*a, **k):
            return StubInst()
        return f


class Em:

    def __init__(self):
        self.MS = {}
        self.dry = True

    def inc(self, inst, sem, n=1):
        sem.v += n
        if not self.dry:
            inst.then_inc(sem.h, n)
        return inst

    def mark(self, name, sem):
        if self.dry:
            assert name not in self.MS, f"dup mark {name}"
            self.MS[name] = (sem, sem.v)
        else:
            assert self.MS[name][1] == sem.v, (name, self.MS[name][1], sem.v)

    def wait(self, eng, name):
        if not self.dry:
            sem, v = self.MS[name]
            eng.wait_ge(sem.h, v)

    def se(self, eng, sem, inst):
        sem.v += 1
        if not self.dry:
            inst.then_inc(sem.h, 1)
            eng.wait_ge(sem.h, sem.v)


_DBG = {}

ATT_POINTS = [(0, fc) for fc in range(7, 16)] + \
             [(qq, fc) for qq in (1, 2, 3) for fc in range(16)]
ATT_UNITS = {}
_u = 0
for _i, _pt in enumerate(ATT_POINTS):
    _n = 2 if _i < 7 else 1
    ATT_UNITS[_pt] = tuple(range(_u, _u + _n))
    _u += _n
assert _u == 64
ATT_EO_WAIT = {16: 2, 17: 2, 32: 6, 34: 6, 49: 10, 50: 10}


def build_nc(debug=False):
    nc = bass.Bass("TRN2", target_bir_lowering=False)

    xown = nc.dram_tensor("xown", [TOK, D], F32, kind="ExternalInput").ap()
    xfull = nc.dram_tensor("xfull", [NROWX, D], F32, kind="ExternalInput").ap()
    wthT = nc.dram_tensor("wthT", [D, L], F32, kind="ExternalInput").ap()
    wphT = nc.dram_tensor("wphT", [D, L], F32, kind="ExternalInput").ap()
    wgT = nc.dram_tensor("wgT", [D, D], F32, kind="ExternalInput").ap()
    wgate = nc.dram_tensor("wgate", [D, E], F32, kind="ExternalInput").ap()
    w0e = nc.dram_tensor("w0e", [D, FF], F32, kind="ExternalInput").ap()
    w1e = nc.dram_tensor("w1e", [D, FF], F32, kind="ExternalInput").ap()
    woe = nc.dram_tensor("woe", [FF, D], F32, kind="ExternalInput").ap()
    ccon = nc.dram_tensor("ccon", [128, 8], F32, kind="ExternalInput").ap()
    ccol8 = nc.dram_tensor("ccol8", [8, 2], F32, kind="ExternalInput").ap()
    cemask = nc.dram_tensor("cemask", [128, E], F32, kind="ExternalInput").ap()
    out = nc.dram_tensor("out", [TOK, D], F32, kind="ExternalOutput").ap()

    dbg = {}
    if debug:
        for name, shape, dt in (
            ("d_gate", [128, NG * E], F32), ("d_sel", [128, 2 * NG], F32),
            ("d_rk", [128, 2 * NG], F32), ("d_prob", [128, 2 * NG], F32),
            ("d_ctbl", [8, 8], F32), ("d_gidx", [128, 2 * NG], U32),
            ("d_tid", [128, 16], U32), ("d_eout", [CAP, D], BF16),
            ("d_mb", [128, 2 * D], BF16), ("d_th", [128, 2 * TOK], BF16),
        ):
            dbg[name] = nc.dram_tensor(name, shape, dt, kind="ExternalOutput").ap()

    packsrc = nc.dram_tensor("packsrc", [PACKW, 1], F32)
    agout = nc.dram_tensor("agout", [NCORE * PACKW, 1], F32)
    marsrc = nc.dram_tensor("marsrc", [L, D], F32)
    mardst = nc.dram_tensor("mardst", [L, D], F32)
    eoutd = nc.dram_tensor("eoutd", [CAP, D], BF16)
    ag2out = nc.dram_tensor("ag2out", [4 * NCORE * 512, D], BF16)
    partd = nc.dram_tensor("partd", [TOK, D], F32)
    RGRP = [list(range(NCORE))]
    RGRP2 = [[0, 1, 2, 3], [4, 5, 6, 7]]

    ident_bf = np.eye(128, dtype=np.float32).astype(ml_dtypes.bfloat16)
    ltri_np = (np.arange(128)[:, None] < np.arange(128)[None, :]).astype(np.float32)
    ltri8_np = (np.arange(8)[:, None] < np.arange(8)[None, :]).astype(np.float32)
    onesr_np = np.ones((1, 128), np.float32)
    onesc_np = np.ones((128, 1), np.float32)
    io8_np = np.tile(np.arange(E, dtype=np.float32), (128, 8))
    rv8_np = 7.0 - io8_np
    kct_np = np.tile(np.array([[1e-5, -1.0, 1.0, 0.0, 0.5, 2.0, 0.0, 0.0]], np.float32), (128, 1))

    es = contextlib.ExitStack()
    with es:
        def sb(name, shape, dt):
            return es.enter_context(nc.sbuf_tensor(name, shape, dt))

        def psum(name, shape, dt=F32):
            return es.enter_context(nc.psum_tensor(name, shape, dt))

        def mksem(name):
            return Sem(es.enter_context(nc.semaphore(name)))

        K_cc = sb("K_cc", [128, 8], F32)
        K_ct = sb("K_ct", [128, 8], F32)
        K_c8 = sb("K_c8", [8, 2], F32)
        K_em = sb("K_em", [128, E], F32)
        K_id = sb("K_id", [128, 128], BF16)
        K_lt = sb("K_lt", [128, 128], F32)
        K_lt8 = sb("K_lt8", [8, 8], F32)
        K_or = sb("K_or", [1, 128], F32)
        K_oc = sb("K_oc", [128, 1], F32)
        K_io = sb("K_io", [128, 64], F32)
        K_rv = sb("K_rv", [128, 64], F32)
        GTID = sb("GTID", [128, NG], F32)
        RALL = sb("RALL", [128, 16], F32)
        W_th = sb("W_thP", [128, 8 * L], BF16)
        W_ph = sb("W_phP", [128, 8 * L], BF16)
        W_ga = sb("W_gaP", [128, 8 * E], BF16)
        W_0 = sb("W_0", [128, 8 * FF], BF16)
        W_1 = sb("W_1", [128, 8 * FF], BF16)
        W_o = sb("W_o", [128, 16 * D], BF16)
        TH = sb("TH", [128, 2 * TOK], BF16)
        GATE = sb("GATE", [128, NG * E], F32)
        SEL1 = sb("SEL1", [128, NG], F32)
        SEL2 = sb("SEL2", [128, NG], F32)
        P1V = sb("P1V", [128, NG], F32)
        P2V = sb("P2V", [128, NG], F32)
        RK1 = sb("RK1", [128, NG], F32)
        RK2 = sb("RK2", [128, NG], F32)
        GI1 = sb("GI1", [128, NG], U32)
        GI2 = sb("GI2", [128, NG], U32)
        DST1 = sb("DST1", [128, NG], U32)
        DST2 = sb("DST2", [128, NG], U32)
        RBASE = sb("RBASE", [1, E], F32)
        MB = sb("MB", [128, 2 * D], BF16)
        CTBL = sb("CTBL", [8, 8], F32)
        SM1 = sb("SM1", [8, 8], F32)
        CNTE = sb("CNTE", [8, 1], F32)
        PROWS = sb("PROWS", [1, 32], F32)
        PBV = sb("PBV", [128, 24], F32)
        RS_M1 = sb("RS_M1", [128, 64], F32)
        RS_M2 = sb("RS_M2", [128, 64], F32)
        RS_T = sb("RS_T", [128, 64], F32)
        RS_G2 = sb("RS_G2", [128, 64], F32)
        RS_S = sb("RS_S", [128, 64], F32)
        RS_A = sb("RS_A", [128, 8], F32)
        RS_B = sb("RS_B", [128, 8], F32)
        RS_R = sb("RS_R", [1, 64], F32)
        RS_R2 = sb("RS_R2", [1, 64], F32)
        RS_BQ = sb("RS_BQ", [1, 64], F32)
        DSF = sb("DSF", [128, 8], F32)
        STAT = sb("STAT", [128, 16], F32)
        SNK = sb("SNK", [128, D], F32)
        SRCU = sb("SRCU", [128, 16], U32)
        TIDF = sb("TIDF", [128, 16], F32)
        TIDU = sb("TIDU", [128, 16], U32)
        VLD = sb("VLD", [128, 16], F32)
        INTER = sb("INTER", [128, 16 * 512], BF16)
        SIL = sb("SIL", [128, 512], BF16)
        MAR = sb("MAR", [128, D], F32)
        T1S = sb("T1S", [128, D], F32)

        ARENA = sb("ARENA", [128, 46080], U8)
        AR_F = ARENA.bitcast(F32)
        AR_B = ARENA.bitcast(BF16)

        class Carver:
            def __init__(self):
                self.off = 0

            def take(self, cols, dt):
                sz = 4 if dt in (F32, U32) else 2
                self.off = (self.off + 63) // 64 * 64
                st = self.off // sz
                h = AR_F if dt == F32 else AR_B
                ap = h.ap()[:, st:st + cols]
                self.off += cols * sz
                assert self.off <= 46080, self.off
                return ap

        cv = Carver()
        XT = cv.take(3 * D, F32)
        XNB = cv.take(4 * D, BF16)
        XF = cv.take(2 * D, BF16)
        PHS = cv.take(2 * L, BF16)
        GES = cv.take(2 * D, BF16)
        W_gg = cv.take(8 * D, BF16)

        cv = Carver()
        XRAW = cv.take(2 * D, F32)
        XDN = cv.take(4 * D, BF16)
        DFB = cv.take(2 * 8 * 512, BF16)
        MDP = cv.take(2 * D, F32)
        cv = Carver()
        XC = cv.take(4 * D, F32)
        RB1 = W_1[:].__getitem__((slice(None), slice(0, 8 * D)))
        RB2 = W_o[:].__getitem__((slice(None), slice(0, 8 * D)))
        OAC = W_0.bitcast(F32).ap()[:, 0:4 * D]
        EO = AR_B.ap()[:, 20480:20480 + 2 * D]

        def EOSLOT(gs):
            if gs < 8:
                return EO[:, (gs % 2) * D:(gs % 2 + 1) * D]
            th_off = {8: 0, 9: 1024, 10: 4096, 11: 5120,
                      12: 2048, 13: 3072, 14: 6144, 15: 7168}[gs]
            return TH[:, th_off:th_off + D]
        OAC2 = AR_F.ap()[:, 4 * D:8 * D]
        XC2 = W_o.bitcast(F32).ap()[:, 4 * D:8 * D]
        MDPS = (MDP[:, 0:D], MDP[:, D:2 * D], T1S[:], MAR[:])

        PTR = psum("PTR", [128, 1024], BF16)
        CH = psum("CH", [128, 512], F32)
        BG0 = psum("BG0", [128, 512], F32)
        BG1 = psum("BG1", [128, 512], F32)
        PM0 = psum("PM0", [128, 512], F32)
        PM1 = psum("PM1", [128, 512], F32)
        PM2 = psum("PM2", [128, 512], F32)
        PM3 = psum("PM3", [128, 512], F32)

        sX = mksem("sX")
        sX1 = mksem("sX1")
        sX2 = mksem("sX2")
        sX3 = mksem("sX3")
        sWb = mksem("sWb")
        sR0 = mksem("sR0")
        sR1 = mksem("sR1")
        sR2 = mksem("sR2")
        sR3 = mksem("sR3")
        sR4 = mksem("sR4")
        sR5 = mksem("sR5")
        sR6 = mksem("sR6")
        sR7 = mksem("sR7")
        sST1 = mksem("sST1")
        sD4 = [mksem(f"sD{j}") for j in range(4)]
        sK = mksem("sK")
        sW = mksem("sW")
        sV = mksem("sV")
        sA = mksem("sA")
        sT = mksem("sT")
        sG = mksem("sG")
        sI = mksem("sI")
        sC = mksem("sC")
        sE = mksem("sE")
        sE1 = mksem("sE1")
        sE2 = mksem("sE2")
        sE3 = mksem("sE3")
        sST = mksem("sST")
        sPd = mksem("sPd")
        sPd1 = mksem("sPd1")
        sPd2 = mksem("sPd2")
        sPd3 = mksem("sPd3")
        sST2 = mksem("sST2")
        sST3 = mksem("sST3")
        sST4 = mksem("sST4")
        sST5 = mksem("sST5")
        sST6 = mksem("sST6")
        sST7 = mksem("sST7")
        sMd = mksem("sMd")
        sPc = [mksem(f"sPc{j}") for j in range(8)]
        ALLSEMS = [sX, sX1, sX2, sX3, sK, sW, sWb, sR0, sR1, sR2, sR3, sR4, sR5, sR6, sR7, sST1, sST2, sST3, sST4, sST5, sST6, sST7, sV, sA, sT, sG, sI, sC, sE, sE1, sE2, sE3, sST, sPd, sPd1, sPd2, sPd3, sMd] + sD4 + sPc

        block = es.enter_context(nc.Block())

        identt = nc.inline_tensor(ident_bf, "identt")
        ltrit = nc.inline_tensor(ltri_np, "ltrit")
        ltri8t = nc.inline_tensor(ltri8_np, "ltri8t")
        onesrt = nc.inline_tensor(onesr_np, "onesrt")
        onesct = nc.inline_tensor(onesc_np, "onesct")
        io8t = nc.inline_tensor(io8_np, "io8t")
        rv8t = nc.inline_tensor(rv8_np, "rv8t")
        kctt = nc.inline_tensor(kct_np, "kctt")
        for i, val in enumerate([1e-5, -1.0, 1.0, 0.0]):
            nc.const_aps.register(F32, val, K_ct[:, i:i + 1])

        em = Em()

        def t3(ap, e=E):
            return ap.rearrange("p (t e) -> p t e", e=e)

        def col3(ap):
            return ap.rearrange("p (t o) -> p t o", o=1)

        def body_sync(sync):
            for dst, src in ((K_cc[:], ccon), (K_c8[:], ccol8), (K_em[:], cemask),
                             (K_id[:], identt.ap()), (K_lt[:], ltrit.ap()),
                             (K_lt8[:], ltri8t.ap()), (K_or[:], onesrt.ap()),
                             (K_oc[:], onesct.ap()), (K_io[:], io8t.ap()),
                             (K_rv[:], rv8t.ap()), (K_ct[:], kctt.ap())):
                em.inc(sync.dma_start(out=dst, in_=src), sK, 16)
            em.mark("k_done", sK)

            def xload(g):
                if g >= 3:
                    em.wait(sync, f"xnb{g - 3}")
                xls = (sX, sX1, sX2)[g % 3]
                em.inc(sync.dma_start(out=XT[:, (g % 3) * D:(g % 3 + 1) * D],
                                      in_=xown[g * 128:(g + 1) * 128, :]),
                       xls, 16)
                em.mark(f"x{g}", xls)

            for g in range(NG):
                xload(g)
            em.wait(sync, "ag1")
            em.inc(sync.dma_start(
                out=CTBL[:],
                in_=agout.ap().rearrange("(c w) o -> c (w o)", w=PACKW)[0:8, 0:8]), sK, 16)
            em.mark("ctbl", sK)
            for j in range(2):
                em.wait(sync, f"marq{j}")
                em.inc(sync.dma_start(out=marsrc.ap()[j * 128:(j + 1) * 128, :],
                                      in_=MAR[:]), sE, 16)
                em.mark(f"mars{j}", sE)

            def xaload(g):
                if g in (0, 1):
                    em.wait(sync, f"ge{NG - 1}")
                elif g == 3:
                    em.wait(sync, "mbp1")
                elif g >= 4:
                    em.wait(sync, f"pstore{g - 4}")
                xasem = (sX, sX1, sX2, sX3)[g % 4]
                em.inc(sync.dma_start(out=MDPS[g % 4],
                                      in_=xown[g * 128:(g + 1) * 128, :]),
                       xasem, 16)
                em.mark(f"xa{g}", xasem)

            def pstore(g):
                em.wait(sync, f"attevx{2 * g + 1}")
                psem = (sPd, sPd1, sPd2, sPd3)[g % 4]
                em.inc(sync.dma_start(out=partd.ap()[g * 128:(g + 1) * 128, :],
                                      in_=MDPS[g % 4]), psem, 16)
                em.mark(f"pstore{g}", psem)

            xaload(0)
            xaload(1)
            xaload(2)
            for lc in range(2):
                em.wait(sync, "arm")
                if lc == 1:
                    em.wait(sync, "mbp0")
                em.inc(sync.dma_start(out=MAR[:],
                                      in_=mardst.ap()[lc * 128:(lc + 1) * 128, :]), sMd, 16)
                em.mark(f"md{lc}", sMd)
            xaload(3)
            for qq in range(4):
                for g in range(8 * qq, 8 * qq + 8):
                    pstore(g)
                    if g + 4 < NG:
                        xaload(g + 4)
            for g in range(NG):
                em.wait(sync, f"oacd{g}")
                osrc = (OAC if g % 8 < 4 else OAC2)[:, (g % 4) * D:(g % 4 + 1) * D]
                ssem = (sST, sST1, sST2, sST3, sST4, sST5, sST6, sST7)[g % 8]
                em.inc(sync.dma_start(out=out[g * 128:(g + 1) * 128, :], in_=osrc),
                       ssem, 16)
                em.mark(f"outd{g}", ssem)
            if debug:
                em.wait(sync, "oacd31")
                em.inc(sync.dma_start(out=dbg["d_gate"], in_=GATE[:]), sST, 16)
                em.inc(sync.dma_start(out=dbg["d_sel"].rearrange("p (k g) -> p k g", k=2)[:, 0, :], in_=SEL1[:]), sST, 16)
                em.inc(sync.dma_start(out=dbg["d_sel"].rearrange("p (k g) -> p k g", k=2)[:, 1, :], in_=SEL2[:]), sST, 16)
                em.inc(sync.dma_start(out=dbg["d_rk"].rearrange("p (k g) -> p k g", k=2)[:, 0, :], in_=RK1[:]), sST, 16)
                em.inc(sync.dma_start(out=dbg["d_rk"].rearrange("p (k g) -> p k g", k=2)[:, 1, :], in_=RK2[:]), sST, 16)
                em.inc(sync.dma_start(out=dbg["d_prob"].rearrange("p (k g) -> p k g", k=2)[:, 0, :], in_=P1V[:]), sST, 16)
                em.inc(sync.dma_start(out=dbg["d_prob"].rearrange("p (k g) -> p k g", k=2)[:, 1, :], in_=P2V[:]), sST, 16)
                em.inc(sync.dma_start(out=dbg["d_ctbl"], in_=CTBL[:]), sST, 16)
                em.inc(sync.dma_start(out=dbg["d_gidx"].rearrange("p (k g) -> p k g", k=2)[:, 0, :], in_=GI1[:]), sST, 16)
                em.inc(sync.dma_start(out=dbg["d_gidx"].rearrange("p (k g) -> p k g", k=2)[:, 1, :], in_=GI2[:]), sST, 16)
                em.inc(sync.dma_start(out=dbg["d_tid"], in_=TIDU[:]), sST, 16)
                em.inc(sync.dma_start(out=dbg["d_eout"], in_=eoutd.ap()), sST, 16)
                em.inc(sync.dma_start(out=dbg["d_mb"], in_=MB[:]), sST, 16)
                em.inc(sync.dma_start(out=dbg["d_th"], in_=TH[:]), sST, 16)
            if not em.dry:
                for _s in (sST, sST1, sST2, sST3, sST4, sST5, sST6, sST7):
                    sync.wait_ge(_s.h, _s.v)

        def body_tensor(tensor):
            em.wait(tensor, "k_done")
            em.wait(tensor, "wA_done")

            def TR(g):
                em.wait(tensor, f"xnb{g}")
                if g >= 1:
                    em.wait(tensor, f"xf{g - 1}")
                xb = (g % 4) * D
                ti = None
                for k in range(8):
                    ti = tensor.transpose(PTR[:, k * 128:(k + 1) * 128],
                                          XNB[:, xb + k * 128: xb + (k + 1) * 128],
                                          K_id[:])
                em.inc(ti, sT)
                em.mark(f"tr{g}", sT)

            def TG(g):
                em.wait(tensor, f"xf{g}")
                if g >= 1:
                    em.wait(tensor, f"phev{g - 1}")
                if g >= 8 and g % 8 == 0:
                    em.wait(tensor, f"rkread{g // 8 - 1}")
                xf = XF[:, (g % 2) * D:(g % 2 + 1) * D]
                first = True
                ti = None
                for fc in range(8):
                    for lc in range(2):
                        ti = tensor.matmul(CH[:, lc * 128:(lc + 1) * 128],
                                           W_th[:, fc * L + lc * 128: fc * L + lc * 128 + 128],
                                           xf[:, fc * 128:(fc + 1) * 128],
                                           start=first, stop=False)
                        first = False
                    ti = tensor.matmul(CH[:, 256:256 + E],
                                       xf[:, fc * 128:(fc + 1) * 128],
                                       W_ga[:, fc * E:(fc + 1) * E],
                                       start=False, stop=(fc == 7))
                em.inc(ti, sT)
                em.mark(f"tg{g}", sT)

            def PHI(g):
                em.wait(tensor, f"tgev{g}")
                xf = XF[:, (g % 2) * D:(g % 2 + 1) * D]
                ti = None
                for fc in range(8):
                    ti = tensor.matmul(CH[:, 0:L], xf[:, fc * 128:(fc + 1) * 128],
                                       W_ph[:, fc * L:(fc + 1) * L],
                                       start=(fc == 0), stop=(fc == 7))
                em.inc(ti, sT)
                em.mark(f"ph{g}", sT)

            def GP(g):
                if g >= 1:
                    em.wait(tensor, f"geev{g - 1}")
                xf = XF[:, (g % 2) * D:(g % 2 + 1) * D]
                ti = None
                for nh in range(2):
                    bank = BG0 if nh == 0 else BG1
                    for fc in range(8):
                        ti = tensor.matmul(bank[:], xf[:, fc * 128:(fc + 1) * 128],
                                           W_gg[:, fc * D + nh * 512: fc * D + nh * 512 + 512],
                                           start=(fc == 0), stop=(fc == 7))
                em.inc(ti, sT)
                em.mark(f"ge{g}", sT)

            def MM(g):
                em.wait(tensor, f"phev{g}")
                em.wait(tensor, f"geev{g}")
                pb = (g % 2) * L
                gb = (g % 2) * D
                ti = None
                for lc in range(2):
                    for nh in range(2):
                        pm = (PM0, PM1, PM2, PM3)[lc * 2 + nh]
                        ti = tensor.matmul(pm[:], PHS[:, pb + lc * 128: pb + (lc + 1) * 128],
                                           GES[:, gb + nh * 512: gb + (nh + 1) * 512],
                                           start=(g == 0), stop=(g == NG - 1))
                em.inc(ti, sT)
                em.mark(f"M{g}", sT)

            def CHUNK(q):
                em.wait(tensor, f"S{q}")
                ti = tensor.matmul(CH[0:1, 0:64], K_oc[:], RS_S[:], start=True, stop=True)
                em.inc(ti, sT)
                em.mark(f"cs{q}", sT)
                em.wait(tensor, f"bq{q}")
                tensor.matmul(CH[:, 0:64], K_or[:], RS_BQ[:], start=True, stop=False)
                ti = tensor.matmul(CH[:, 0:64], K_lt[:], RS_S[:], start=False, stop=True)
                em.inc(ti, sT)
                em.mark(f"cb{q}", sT)

            for g in range(NG + 2):
                if 2 <= g:
                    GP(g - 2)
                if g < NG:
                    TR(g)
                if 2 <= g:
                    PHI(g - 2)
                    if (g - 2) % 8 == 7:
                        CHUNK((g - 2) // 8)
                if 1 <= g < NG + 1:
                    TG(g - 1)
                if 3 <= g:
                    MM(g - 3)
            MM(NG - 1)

            em.wait(tensor, f"rkread{NQ - 1}")
            em.wait(tensor, "cnte")
            tensor.matmul(CH[0:1, 0:8], CNTE[:], K_lt8[:], start=True, stop=False)
            tensor.matmul(CH[0:1, 8:9], CNTE[:], K_oc[0:8, 0:1], start=False, stop=False)
            ti = tensor.matmul(CH[0:1, 16:24], K_c8[:, 0:1], CTBL[:],
                               start=False, stop=True)
            em.inc(ti, sT)
            em.mark("prows", sT)
            em.wait(tensor, "prows_sb")
            tensor.matmul(CH[:, 0:8], K_or[:], PROWS[0:1, 16:24], start=True, stop=False)
            tensor.matmul(CH[:, 8:9], K_or[:], PROWS[0:1, 8:9], start=False, stop=False)
            ti = tensor.matmul(CH[:, 16:24], K_or[:], PROWS[0:1, 24:32],
                               start=False, stop=True)
            em.inc(ti, sT)
            em.mark("bcasts", sT)

            def dtr_one(sl):
                em.wait(tensor, f"xdnv{sl}")
                em.wait(tensor, f"dfev{sl - 1}" if sl > 0 else f"xf{NG - 1}")
                xb = (sl % 4) * D
                ti = None
                for k in range(8):
                    ti = tensor.transpose(PTR[:, k * 128:(k + 1) * 128],
                                          XDN[:, xb + k * 128: xb + (k + 1) * 128],
                                          K_id[:])
                em.inc(ti, sT)
                em.mark(f"dtr{sl}", sT)

            def att_unit(u):
                g, nh = u // 2, u % 2
                if u == 0:
                    em.wait(tensor, "pb")
                    em.wait(tensor, "mb")
                elif u >= 3:
                    em.wait(tensor, f"attevx{u - 3}")
                if u in ATT_EO_WAIT:
                    em.wait(tensor, f"eo{ATT_EO_WAIT[u]}")
                bank = (CH, PM2, PM3)[u % 3]
                ti = None
                for lc in range(2):
                    ti = tensor.matmul(bank[:, 0:512],
                                       TH[:, lc * TOK + g * 128: lc * TOK + (g + 1) * 128],
                                       MB[:, lc * D + nh * 512: lc * D + nh * 512 + 512],
                                       start=(lc == 0), stop=(lc == 1))
                em.inc(ti, sT)
                em.mark(f"attx{u}", sT)

            def units_at(qq, fc):
                return ATT_UNITS.get((qq, fc), ())

            for sl in range(4):
                dtr_one(sl)
            for qq in range(4):
                for j in range(4):
                    em.wait(tensor, f"dfev{4 * qq + j}")
                if qq == 0:
                    em.wait(tensor, "wB_done")
                dfb = (qq % 2) * (8 * 512)
                for fc in range(16):
                    b0, b1 = ((BG0, BG1) if fc % 2 == 0 else (PM0, PM1))
                    if fc >= 2:
                        em.wait(tensor, f"inter{qq}_{fc - 2}")
                    elif qq > 0:
                        em.wait(tensor, f"inter{qq - 1}_{14 + fc}")
                        if fc == 0:
                            em.wait(tensor, f"eo{4 * qq - 1}")
                    elif qq == 0 and fc < 2:
                        em.wait(tensor, "marq1")
                        em.wait(tensor, f"geev{NG - 1}")
                    for dc in range(8):
                        tensor.matmul(b0[:],
                                      W_0[:, dc * FF + fc * 128: dc * FF + fc * 128 + 128],
                                      DFB[:, dfb + dc * 512: dfb + dc * 512 + 512],
                                      start=(dc == 0), stop=(dc == 7))
                    ti = None
                    for dc in range(8):
                        ti = tensor.matmul(b1[:],
                                           W_1[:, dc * FF + fc * 128: dc * FF + fc * 128 + 128],
                                           DFB[:, dfb + dc * 512: dfb + dc * 512 + 512],
                                           start=(dc == 0), stop=(dc == 7))
                    em.inc(ti, sT)
                    em.mark(f"h{qq}_{fc}", sT)
                    for u in units_at(qq, fc):
                        att_unit(u)
                em.wait(tensor, f"inter{qq}_15")
                for s2 in range(4):
                    gs = qq * 4 + s2
                    if gs > 1:
                        em.wait(tensor, f"eo{gs - 2}")
                    if s2 == 0:
                        em.wait(tensor, f"attevx{16 * qq + 15}")
                    ti = None
                    for nh in range(2):
                        bank = (PM2, PM3)[nh] if gs % 2 == 0 else (BG0, BG1)[nh]
                        for fc in range(16):
                            ti = tensor.matmul(
                                bank[:],
                                INTER[:, fc * 512 + s2 * 128: fc * 512 + s2 * 128 + 128],
                                W_o[:, fc * D + nh * 512: fc * D + nh * 512 + 512],
                                start=(fc == 0), stop=(fc == 15))
                    em.inc(ti, sT)
                    em.mark(f"omm{gs}", sT)
                if qq < 3:
                    for sl in range(4 * (qq + 1), 4 * (qq + 1) + 4):
                        dtr_one(sl)

        def body_vector(v):
            def se(inst):
                em.se(v, sV, inst)
            em.wait(v, "k_done")
            em.wait(v, "iotas")
            se(v.memset(RBASE[:], 0.0))
            se(v.tensor_scalar(GTID[:], GTID[:], K_cc[:, 0:1], None, ALU.add))
            em.mark("gtid", sV)

            def stats_work(g):
                gb = (g % 3) * D
                b = (g % 2) * 8
                em.wait(v, f"x{g}")
                se(v.reduce_sum(STAT[:, b:b + 1], XT[:, gb:gb + D], axis=AX))
                em.mark(f"sum{g}", sV)
                em.wait(v, f"ssq{g}")
                se(v.tensor_scalar(STAT[:, b + 2:b + 3], STAT[:, b:b + 1], -1.0 / D, None, ALU.mult))
                se(v.tensor_mul(STAT[:, b + 3:b + 4], STAT[:, b + 2:b + 3], STAT[:, b + 2:b + 3]))
                se(v.scalar_tensor_tensor(STAT[:, b + 4:b + 5], STAT[:, b + 1:b + 2], 1.0 / D,
                                          STAT[:, b + 3:b + 4], ALU.mult, ALU.subtract))
                em.mark(f"var{g}", sV)
                em.wait(v, f"sqrt{g}")
                se(v.reciprocal(STAT[:, b + 6:b + 7], STAT[:, b + 5:b + 6]))
                if g >= 4:
                    em.wait(v, f"tr{g - 4}")
                se(v.tensor_scalar(XNB[:, (g % 4) * D:(g % 4 + 1) * D], XT[:, gb:gb + D],
                                   STAT[:, b + 2:b + 3],
                                   STAT[:, b + 6:b + 7], ALU.add, ALU.mult))
                em.mark(f"xnb{g}", sV)

            def xfev(g):
                em.wait(v, f"tr{g}")
                if g >= 2:
                    em.wait(v, f"ge{g - 2}")
                se(v.tensor_copy(XF[:, (g % 2) * D:(g % 2 + 1) * D], PTR[:]))
                em.mark(f"xf{g}", sV)

            def tgev(g):
                em.wait(v, f"tg{g}")
                se(v.tensor_copy(GATE[:, g * E:(g + 1) * E], CH[:, 256:256 + E]))
                se(v.tensor_copy(TH[:, g * 128:(g + 1) * 128], CH[:, 0:128]))
                se(v.tensor_copy(TH[:, TOK + g * 128: TOK + (g + 1) * 128],
                                 CH[:, 128:256]))
                em.mark(f"tgev{g}", sV)

            def phev(g):
                em.wait(v, f"ph{g}")
                if g >= 2:
                    em.wait(v, f"M{g - 2}")
                se(v.tensor_copy(PHS[:, (g % 2) * L:(g % 2 + 1) * L], CH[:, 0:L]))
                em.mark(f"phev{g}", sV)

            def chunk_work(q):
                    gsl = GATE[:, q * 64:(q + 1) * 64]
                    g3 = t3(gsl)
                    io3 = t3(K_io[:])
                    S1 = SEL1[:, q * 8:(q + 1) * 8]
                    S2 = SEL2[:, q * 8:(q + 1) * 8]
                    se(v.tensor_reduce(col3(RS_A[:]), g3, AX, ALU.max))
                    se(v.tensor_tensor(t3(RS_M1[:]), g3,
                                       col3(RS_A[:]).broadcast_to([128, 8, E]),
                                       ALU.is_equal))
                    se(v.tensor_mul(RS_T[:], RS_M1[:], K_rv[:]))
                    se(v.tensor_reduce(col3(RS_B[:]), t3(RS_T[:]), AX, ALU.max))
                    se(v.tensor_scalar(S1, RS_B[:], -1.0, 7.0, ALU.mult, ALU.add))
                    se(v.tensor_tensor(t3(RS_M1[:]), io3,
                                       col3(S1).broadcast_to([128, 8, E]), ALU.is_equal))
                    se(v.scalar_tensor_tensor(RS_G2[:], RS_M1[:], -1e30, gsl,
                                              ALU.mult, ALU.add))
                    se(v.tensor_reduce(col3(RS_A[:]), t3(RS_G2[:]), AX, ALU.max))
                    se(v.tensor_reduce(col3(RS_B[:]), g3, AX, ALU.max))
                    se(v.tensor_sub(P1V[:, q * 8:(q + 1) * 8], RS_B[:], RS_A[:]))
                    em.mark(f"dl{q}", sV)
                    se(v.tensor_tensor(t3(RS_M2[:]), t3(RS_G2[:]),
                                       col3(RS_A[:]).broadcast_to([128, 8, E]),
                                       ALU.is_equal))
                    se(v.tensor_mul(RS_T[:], RS_M2[:], K_rv[:]))
                    se(v.tensor_reduce(col3(RS_B[:]), t3(RS_T[:]), AX, ALU.max))
                    se(v.tensor_scalar(S2, RS_B[:], -1.0, 7.0, ALU.mult, ALU.add))
                    se(v.tensor_tensor(t3(RS_M2[:]), io3,
                                       col3(S2).broadcast_to([128, 8, E]), ALU.is_equal))
                    se(v.tensor_add(RS_S[:], RS_M1[:], RS_M2[:]))
                    em.mark(f"S{q}", sV)
                    em.wait(v, f"cs{q}")
                    se(v.tensor_copy(RS_R[:], CH[0:1, 0:64]))
                    bufs = (RS_R, RS_R2)
                    for step, sh in enumerate((1, 2, 4)):
                        a = bufs[step % 2][:].rearrange("p (t e) -> p t e", e=E)
                        b = bufs[(step + 1) % 2][:].rearrange("p (t e) -> p t e", e=E)
                        se(v.tensor_copy(b[:, 0:sh, :], a[:, 0:sh, :]))
                        se(v.tensor_add(b[:, sh:8, :], a[:, sh:8, :], a[:, 0:8 - sh, :]))
                    PFX = bufs[1]
                    r3 = PFX[:].rearrange("p (t e) -> p t e", e=E)
                    bq3 = RS_BQ[:].rearrange("p (t e) -> p t e", e=E)
                    se(v.tensor_copy(bq3[:, 0:1, :],
                                     RBASE[:].rearrange("p (o e) -> p o e", e=E)))
                    se(v.tensor_add(bq3[:, 1:8, :], r3[:, 0:7, :],
                                    RBASE[:].rearrange("p (o e) -> p o e", e=E)
                                    .broadcast_to([1, 7, E])))
                    se(v.tensor_add(RBASE[:], RBASE[:], PFX[:, 56:64]))
                    em.mark(f"bq{q}", sV)
                    em.wait(v, f"cb{q}")
                    se(v.tensor_mul(RS_T[:], RS_M1[:], CH[:, 0:64]))
                    se(v.tensor_reduce(col3(RK1[:, q * 8:(q + 1) * 8]), t3(RS_T[:]),
                                       AX, ALU.add))
                    se(v.tensor_mul(RS_T[:], RS_M2[:], CH[:, 0:64]))
                    se(v.tensor_reduce(col3(RK2[:, q * 8:(q + 1) * 8]), t3(RS_T[:]),
                                           AX, ALU.add))
                    em.mark(f"rkread{q}", sV)
                    se(v.scalar_tensor_tensor(DSF[:], S1, 8192.0,
                                              RK1[:, q * 8:(q + 1) * 8], ALU.mult, ALU.add))
                    se(v.tensor_scalar(DSF[:], DSF[:], 16.0, None, ALU.add))
                    se(v.tensor_copy(DST1[:, q * 8:(q + 1) * 8], DSF[:]))
                    se(v.scalar_tensor_tensor(DSF[:], S2, 8192.0,
                                              RK2[:, q * 8:(q + 1) * 8], ALU.mult, ALU.add))
                    se(v.tensor_scalar(DSF[:], DSF[:], 16.0, None, ALU.add))
                    se(v.tensor_copy(DST2[:, q * 8:(q + 1) * 8], DSF[:]))
                    em.mark(f"pack{q}", sV)


            for g in range(NG):
                stats_work(g)
                if g >= 1:
                    xfev(g - 1)
                if g >= 2:
                    tgev(g - 2)
                    phev(g - 2)
                    if (g - 2) % 8 == 7:
                        chunk_work((g - 2) // 8)
            xfev(NG - 1)
            tgev(NG - 2)
            phev(NG - 2)
            tgev(NG - 1)
            phev(NG - 1)
            chunk_work(NQ - 1)

            em.wait(v, f"M{NG - 1}")
            for j in range(2):
                if j > 0:
                    em.wait(v, f"mars{j - 1}")
                se(v.tensor_copy(MAR[:, 0:512], (PM0, PM1, PM2, PM3)[j * 2 + 0][:]))
                se(v.tensor_copy(MAR[:, 512:1024], (PM0, PM1, PM2, PM3)[j * 2 + 1][:]))
                em.mark(f"marq{j}", sV)

            em.wait(v, "ctbl")
            se(v.tensor_mul(SM1[:], CTBL[:], K_em[0:8, :]))
            se(v.reduce_sum(CNTE[:], SM1[:], axis=AX))
            em.mark("cnte", sV)
            em.wait(v, "prows")
            se(v.tensor_copy(PROWS[0:1, 0:24], CH[0:1, 0:24]))
            se(v.tensor_scalar(PROWS[0:1, 24:32], PROWS[0:1, 0:8], -1.0, None,
                                   ALU.mult))
            em.mark("prows_sb", sV)
            em.wait(v, "bcasts")
            se(v.tensor_copy(PBV[:, 0:24], CH[:, 0:24]))
            em.mark("pb", sV)

            def dA(s):
                se(v.tensor_scalar(DSF[:], PBV[:, 16:24], RALL[:, s:s + 1], None,
                                   ALU.add))
                se(v.tensor_scalar(RS_A[:], DSF[:], 0.0, None, ALU.is_ge))
                se(v.tensor_reduce(STAT[:, 9:10], RS_A[:], AX, ALU.add))
                se(v.scalar_tensor_tensor(RS_B[:], RS_A[:], -BIG, DSF[:],
                                          ALU.mult, ALU.add))
                se(v.tensor_reduce(STAT[:, 10:11], RS_B[:], AX, ALU.min))
                se(v.tensor_scalar(STAT[:, 10:11], STAT[:, 10:11], BIG, None, ALU.add))
                se(v.tensor_scalar(STAT[:, 9:10], STAT[:, 9:10], -1.0, None, ALU.add))
                se(v.scalar_tensor_tensor(STAT[:, 11:12], STAT[:, 9:10], float(PACKW),
                                          STAT[:, 10:11], ALU.mult, ALU.add))
                se(v.tensor_scalar(STAT[:, 11:12], STAT[:, 11:12], K_cc[:, 1:2],
                                   None, ALU.add))
                se(v.tensor_copy(SRCU[:, s:s + 1], STAT[:, 11:12]))
                se(v.tensor_scalar(VLD[:, s:s + 1], RALL[:, s:s + 1], PBV[:, 8:9],
                                       None, ALU.is_lt))
                em.mark(f"src{s}", sV)

            def dB(s):
                em.wait(v, f"idg{s}")
                se(v.tensor_mul(STAT[:, 12:13], TIDF[:, s:s + 1], VLD[:, s:s + 1]))
                se(v.scalar_tensor_tensor(STAT[:, 13:14], VLD[:, s:s + 1], -ZROW,
                                          STAT[:, 12:13], ALU.mult, ALU.add))
                se(v.tensor_scalar(STAT[:, 13:14], STAT[:, 13:14], ZROW, None, ALU.add))
                se(v.tensor_copy(TIDU[:, s:s + 1], STAT[:, 13:14]))
                em.mark(f"tid{s}", sV)

            def dC(s):
                rb = (s % 2) * D
                em.wait(v, f"rowg{s}")
                se(v.reduce_sum(STAT[:, 0:1], XRAW[:, rb:rb + D], axis=AX))
                em.mark(f"dsum{s}", sV)
                em.wait(v, f"dssq{s}")
                se(v.tensor_scalar(STAT[:, 2:3], STAT[:, 0:1], -1.0 / D, None, ALU.mult))
                se(v.tensor_mul(STAT[:, 3:4], STAT[:, 2:3], STAT[:, 2:3]))
                se(v.scalar_tensor_tensor(STAT[:, 4:5], STAT[:, 1:2], 1.0 / D,
                                              STAT[:, 3:4], ALU.mult, ALU.subtract))
                em.mark(f"dvar{s}", sV)

            def dD(s):
                rb = (s % 2) * D
                em.wait(v, f"dsqrt{s}")
                se(v.reciprocal(STAT[:, 6:7], STAT[:, 5:6]))
                if s in (0, 1):
                    em.wait(v, f"xnb{NG - 1}")
                elif s == 2:
                    em.wait(v, f"tr{NG - 2}")
                elif s == 3:
                    em.wait(v, f"tr{NG - 1}")
                else:
                    em.wait(v, f"dtr{s - 4}")
                xb = (s % 4) * D
                se(v.tensor_scalar(XDN[:, xb:xb + D], XRAW[:, rb:rb + D],
                                       STAT[:, 2:3], STAT[:, 6:7], ALU.add, ALU.mult))
                em.mark(f"xdnv{s}", sV)

            def dE(sl):
                em.wait(v, f"dtr{sl}")
                dfb = ((sl // 4) % 2) * (8 * 512)
                off = (sl % 4) * 128
                dst = DFB[:, dfb:dfb + 8 * 512].rearrange(
                    "p (k c) -> p k c", c=512)[:, :, off:off + 128]
                src = PTR[:].rearrange("p (k c) -> p k c", c=128)
                se(v.tensor_copy(dst, src))
                em.mark(f"dfev{sl}", sV)

            def inter_one(qq, fc):
                b1 = BG1 if fc % 2 == 0 else PM1
                em.wait(v, f"sil{qq}_{fc}")
                if qq > 0 and fc == 0:
                    em.wait(v, f"omm{qq * 4 - 1}")
                se(v.tensor_tensor(INTER[:, fc * 512:(fc + 1) * 512], SIL[:],
                                   b1[:], ALU.mult))
                em.mark(f"inter{qq}_{fc}", sV)

            def eo_one(qq, s2):
                gs = qq * 4 + s2
                em.wait(v, f"omm{gs}")
                if 2 <= gs < 8:
                    em.wait(v, f"eod{gs - 2}")
                dst = EOSLOT(gs)
                b0, b1 = (PM2, PM3) if gs % 2 == 0 else (BG0, BG1)
                se(v.tensor_copy(dst[:, 0:512], b0[:]))
                se(v.tensor_copy(dst[:, 512:1024], b1[:]))
                em.mark(f"eo{gs}", sV)

            def mbp_one(lc):
                em.wait(v, f"md{lc}")
                se(v.tensor_scalar(MB[:, lc * D:(lc + 1) * D], MAR[:], 1.0 / 32.0,
                                   None, ALU.mult))
                em.mark(f"mbp{lc}", sV)
                if lc == 1:
                    em.mark("mb", sV)

            def attevx_one(u):
                g, nh = u // 2, u % 2
                em.wait(v, f"attx{u}")
                if nh == 0:
                    em.wait(v, f"xa{g}")
                bank = (CH, PM2, PM3)[u % 3]
                dst = MDPS[g % 4][:, nh * 512: nh * 512 + 512]
                se(v.tensor_tensor(dst, dst, bank[:, 0:512], ALU.add))
                em.mark(f"attevx{u}", sV)

            for f in (lambda: dA(0), lambda: dA(1), lambda: dA(2), lambda: dA(3),
                      lambda: dB(0), lambda: dB(1), lambda: dC(0), lambda: dB(2),
                      lambda: dD(0), lambda: dC(1), lambda: dB(3), lambda: dD(1),
                      lambda: dC(2), lambda: dD(2), lambda: dC(3), lambda: dD(3),
                      lambda: dE(0), lambda: dE(1), lambda: dE(2), lambda: dE(3)):
                f()
            for k, (SEL, RK, GI) in enumerate(((SEL1, RK1, GI1), (SEL2, RK2, GI2))):
                for e in range(E):
                    se(v.tensor_scalar(RS_M1[:, 0:NG], SEL[:], float(e), None,
                                       ALU.is_equal))
                    if e == 0:
                        se(v.tensor_scalar(RS_M2[:, 0:NG], RS_M1[:, 0:NG],
                                           PBV[:, 0:1], None, ALU.mult))
                    else:
                        se(v.scalar_tensor_tensor(RS_M2[:, 0:NG], RS_M1[:, 0:NG],
                                                  PBV[:, e:e + 1], RS_M2[:, 0:NG],
                                                  ALU.mult, ALU.add))
                se(v.tensor_add(RK[:], RK[:], RS_M2[:, 0:NG]))
                se(v.tensor_scalar_min(RS_M1[:, 0:NG], RK[:], float(CAP - 1)))
                se(v.tensor_scalar(RS_T[:, 0:NG], RS_M1[:, 0:NG], 640.0, None, ALU.is_ge))
                se(v.tensor_scalar(RS_S[:, 0:NG], RS_T[:, 0:NG], 640.0, None, ALU.mult))
                se(v.tensor_scalar(RS_T[:, 0:NG], RS_M1[:, 0:NG], 1280.0, None, ALU.is_ge))
                se(v.scalar_tensor_tensor(RS_S[:, 0:NG], RS_T[:, 0:NG], 640.0,
                                          RS_S[:, 0:NG], ALU.mult, ALU.add))
                se(v.tensor_scalar(RS_G2[:, 0:NG], RS_M1[:, 0:NG], 1792.0, None, ALU.is_ge))
                se(v.scalar_tensor_tensor(RS_S[:, 0:NG], RS_G2[:, 0:NG], 512.0,
                                          RS_S[:, 0:NG], ALU.mult, ALU.add))
                se(v.tensor_scalar(RS_M2[:, 0:NG], RS_T[:, 0:NG], -128.0, 640.0,
                                   ALU.mult, ALU.add))
                se(v.scalar_tensor_tensor(RS_M2[:, 0:NG], RS_G2[:, 0:NG], -256.0,
                                          RS_M2[:, 0:NG], ALU.mult, ALU.add))
                se(v.tensor_mul(RS_M2[:, 0:NG], RS_M2[:, 0:NG], SEL[:]))
                se(v.scalar_tensor_tensor(RS_M1[:, 0:NG], RS_S[:, 0:NG], 7.0,
                                          RS_M1[:, 0:NG], ALU.mult, ALU.add))
                se(v.tensor_add(RS_M1[:, 0:NG], RS_M1[:, 0:NG], RS_M2[:, 0:NG]))
                se(v.tensor_copy(GI[:], RS_M1[:, 0:NG]))
                em.mark(f"gidx{k}", sV)

            def units_at_v(qq, fc):
                return ATT_UNITS.get((qq, fc), ())

            for qq in range(4):
                S = [4 * (qq + 1) + j for j in range(4)] if qq < 3 else None
                sched = []
                for fc in range(16):
                    sched.append(("i", fc))
                    for u in units_at_v(qq, fc):
                        sched.append(("X", u))
                    if qq == 0 and fc == 5:
                        sched.append(("M", 0))
                    if qq == 0 and fc == 6:
                        sched.append(("M", 1))
                    if S is not None:
                        if fc == 0:
                            sched.append(("A", S[0]))
                        elif fc == 1:
                            sched.append(("A", S[1]))
                        elif fc == 3:
                            sched.append(("B", S[0]))
                        elif fc == 4:
                            sched.append(("A", S[2]))
                        elif fc == 5:
                            sched.append(("B", S[1]))
                        elif fc == 6:
                            sched.append(("C", S[0]))
                        elif fc == 7:
                            sched.append(("A", S[3]))
                        elif fc == 8:
                            sched.append(("D", S[0]))
                            sched.append(("B", S[2]))
                        elif fc == 9:
                            sched.append(("C", S[1]))
                        elif fc == 10:
                            sched.append(("D", S[1]))
                            sched.append(("B", S[3]))
                        elif fc == 11:
                            sched.append(("C", S[2]))
                        elif fc == 12:
                            sched.append(("D", S[2]))
                        elif fc == 13:
                            sched.append(("C", S[3]))
                        elif fc == 14:
                            sched.append(("D", S[3]))
                for kind, val in sched:
                    if kind == "i":
                        inter_one(qq, val)
                    elif kind == "X":
                        attevx_one(val)
                    elif kind == "M":
                        mbp_one(val)
                    elif kind == "A":
                        dA(val)
                    elif kind == "B":
                        dB(val)
                    elif kind == "C":
                        dC(val)
                    elif kind == "D":
                        dD(val)
                for s2 in range(4):
                    eo_one(qq, s2)
                if S is not None:
                    for j in range(4):
                        dE(S[j])

            em.wait(v, f"sig{NQ - 1}")
            for g in range(NG):
                ob = (g % 4) * D
                rb = (g % 8) * D
                xb = (g % 4) * D
                XCS = XC if g % 8 < 4 else XC2
                OCS = OAC if g % 8 < 4 else OAC2
                em.wait(v, f"pc{g}")
                em.wait(v, f"rbg{g}")
                if g >= 8:
                    em.wait(v, f"outd{g - 8}")
                se(v.scalar_tensor_tensor(OCS[:, ob:ob + D], RB1[:, rb:rb + D],
                                          P1V[:, g:g + 1], XCS[:, xb:xb + D],
                                          ALU.mult, ALU.add))
                em.mark(f"xcr{g}", sV)
                se(v.scalar_tensor_tensor(OCS[:, ob:ob + D], RB2[:, rb:rb + D],
                                          P2V[:, g:g + 1], OCS[:, ob:ob + D],
                                          ALU.mult, ALU.add))
                em.mark(f"oacd{g}", sV)

        def body_scalar(scalar):
            def sse(inst):
                em.se(scalar, sA, inst)
            em.wait(scalar, "k_done")
            def sig_work(q):
                em.wait(scalar, f"dl{q}")
                sse(scalar.activation(P1V[:, q * 8:(q + 1) * 8],
                                      P1V[:, q * 8:(q + 1) * 8], AF.Sigmoid))
                sse(scalar.activation(P2V[:, q * 8:(q + 1) * 8],
                                      P1V[:, q * 8:(q + 1) * 8],
                                      AF.Copy, scale=-1.0, bias=1.0))
                em.mark(f"sig{q}", sA)

            def geev(g):
                em.wait(scalar, f"ge{g}")
                if g >= 2:
                    em.wait(scalar, f"M{g - 2}")
                gb = (g % 2) * D
                sse(scalar.activation(GES[:, gb:gb + 512], BG0[:], AF.Copy))
                sse(scalar.activation(GES[:, gb + 512:gb + D], BG1[:], AF.Copy))
                em.mark(f"geev{g}", sA)

            for g in range(NG):
                gb = (g % 3) * D
                b = (g % 2) * 8
                em.wait(scalar, f"x{g}")
                if g >= 2:
                    em.wait(scalar, f"xnb{g - 2}")
                elif g == 0:
                    em.wait(scalar, "pzero")
                sse(scalar.activation(SNK[:], XT[:, gb:gb + D], AF.Square,
                                      accum_out=STAT[:, b + 1:b + 2]))
                em.mark(f"ssq{g}", sA)
                em.wait(scalar, f"var{g}")
                sse(scalar.activation(STAT[:, b + 5:b + 6], STAT[:, b + 4:b + 5], AF.Sqrt, bias=1e-5))
                em.mark(f"sqrt{g}", sA)
                if g >= 2:
                    geev(g - 2)
                if g % 8 == 2 and g >= 10:
                    sig_work((g - 10) // 8)
            geev(NG - 2)
            geev(NG - 1)
            sig_work(NQ - 1)
            def sil_one(qq, fc):
                b0 = BG0 if fc % 2 == 0 else PM0
                em.wait(scalar, f"h{qq}_{fc}")
                if fc >= 1:
                    em.wait(scalar, f"inter{qq}_{fc - 1}")
                elif qq > 0:
                    em.wait(scalar, f"inter{qq - 1}_15")
                sse(scalar.activation(SIL[:], b0[:], AF.Silu))
                em.mark(f"sil{qq}_{fc}", sA)

            def dssq_one(s):
                rb = (s % 2) * D
                em.wait(scalar, f"rowg{s}")
                if s > 0:
                    em.wait(scalar, f"dvar{s - 1}")
                sse(scalar.activation(SNK[:], XRAW[:, rb:rb + D], AF.Square,
                                      accum_out=STAT[:, 1:2]))
                em.mark(f"dssq{s}", sA)

            def dsqrt_one(s):
                em.wait(scalar, f"dvar{s}")
                sse(scalar.activation(STAT[:, 5:6], STAT[:, 4:5], AF.Sqrt, bias=1e-5))
                em.mark(f"dsqrt{s}", sA)

            for s in range(4):
                dssq_one(s)
                dsqrt_one(s)
            def eod_one(gs):
                em.wait(scalar, f"eo{gs}")
                if gs >= 4:
                    em.wait(scalar, f"eod{gs - 4}")
                esem = (sE, sE1, sE2, sE3)[gs % 4]
                em.inc(scalar.dma_start(out=eoutd.ap()[gs * 128:(gs + 1) * 128, :],
                                        in_=EOSLOT(gs)), esem, 16)
                em.mark(f"eod{gs}", esem)

            for qq in range(4):
                S = [4 * (qq + 1) + j for j in range(4)] if qq < 3 else None
                if qq >= 1:
                    for s2 in range(4):
                        eod_one(4 * (qq - 1) + s2)
                for fc in range(16):
                    sil_one(qq, fc)
                    if S is not None:
                        if fc == 6:
                            dssq_one(S[0])
                        elif fc == 7:
                            dsqrt_one(S[0])
                        elif fc == 9:
                            dssq_one(S[1])
                            dsqrt_one(S[1])
                        elif fc == 11:
                            dssq_one(S[2])
                            dsqrt_one(S[2])
                        elif fc == 13:
                            dssq_one(S[3])
                            dsqrt_one(S[3])

            for gs in range(12, 16):
                eod_one(gs)
            for g in range(NG):
                if g == 0:
                    em.wait(scalar, "xdnv14")
                elif g == 1:
                    em.wait(scalar, "xdnv15")
                elif g in (2, 3):
                    em.wait(scalar, "dfev15")
                elif g in (4, 5, 6, 7):
                    em.wait(scalar, "omm15")
                else:
                    em.wait(scalar, f"xcr{g - 8}")
                em.wait(scalar, f"pstore{g}")
                xsem = sPc[g % 8]
                dst = (XC if g % 8 < 4 else XC2)[:, (g % 4) * D:(g % 4 + 1) * D]
                em.inc(scalar.dma_start(out=dst,
                                        in_=partd.ap()[g * 128:(g + 1) * 128, :]),
                       xsem, 16)
                em.mark(f"pc{g}", xsem)

        def body_gpsimd(gp):
            em.inc(gp.memset(SNK[:], 0.0), sG)
            em.mark("zro", sG)
            em.wait(gp, "zro")
            em.inc(gp.dma_start(
                out=packsrc.ap()[16:PACKW, :].rearrange("(p b) o -> p (b o)", p=128),
                in_=SNK[:, 0:512]), sI, 16)
            em.inc(gp.dma_start(
                out=packsrc.ap().rearrange("(a b) o -> a (b o)", a=1)[0:1, 8:16],
                in_=SNK[0:1, 0:8]), sI, 16)
            em.mark("pzero", sI)
            gp.iota(GTID[:], pattern=[[128, NG]], base=0, channel_multiplier=1,
                    allow_small_or_imprecise_dtypes=True)
            em.inc(gp.iota(RALL[:], pattern=[[128, 16]], base=0, channel_multiplier=1,
                           allow_small_or_imprecise_dtypes=True), sG)
            em.mark("iotas", sG)
            em.inc(gp.dma_start(out=W_th.rearrange("p (fc l) -> p fc l", l=L),
                                in_=wthT.rearrange("(fc p) l -> p fc l", p=128)), sW, 16)
            em.inc(gp.dma_start(out=W_ph.rearrange("p (fc l) -> p fc l", l=L),
                                in_=wphT.rearrange("(fc p) l -> p fc l", p=128)), sW, 16)
            em.inc(gp.dma_start(out=W_gg.rearrange("p (fc d) -> p fc d", d=D),
                                in_=wgT.rearrange("(fc p) d -> p fc d", p=128)), sW, 16)
            em.inc(gp.dma_start(out=W_ga.rearrange("p (fc e) -> p fc e", e=E),
                                in_=wgate.rearrange("(fc p) e -> p fc e", p=128)), sW, 16)
            em.mark("wA_done", sW)
            em.inc(gp.dma_start(out=W_0[:].rearrange("p (dc f) -> p dc f", f=FF),
                                in_=w0e.rearrange("(dc p) f -> p dc f", p=128)), sWb, 16)
            em.inc(gp.dma_start(out=W_1[:].rearrange("p (dc f) -> p dc f", f=FF),
                                in_=w1e.rearrange("(dc p) f -> p dc f", p=128)), sWb, 16)
            em.inc(gp.dma_start(out=W_o[:].rearrange("p (fc d) -> p fc d", d=D),
                                in_=woe.rearrange("(fc p) d -> p fc d", p=128)), sWb, 16)
            em.mark("wB_done", sWb)
            em.wait(gp, "pzero")
            for q in range(NQ):
                em.wait(gp, f"pack{q}")
                for g in range(8 * q, 8 * q + 8):
                    em.inc(gp.indirect_dma_start(
                        out=packsrc.ap(),
                        out_offset=IndirectOffsetOnAxis(ap=DST1[:, g:g + 1], axis=0),
                        in_=GTID[:, g:g + 1], in_offset=None), sI, 16)
                    em.inc(gp.indirect_dma_start(
                        out=packsrc.ap(),
                        out_offset=IndirectOffsetOnAxis(ap=DST2[:, g:g + 1], axis=0),
                        in_=GTID[:, g:g + 1], in_offset=None), sI, 16)
            em.inc(gp.dma_start(
                out=packsrc.ap().rearrange("(a b) o -> a (b o)", a=1)[0:1, 0:8],
                in_=RBASE[:]), sI, 16)
            em.mark("packall", sI)
            if not em.dry:
                gp.wait_ge(sI.h, sI.v)
            em.inc(gp.collective_compute("AllGather", ALU.bypass, replica_groups=RGRP,
                                         ins=[packsrc.ap().opt()],
                                         outs=[agout.ap().opt()]), sC)
            em.mark("ag1", sC)
            em.wait(gp, "mars1")
            em.inc(gp.collective_compute("AllReduce", ALU.add, replica_groups=RGRP2,
                                         ins=[marsrc.ap().opt()],
                                         outs=[mardst.ap().opt()]), sC)
            em.mark("arm", sC)
            def idg_one(s):
                em.wait(gp, f"src{s}")
                em.inc(gp.indirect_dma_start(
                    out=TIDF[:, s:s + 1], out_offset=None,
                    in_=agout.ap(),
                    in_offset=IndirectOffsetOnAxis(ap=SRCU[:, s:s + 1], axis=0)),
                       sD4[s % 4], 16)
                em.mark(f"idg{s}", sD4[s % 4])

            def rowg_one(s):
                em.wait(gp, f"tid{s}")
                if s >= 2:
                    em.wait(gp, f"xdnv{s - 2}")
                else:
                    em.wait(gp, f"xnb{NG - 1}")
                em.inc(gp.indirect_dma_start(
                    out=XRAW[:, (s % 2) * D:(s % 2 + 1) * D], out_offset=None,
                    in_=xfull,
                    in_offset=IndirectOffsetOnAxis(ap=TIDU[:, s:s + 1], axis=0)),
                       sD4[s % 4], 16)
                em.mark(f"rowg{s}", sD4[s % 4])

            for t in range(4):
                S = [4 * t + j for j in range(4)]
                idg_one(S[0]); idg_one(S[1]); rowg_one(S[0]); idg_one(S[2])
                rowg_one(S[1]); idg_one(S[3]); rowg_one(S[2]); rowg_one(S[3])
            AG2B = (0, 640, 1280, 1792, 2048)
            for qc in range(4):
                lo, hi = AG2B[qc], AG2B[qc + 1]
                ng = hi // 128
                for j in range(4):
                    top = [g for g in range(ng) if g % 4 == j]
                    if top:
                        em.wait(gp, f"eod{top[-1]}")
                em.inc(gp.collective_compute(
                    "AllGather", ALU.bypass, replica_groups=RGRP,
                    ins=[eoutd.ap()[lo:hi, :].opt()],
                    outs=[ag2out.ap()[NCORE * lo:NCORE * hi, :].opt()]), sC)
                em.mark(f"ag2c{qc}", sC)
            em.wait(gp, "ag2c3")
            em.wait(gp, "gidx0")
            em.wait(gp, "gidx1")
            for g in range(NG):
                if g >= 8:
                    em.wait(gp, f"oacd{g - 8}")
                rbs = (g % 8) * D
                sR = (sR0, sR1, sR2, sR3, sR4, sR5, sR6, sR7)[g % 8]
                em.inc(gp.indirect_dma_start(
                    out=RB1[:, rbs:rbs + D], out_offset=None, in_=ag2out.ap(),
                    in_offset=IndirectOffsetOnAxis(ap=GI1[:, g:g + 1], axis=0)), sR, 16)
                em.inc(gp.indirect_dma_start(
                    out=RB2[:, rbs:rbs + D], out_offset=None, in_=ag2out.ap(),
                    in_offset=IndirectOffsetOnAxis(ap=GI2[:, g:g + 1], axis=0)), sR, 16)
                em.mark(f"rbg{g}", sR)
            if not em.dry:
                for _s in (sST, sST1, sST2, sST3, sST4, sST5, sST6, sST7):
                    gp.wait_ge(_s.h, _s.v)

        bodies = (body_sync, body_tensor, body_vector, body_scalar, body_gpsimd)
        stub = StubEng()
        for b in bodies:
            b(stub)
        _DBG["MS"] = {k: (id(v[0]), v[1]) for k, v in em.MS.items()}
        _DBG["SEMS"] = {id(s): n for s, n in zip(ALLSEMS, ("sX sX1 sX2 sX3 sK sW sWb sR0 sR1 sR2 sR3 sR4 sR5 sR6 sR7 sST1 sST2 sST3 sST4 sST5 sST6 sST7 sV sA sT sG sI sC sE sE1 sE2 sE3 sST sPd sPd1 sPd2 sPd3 sMd sD0 sD1 sD2 sD3".split() + [f"sPc{j}" for j in range(8)]))}
        for s in ALLSEMS:
            s.v = 0
        em.dry = False
        block.sync(body_sync)
        block.tensor(body_tensor)
        block.vector(body_vector)
        block.scalar(body_scalar)
        block.gpsimd(body_gpsimd)

    return nc


_CACHE = {}


def _get_nc():
    if "nc" not in _CACHE:
        _CACHE["nc"] = build_nc(debug=False)
    return _CACHE["nc"]


def make_in_maps(L0_attn_x, L0_attn_W_theta, L0_attn_W_phi, L0_attn_W_g,
                 L0_ffn_wg, L0_ffn_w0, L0_ffn_w1, L0_ffn_wo):
    x2d = np.ascontiguousarray(np.asarray(L0_attn_x, np.float32).reshape(32768, D))
    xfull = np.zeros((NROWX, D), np.float32)
    xfull[:32768] = x2d
    wthT = np.ascontiguousarray(np.asarray(L0_attn_W_theta, np.float32).T)
    wphT = np.ascontiguousarray(np.asarray(L0_attn_W_phi, np.float32).T)
    wgT = np.ascontiguousarray(np.asarray(L0_attn_W_g, np.float32).T)
    wgate = np.ascontiguousarray(np.asarray(L0_ffn_wg, np.float32))
    w0 = np.asarray(L0_ffn_w0, np.float32)
    w1 = np.asarray(L0_ffn_w1, np.float32)
    wo = np.asarray(L0_ffn_wo, np.float32)
    in_maps = []
    for c in range(NCORE):
        ccon = np.zeros((128, 8), np.float32)
        ccon[:, 0] = c * TOK
        ccon[:, 1] = c * 8192 + 16
        ccon[:, 2] = 1.0 if c < 4 else 0.0
        ccon[:, 3] = 0.0 if c < 4 else 1.0
        ccon[:, 4] = c
        ccol8 = np.zeros((8, 2), np.float32)
        ccol8[:c, 0] = 1.0
        cem = np.zeros((128, E), np.float32)
        cem[:, c] = 1.0
        in_maps.append({
            "xown": np.ascontiguousarray(x2d[c * TOK:(c + 1) * TOK]),
            "xfull": xfull,
            "wthT": wthT, "wphT": wphT, "wgT": wgT, "wgate": wgate,
            "w0e": np.ascontiguousarray(w0[c]),
            "w1e": np.ascontiguousarray(w1[c]),
            "woe": np.ascontiguousarray(wo[c]),
            "ccon": ccon, "ccol8": ccol8, "cemask": cem,
        })
    return in_maps


def kernel(L0_attn_x, L0_attn_W_theta, L0_attn_W_phi, L0_attn_W_g,
           L0_ffn_wg, L0_ffn_w0, L0_ffn_w1, L0_ffn_wo):
    nc = _get_nc()
    in_maps = make_in_maps(L0_attn_x, L0_attn_W_theta, L0_attn_W_phi, L0_attn_W_g,
                           L0_ffn_wg, L0_ffn_w0, L0_ffn_w1, L0_ffn_wo)
    res = run_bass_kernel_spmd(nc, in_maps, core_ids=list(range(NCORE)))
    outs = np.concatenate([res.results[c]["out"] for c in range(NCORE)], axis=0)
    return outs.reshape(2, 16384, D)


# version 23
# speedup vs baseline: 1.0274x; 1.0274x over previous
import contextlib
import numpy as np
import ml_dtypes
import concourse.bass as bass
import concourse.mybir as mybir
from concourse.bass_utils import run_bass_kernel_spmd
from concourse.bass import IndirectOffsetOnAxis

F32 = mybir.dt.float32
BF16 = mybir.dt.bfloat16
U32 = mybir.dt.uint32
U8 = mybir.dt.uint8
AF = mybir.ActivationFunctionType
ALU = mybir.AluOpType
AX = mybir.AxisListType.X

NCORE = 8
TOK = 4096
NG = 32
NQ = 4
D = 1024
L = 256
E = 8
FF = 2048
CAP = 2048
PACKW = 16 + E * 8192
NROWX = 32768 + 32
ZROW = 32768.0
BIG = float(2 ** 24)


class Sem:
    def __init__(self, h):
        self.h = h
        self.v = 0


class StubInst:
    def then_inc(self, *a, **k):
        return self


class StubEng:
    def __getattr__(self, name):
        def f(*a, **k):
            return StubInst()
        return f


class Em:

    def __init__(self):
        self.MS = {}
        self.dry = True

    def inc(self, inst, sem, n=1):
        sem.v += n
        if not self.dry:
            inst.then_inc(sem.h, n)
        return inst

    def mark(self, name, sem):
        if self.dry:
            assert name not in self.MS, f"dup mark {name}"
            self.MS[name] = (sem, sem.v)
        else:
            assert self.MS[name][1] == sem.v, (name, self.MS[name][1], sem.v)

    def wait(self, eng, name):
        if not self.dry:
            sem, v = self.MS[name]
            eng.wait_ge(sem.h, v)

    def se(self, eng, sem, inst):
        sem.v += 1
        if not self.dry:
            inst.then_inc(sem.h, 1)
            eng.wait_ge(sem.h, sem.v)


_DBG = {}

ATT_POINTS = [(0, fc) for fc in range(7, 16)] + \
             [(qq, fc) for qq in (1, 2, 3) for fc in range(16)]
ATT_UNITS = {}
_u = 0
for _i, _pt in enumerate(ATT_POINTS):
    _n = 2 if _i < 7 else 1
    ATT_UNITS[_pt] = tuple(range(_u, _u + _n))
    _u += _n
assert _u == 64
ATT_EO_WAIT = {16: 2, 17: 2, 32: 6, 34: 6, 49: 10, 50: 10}


def build_nc(debug=False):
    nc = bass.Bass("TRN2", target_bir_lowering=False)

    xown = nc.dram_tensor("xown", [TOK, D], F32, kind="ExternalInput").ap()
    xfull = nc.dram_tensor("xfull", [NROWX, D], F32, kind="ExternalInput").ap()
    wthT = nc.dram_tensor("wthT", [D, L], F32, kind="ExternalInput").ap()
    wphT = nc.dram_tensor("wphT", [D, L], F32, kind="ExternalInput").ap()
    wgT = nc.dram_tensor("wgT", [D, D], F32, kind="ExternalInput").ap()
    wgate = nc.dram_tensor("wgate", [D, E], F32, kind="ExternalInput").ap()
    w0e = nc.dram_tensor("w0e", [D, FF], F32, kind="ExternalInput").ap()
    w1e = nc.dram_tensor("w1e", [D, FF], F32, kind="ExternalInput").ap()
    woe = nc.dram_tensor("woe", [FF, D], F32, kind="ExternalInput").ap()
    ccon = nc.dram_tensor("ccon", [128, 8], F32, kind="ExternalInput").ap()
    ccol8 = nc.dram_tensor("ccol8", [8, 2], F32, kind="ExternalInput").ap()
    cemask = nc.dram_tensor("cemask", [128, E], F32, kind="ExternalInput").ap()
    out = nc.dram_tensor("out", [TOK, D], F32, kind="ExternalOutput").ap()

    dbg = {}
    if debug:
        for name, shape, dt in (
            ("d_gate", [128, NG * E], F32), ("d_sel", [128, 2 * NG], F32),
            ("d_rk", [128, 2 * NG], F32), ("d_prob", [128, 2 * NG], F32),
            ("d_ctbl", [8, 8], F32), ("d_gidx", [128, 2 * NG], U32),
            ("d_tid", [128, 16], U32), ("d_eout", [CAP, D], BF16),
            ("d_mb", [128, 2 * D], BF16), ("d_th", [128, 2 * TOK], BF16),
        ):
            dbg[name] = nc.dram_tensor(name, shape, dt, kind="ExternalOutput").ap()

    packsrc = nc.dram_tensor("packsrc", [PACKW, 1], F32)
    agout = nc.dram_tensor("agout", [NCORE * PACKW, 1], F32)
    marsrc = nc.dram_tensor("marsrc", [L, D], F32)
    mardst = nc.dram_tensor("mardst", [L, D], F32)
    eoutd = nc.dram_tensor("eoutd", [CAP, D], BF16)
    ag2out = nc.dram_tensor("ag2out", [4 * NCORE * 512, D], BF16)
    partd = nc.dram_tensor("partd", [TOK, D], F32)
    RGRP = [list(range(NCORE))]
    RGRP2 = [[0, 1, 2, 3], [4, 5, 6, 7]]

    ident_bf = np.eye(128, dtype=np.float32).astype(ml_dtypes.bfloat16)
    ltri_np = (np.arange(128)[:, None] < np.arange(128)[None, :]).astype(np.float32)
    ltri8_np = (np.arange(8)[:, None] < np.arange(8)[None, :]).astype(np.float32)
    onesr_np = np.ones((1, 128), np.float32)
    onesc_np = np.ones((128, 1), np.float32)
    io8_np = np.tile(np.arange(E, dtype=np.float32), (128, 8))
    rv8_np = 7.0 - io8_np
    kct_np = np.tile(np.array([[1e-5, -1.0, 1.0, 0.0, 0.5, 2.0, 0.0, 0.0]], np.float32), (128, 1))

    es = contextlib.ExitStack()
    with es:
        def sb(name, shape, dt):
            return es.enter_context(nc.sbuf_tensor(name, shape, dt))

        def psum(name, shape, dt=F32):
            return es.enter_context(nc.psum_tensor(name, shape, dt))

        def mksem(name):
            return Sem(es.enter_context(nc.semaphore(name)))

        K_cc = sb("K_cc", [128, 8], F32)
        K_ct = sb("K_ct", [128, 8], F32)
        K_c8 = sb("K_c8", [8, 2], F32)
        K_em = sb("K_em", [128, E], F32)
        K_id = sb("K_id", [128, 128], BF16)
        K_lt = sb("K_lt", [128, 128], F32)
        K_lt8 = sb("K_lt8", [8, 8], F32)
        K_or = sb("K_or", [1, 128], F32)
        K_oc = sb("K_oc", [128, 1], F32)
        K_io = sb("K_io", [128, 64], F32)
        K_rv = sb("K_rv", [128, 64], F32)
        GTID = sb("GTID", [128, NG], F32)
        RALL = sb("RALL", [128, 16], F32)
        W_th = sb("W_thP", [128, 8 * L], BF16)
        W_ph = sb("W_phP", [128, 8 * L], BF16)
        W_ga = sb("W_gaP", [128, 8 * E], BF16)
        W_0 = sb("W_0", [128, 8 * FF], BF16)
        W_1 = sb("W_1", [128, 8 * FF], BF16)
        W_o = sb("W_o", [128, 16 * D], BF16)
        TH = sb("TH", [128, 2 * TOK], BF16)
        GATE = sb("GATE", [128, NG * E], F32)
        SEL1 = sb("SEL1", [128, NG], F32)
        SEL2 = sb("SEL2", [128, NG], F32)
        P1V = sb("P1V", [128, NG], F32)
        P2V = sb("P2V", [128, NG], F32)
        RK1 = sb("RK1", [128, NG], F32)
        RK2 = sb("RK2", [128, NG], F32)
        GI1 = sb("GI1", [128, NG], U32)
        GI2 = sb("GI2", [128, NG], U32)
        DST1 = sb("DST1", [128, NG], U32)
        DST2 = sb("DST2", [128, NG], U32)
        RBASE = sb("RBASE", [1, E], F32)
        MB = sb("MB", [128, 2 * D], BF16)
        CTBL = sb("CTBL", [8, 8], F32)
        SM1 = sb("SM1", [8, 8], F32)
        CNTE = sb("CNTE", [8, 1], F32)
        PROWS = sb("PROWS", [1, 32], F32)
        PBV = sb("PBV", [128, 24], F32)
        RS_M1 = sb("RS_M1", [128, 64], F32)
        RS_M2 = sb("RS_M2", [128, 64], F32)
        RS_T = sb("RS_T", [128, 64], F32)
        RS_G2 = sb("RS_G2", [128, 64], F32)
        RS_S = sb("RS_S", [128, 64], F32)
        RS_A = sb("RS_A", [128, 8], F32)
        RS_B = sb("RS_B", [128, 8], F32)
        RS_R = sb("RS_R", [1, 64], F32)
        RS_R2 = sb("RS_R2", [1, 64], F32)
        RS_BQ = sb("RS_BQ", [1, 64], F32)
        DSF = sb("DSF", [128, 8], F32)
        STAT = sb("STAT", [128, 16], F32)
        SNK = sb("SNK", [128, D], F32)
        SRCU = sb("SRCU", [128, 16], U32)
        TIDF = sb("TIDF", [128, 16], F32)
        TIDU = sb("TIDU", [128, 16], U32)
        VLD = sb("VLD", [128, 16], F32)
        INTER = sb("INTER", [128, 16 * 512], BF16)
        SIL = sb("SIL", [128, 512], BF16)
        MAR = sb("MAR", [128, D], F32)
        T1S = sb("T1S", [128, D], F32)

        ARENA = sb("ARENA", [128, 46080], U8)
        AR_F = ARENA.bitcast(F32)
        AR_B = ARENA.bitcast(BF16)

        class Carver:
            def __init__(self):
                self.off = 0

            def take(self, cols, dt):
                sz = 4 if dt in (F32, U32) else 2
                self.off = (self.off + 63) // 64 * 64
                st = self.off // sz
                h = AR_F if dt == F32 else AR_B
                ap = h.ap()[:, st:st + cols]
                self.off += cols * sz
                assert self.off <= 46080, self.off
                return ap

        cv = Carver()
        XT = cv.take(3 * D, F32)
        XNB = cv.take(4 * D, BF16)
        XF = cv.take(2 * D, BF16)
        PHS = cv.take(2 * L, BF16)
        GES = cv.take(2 * D, BF16)
        W_gg = cv.take(8 * D, BF16)

        cv = Carver()
        XRAW = cv.take(2 * D, F32)
        XDN = cv.take(4 * D, BF16)
        DFB = cv.take(2 * 8 * 512, BF16)
        MDP = cv.take(2 * D, F32)
        cv = Carver()
        XC = cv.take(4 * D, F32)
        RB1 = W_1[:].__getitem__((slice(None), slice(0, 8 * D)))
        RB2 = W_o[:].__getitem__((slice(None), slice(0, 8 * D)))
        OAC = W_0.bitcast(F32).ap()[:, 0:4 * D]
        EO = AR_B.ap()[:, 20480:20480 + 2 * D]

        def EOSLOT(gs):
            if gs < 8:
                return EO[:, (gs % 2) * D:(gs % 2 + 1) * D]
            th_off = {8: 0, 9: 1024, 10: 4096, 11: 5120,
                      12: 2048, 13: 3072, 14: 6144, 15: 7168}[gs]
            return TH[:, th_off:th_off + D]
        OAC2 = AR_F.ap()[:, 4 * D:8 * D]
        XC2 = W_o.bitcast(F32).ap()[:, 4 * D:8 * D]
        MDPS = (MDP[:, 0:D], MDP[:, D:2 * D], T1S[:], MAR[:])

        PTR = psum("PTR", [128, 1024], BF16)
        CH = psum("CH", [128, 512], F32)
        BG0 = psum("BG0", [128, 512], F32)
        BG1 = psum("BG1", [128, 512], F32)
        PM0 = psum("PM0", [128, 512], F32)
        PM1 = psum("PM1", [128, 512], F32)
        PM2 = psum("PM2", [128, 512], F32)
        PM3 = psum("PM3", [128, 512], F32)

        sX = mksem("sX")
        sX1 = mksem("sX1")
        sX2 = mksem("sX2")
        sX3 = mksem("sX3")
        sWb = mksem("sWb")
        sR0 = mksem("sR0")
        sR1 = mksem("sR1")
        sR2 = mksem("sR2")
        sR3 = mksem("sR3")
        sR4 = mksem("sR4")
        sR5 = mksem("sR5")
        sR6 = mksem("sR6")
        sR7 = mksem("sR7")
        sST1 = mksem("sST1")
        sD4 = [mksem(f"sD{j}") for j in range(4)]
        sK = mksem("sK")
        sW = mksem("sW")
        sV = mksem("sV")
        sA = mksem("sA")
        sT = mksem("sT")
        sG = mksem("sG")
        sI = mksem("sI")
        sC = mksem("sC")
        sE = mksem("sE")
        sE1 = mksem("sE1")
        sE2 = mksem("sE2")
        sE3 = mksem("sE3")
        sST = mksem("sST")
        sPd = mksem("sPd")
        sPd1 = mksem("sPd1")
        sPd2 = mksem("sPd2")
        sPd3 = mksem("sPd3")
        sST2 = mksem("sST2")
        sST3 = mksem("sST3")
        sST4 = mksem("sST4")
        sST5 = mksem("sST5")
        sST6 = mksem("sST6")
        sST7 = mksem("sST7")
        sMd = mksem("sMd")
        sPc = [mksem(f"sPc{j}") for j in range(8)]
        ALLSEMS = [sX, sX1, sX2, sX3, sK, sW, sWb, sR0, sR1, sR2, sR3, sR4, sR5, sR6, sR7, sST1, sST2, sST3, sST4, sST5, sST6, sST7, sV, sA, sT, sG, sI, sC, sE, sE1, sE2, sE3, sST, sPd, sPd1, sPd2, sPd3, sMd] + sD4 + sPc

        block = es.enter_context(nc.Block())

        identt = nc.inline_tensor(ident_bf, "identt")
        ltrit = nc.inline_tensor(ltri_np, "ltrit")
        ltri8t = nc.inline_tensor(ltri8_np, "ltri8t")
        onesrt = nc.inline_tensor(onesr_np, "onesrt")
        onesct = nc.inline_tensor(onesc_np, "onesct")
        io8t = nc.inline_tensor(io8_np, "io8t")
        rv8t = nc.inline_tensor(rv8_np, "rv8t")
        kctt = nc.inline_tensor(kct_np, "kctt")
        for i, val in enumerate([1e-5, -1.0, 1.0, 0.0]):
            nc.const_aps.register(F32, val, K_ct[:, i:i + 1])

        em = Em()

        def t3(ap, e=E):
            return ap.rearrange("p (t e) -> p t e", e=e)

        def col3(ap):
            return ap.rearrange("p (t o) -> p t o", o=1)

        def body_sync(sync):
            for dst, src in ((K_cc[:], ccon), (K_c8[:], ccol8), (K_em[:], cemask),
                             (K_id[:], identt.ap()), (K_lt[:], ltrit.ap()),
                             (K_lt8[:], ltri8t.ap()), (K_or[:], onesrt.ap()),
                             (K_oc[:], onesct.ap()), (K_io[:], io8t.ap()),
                             (K_rv[:], rv8t.ap()), (K_ct[:], kctt.ap())):
                em.inc(sync.dma_start(out=dst, in_=src), sK, 16)
            em.mark("k_done", sK)

            def xload(g):
                if g >= 3:
                    em.wait(sync, f"xnb{g - 3}")
                xls = (sX, sX1, sX2)[g % 3]
                em.inc(sync.dma_start(out=XT[:, (g % 3) * D:(g % 3 + 1) * D],
                                      in_=xown[g * 128:(g + 1) * 128, :]),
                       xls, 16)
                em.mark(f"x{g}", xls)

            for g in range(NG):
                xload(g)
            em.wait(sync, "ag1")
            em.inc(sync.dma_start(
                out=CTBL[:],
                in_=agout.ap().rearrange("(c w) o -> c (w o)", w=PACKW)[0:8, 0:8]), sK, 16)
            em.mark("ctbl", sK)
            for j in range(2):
                em.wait(sync, f"marq{j}")
                em.inc(sync.dma_start(out=marsrc.ap()[j * 128:(j + 1) * 128, :],
                                      in_=MAR[:]), sE, 16)
                em.mark(f"mars{j}", sE)

            def xaload(g):
                if g in (0, 1):
                    em.wait(sync, f"ge{NG - 1}")
                elif g == 3:
                    em.wait(sync, "mbp1")
                elif g >= 4:
                    em.wait(sync, f"pstore{g - 4}")
                xasem = (sX, sX1, sX2, sX3)[g % 4]
                em.inc(sync.dma_start(out=MDPS[g % 4],
                                      in_=xown[g * 128:(g + 1) * 128, :]),
                       xasem, 16)
                em.mark(f"xa{g}", xasem)

            def pstore(g):
                em.wait(sync, f"attevx{2 * g + 1}")
                psem = (sPd, sPd1, sPd2, sPd3)[g % 4]
                em.inc(sync.dma_start(out=partd.ap()[g * 128:(g + 1) * 128, :],
                                      in_=MDPS[g % 4]), psem, 16)
                em.mark(f"pstore{g}", psem)

            xaload(0)
            xaload(1)
            xaload(2)
            for lc in range(2):
                em.wait(sync, "arm")
                if lc == 1:
                    em.wait(sync, "mbp0")
                em.inc(sync.dma_start(out=MAR[:],
                                      in_=mardst.ap()[lc * 128:(lc + 1) * 128, :]), sMd, 16)
                em.mark(f"md{lc}", sMd)
            xaload(3)
            for qq in range(4):
                for g in range(8 * qq, 8 * qq + 8):
                    pstore(g)
                    if g + 4 < NG:
                        xaload(g + 4)
            for g in range(NG):
                em.wait(sync, f"oacd{g}")
                osrc = (OAC if g % 8 < 4 else OAC2)[:, (g % 4) * D:(g % 4 + 1) * D]
                ssem = (sST, sST1, sST2, sST3, sST4, sST5, sST6, sST7)[g % 8]
                em.inc(sync.dma_start(out=out[g * 128:(g + 1) * 128, :], in_=osrc),
                       ssem, 16)
                em.mark(f"outd{g}", ssem)
            if debug:
                em.wait(sync, "oacd31")
                em.inc(sync.dma_start(out=dbg["d_gate"], in_=GATE[:]), sST, 16)
                em.inc(sync.dma_start(out=dbg["d_sel"].rearrange("p (k g) -> p k g", k=2)[:, 0, :], in_=SEL1[:]), sST, 16)
                em.inc(sync.dma_start(out=dbg["d_sel"].rearrange("p (k g) -> p k g", k=2)[:, 1, :], in_=SEL2[:]), sST, 16)
                em.inc(sync.dma_start(out=dbg["d_rk"].rearrange("p (k g) -> p k g", k=2)[:, 0, :], in_=RK1[:]), sST, 16)
                em.inc(sync.dma_start(out=dbg["d_rk"].rearrange("p (k g) -> p k g", k=2)[:, 1, :], in_=RK2[:]), sST, 16)
                em.inc(sync.dma_start(out=dbg["d_prob"].rearrange("p (k g) -> p k g", k=2)[:, 0, :], in_=P1V[:]), sST, 16)
                em.inc(sync.dma_start(out=dbg["d_prob"].rearrange("p (k g) -> p k g", k=2)[:, 1, :], in_=P2V[:]), sST, 16)
                em.inc(sync.dma_start(out=dbg["d_ctbl"], in_=CTBL[:]), sST, 16)
                em.inc(sync.dma_start(out=dbg["d_gidx"].rearrange("p (k g) -> p k g", k=2)[:, 0, :], in_=GI1[:]), sST, 16)
                em.inc(sync.dma_start(out=dbg["d_gidx"].rearrange("p (k g) -> p k g", k=2)[:, 1, :], in_=GI2[:]), sST, 16)
                em.inc(sync.dma_start(out=dbg["d_tid"], in_=TIDU[:]), sST, 16)
                em.inc(sync.dma_start(out=dbg["d_eout"], in_=eoutd.ap()), sST, 16)
                em.inc(sync.dma_start(out=dbg["d_mb"], in_=MB[:]), sST, 16)
                em.inc(sync.dma_start(out=dbg["d_th"], in_=TH[:]), sST, 16)
            if not em.dry:
                for _s in (sST, sST1, sST2, sST3, sST4, sST5, sST6, sST7):
                    sync.wait_ge(_s.h, _s.v)

        def body_tensor(tensor):
            em.wait(tensor, "k_done")
            em.wait(tensor, "wA_done")

            def TR(g):
                em.wait(tensor, f"xnb{g}")
                if g >= 1:
                    em.wait(tensor, f"xf{g - 1}")
                xb = (g % 4) * D
                ti = None
                for k in range(8):
                    ti = tensor.transpose(PTR[:, k * 128:(k + 1) * 128],
                                          XNB[:, xb + k * 128: xb + (k + 1) * 128],
                                          K_id[:])
                em.inc(ti, sT)
                em.mark(f"tr{g}", sT)

            def TG(g):
                em.wait(tensor, f"xf{g}")
                if g >= 1:
                    em.wait(tensor, f"phev{g - 1}")
                if g >= 8 and g % 8 == 0:
                    em.wait(tensor, f"rkread{g // 8 - 1}")
                xf = XF[:, (g % 2) * D:(g % 2 + 1) * D]
                first = True
                ti = None
                for fc in range(8):
                    for lc in range(2):
                        ti = tensor.matmul(CH[:, lc * 128:(lc + 1) * 128],
                                           W_th[:, fc * L + lc * 128: fc * L + lc * 128 + 128],
                                           xf[:, fc * 128:(fc + 1) * 128],
                                           start=first, stop=False)
                        first = False
                    ti = tensor.matmul(CH[:, 256:256 + E],
                                       xf[:, fc * 128:(fc + 1) * 128],
                                       W_ga[:, fc * E:(fc + 1) * E],
                                       start=False, stop=(fc == 7))
                em.inc(ti, sT)
                em.mark(f"tg{g}", sT)

            def PHI(g):
                em.wait(tensor, f"tgev{g}")
                xf = XF[:, (g % 2) * D:(g % 2 + 1) * D]
                ti = None
                for fc in range(8):
                    ti = tensor.matmul(CH[:, 0:L], xf[:, fc * 128:(fc + 1) * 128],
                                       W_ph[:, fc * L:(fc + 1) * L],
                                       start=(fc == 0), stop=(fc == 7))
                em.inc(ti, sT)
                em.mark(f"ph{g}", sT)

            def GP(g):
                if g >= 1:
                    em.wait(tensor, f"geev{g - 1}")
                xf = XF[:, (g % 2) * D:(g % 2 + 1) * D]
                ti = None
                for nh in range(2):
                    bank = BG0 if nh == 0 else BG1
                    for fc in range(8):
                        ti = tensor.matmul(bank[:], xf[:, fc * 128:(fc + 1) * 128],
                                           W_gg[:, fc * D + nh * 512: fc * D + nh * 512 + 512],
                                           start=(fc == 0), stop=(fc == 7))
                em.inc(ti, sT)
                em.mark(f"ge{g}", sT)

            def MM(g):
                em.wait(tensor, f"phev{g}")
                em.wait(tensor, f"geev{g}")
                pb = (g % 2) * L
                gb = (g % 2) * D
                ti = None
                for lc in range(2):
                    for nh in range(2):
                        pm = (PM0, PM1, PM2, PM3)[lc * 2 + nh]
                        ti = tensor.matmul(pm[:], PHS[:, pb + lc * 128: pb + (lc + 1) * 128],
                                           GES[:, gb + nh * 512: gb + (nh + 1) * 512],
                                           start=(g == 0), stop=(g == NG - 1))
                em.inc(ti, sT)
                em.mark(f"M{g}", sT)

            def CHUNK(q):
                em.wait(tensor, f"S{q}")
                ti = tensor.matmul(CH[0:1, 0:64], K_oc[:], RS_S[:], start=True, stop=True)
                em.inc(ti, sT)
                em.mark(f"cs{q}", sT)
                em.wait(tensor, f"bq{q}")
                tensor.matmul(CH[:, 0:64], K_or[:], RS_BQ[:], start=True, stop=False)
                ti = tensor.matmul(CH[:, 0:64], K_lt[:], RS_S[:], start=False, stop=True)
                em.inc(ti, sT)
                em.mark(f"cb{q}", sT)

            for g in range(NG + 2):
                if 2 <= g:
                    GP(g - 2)
                if g < NG:
                    TR(g)
                if 2 <= g:
                    PHI(g - 2)
                    if (g - 2) % 8 == 7:
                        CHUNK((g - 2) // 8)
                if 1 <= g < NG + 1:
                    TG(g - 1)
                if 3 <= g:
                    MM(g - 3)
            MM(NG - 1)

            em.wait(tensor, f"rkread{NQ - 1}")
            em.wait(tensor, "cnte")
            tensor.matmul(CH[0:1, 0:8], CNTE[:], K_lt8[:], start=True, stop=False)
            tensor.matmul(CH[0:1, 8:9], CNTE[:], K_oc[0:8, 0:1], start=False, stop=False)
            ti = tensor.matmul(CH[0:1, 16:24], K_c8[:, 0:1], CTBL[:],
                               start=False, stop=True)
            em.inc(ti, sT)
            em.mark("prows", sT)
            em.wait(tensor, "prows_sb")
            tensor.matmul(CH[:, 0:8], K_or[:], PROWS[0:1, 16:24], start=True, stop=False)
            tensor.matmul(CH[:, 8:9], K_or[:], PROWS[0:1, 8:9], start=False, stop=False)
            ti = tensor.matmul(CH[:, 16:24], K_or[:], PROWS[0:1, 24:32],
                               start=False, stop=True)
            em.inc(ti, sT)
            em.mark("bcasts", sT)

            def dtr_one(sl):
                em.wait(tensor, f"xdnv{sl}")
                em.wait(tensor, f"dfev{sl - 1}" if sl > 0 else f"xf{NG - 1}")
                xb = (sl % 4) * D
                ti = None
                for k in range(8):
                    ti = tensor.transpose(PTR[:, k * 128:(k + 1) * 128],
                                          XDN[:, xb + k * 128: xb + (k + 1) * 128],
                                          K_id[:])
                em.inc(ti, sT)
                em.mark(f"dtr{sl}", sT)

            def att_unit(u):
                g, nh = u // 2, u % 2
                if u == 0:
                    em.wait(tensor, "pb")
                    em.wait(tensor, "mb")
                elif u >= 3:
                    em.wait(tensor, f"attevx{u - 3}")
                if u in ATT_EO_WAIT:
                    em.wait(tensor, f"eo{ATT_EO_WAIT[u]}")
                bank = (CH, PM2, PM3)[u % 3]
                ti = None
                for lc in range(2):
                    ti = tensor.matmul(bank[:, 0:512],
                                       TH[:, lc * TOK + g * 128: lc * TOK + (g + 1) * 128],
                                       MB[:, lc * D + nh * 512: lc * D + nh * 512 + 512],
                                       start=(lc == 0), stop=(lc == 1))
                em.inc(ti, sT)
                em.mark(f"attx{u}", sT)

            def units_at(qq, fc):
                return ATT_UNITS.get((qq, fc), ())

            for sl in range(4):
                dtr_one(sl)
            for qq in range(4):
                for j in range(4):
                    em.wait(tensor, f"dfev{4 * qq + j}")
                if qq == 0:
                    em.wait(tensor, "wB_done")
                dfb = (qq % 2) * (8 * 512)
                for fc in range(16):
                    b0, b1 = ((BG0, BG1) if fc % 2 == 0 else (PM0, PM1))
                    if fc >= 2:
                        em.wait(tensor, f"inter{qq}_{fc - 2}")
                    elif qq > 0:
                        em.wait(tensor, f"inter{qq - 1}_{14 + fc}")
                        if fc == 0:
                            em.wait(tensor, f"eo{4 * qq - 1}")
                    elif qq == 0 and fc < 2:
                        em.wait(tensor, "marq1")
                        em.wait(tensor, f"geev{NG - 1}")
                    for dc in range(8):
                        tensor.matmul(b0[:],
                                      W_0[:, dc * FF + fc * 128: dc * FF + fc * 128 + 128],
                                      DFB[:, dfb + dc * 512: dfb + dc * 512 + 512],
                                      start=(dc == 0), stop=(dc == 7))
                    ti = None
                    for dc in range(8):
                        ti = tensor.matmul(b1[:],
                                           W_1[:, dc * FF + fc * 128: dc * FF + fc * 128 + 128],
                                           DFB[:, dfb + dc * 512: dfb + dc * 512 + 512],
                                           start=(dc == 0), stop=(dc == 7))
                    em.inc(ti, sT)
                    em.mark(f"h{qq}_{fc}", sT)
                    for u in units_at(qq, fc):
                        att_unit(u)
                em.wait(tensor, f"inter{qq}_15")
                for s2 in range(4):
                    gs = qq * 4 + s2
                    if gs > 1:
                        em.wait(tensor, f"eo{gs - 2}")
                    if s2 == 0:
                        em.wait(tensor, f"attevx{16 * qq + 15}")
                    ti = None
                    for nh in range(2):
                        bank = (PM2, PM3)[nh] if gs % 2 == 0 else (BG0, BG1)[nh]
                        for fc in range(16):
                            ti = tensor.matmul(
                                bank[:],
                                INTER[:, fc * 512 + s2 * 128: fc * 512 + s2 * 128 + 128],
                                W_o[:, fc * D + nh * 512: fc * D + nh * 512 + 512],
                                start=(fc == 0), stop=(fc == 15))
                    em.inc(ti, sT)
                    em.mark(f"omm{gs}", sT)
                if qq < 3:
                    for sl in range(4 * (qq + 1), 4 * (qq + 1) + 4):
                        dtr_one(sl)

        def body_vector(v):
            def se(inst):
                em.se(v, sV, inst)
            em.wait(v, "k_done")
            em.wait(v, "iotas")
            se(v.memset(RBASE[:], 0.0))
            se(v.tensor_scalar(GTID[:], GTID[:], K_cc[:, 0:1], None, ALU.add))
            em.mark("gtid", sV)

            def stats_work(g):
                gb = (g % 3) * D
                b = (g % 2) * 8
                em.wait(v, f"x{g}")
                se(v.reduce_sum(STAT[:, b:b + 1], XT[:, gb:gb + D], axis=AX))
                em.mark(f"sum{g}", sV)
                em.wait(v, f"ssq{g}")
                se(v.tensor_scalar(STAT[:, b + 2:b + 3], STAT[:, b:b + 1], -1.0 / D, None, ALU.mult))
                se(v.tensor_mul(STAT[:, b + 3:b + 4], STAT[:, b + 2:b + 3], STAT[:, b + 2:b + 3]))
                se(v.scalar_tensor_tensor(STAT[:, b + 4:b + 5], STAT[:, b + 1:b + 2], 1.0 / D,
                                          STAT[:, b + 3:b + 4], ALU.mult, ALU.subtract))
                em.mark(f"var{g}", sV)
                em.wait(v, f"sqrt{g}")
                se(v.reciprocal(STAT[:, b + 6:b + 7], STAT[:, b + 5:b + 6]))
                if g >= 4:
                    em.wait(v, f"tr{g - 4}")
                se(v.tensor_scalar(XNB[:, (g % 4) * D:(g % 4 + 1) * D], XT[:, gb:gb + D],
                                   STAT[:, b + 2:b + 3],
                                   STAT[:, b + 6:b + 7], ALU.add, ALU.mult))
                em.mark(f"xnb{g}", sV)

            def xfev(g):
                em.wait(v, f"tr{g}")
                if g >= 2:
                    em.wait(v, f"ge{g - 2}")
                se(v.tensor_copy(XF[:, (g % 2) * D:(g % 2 + 1) * D], PTR[:]))
                em.mark(f"xf{g}", sV)

            def tgev(g):
                em.wait(v, f"tg{g}")
                se(v.tensor_copy(GATE[:, g * E:(g + 1) * E], CH[:, 256:256 + E]))
                se(v.tensor_copy(TH[:, g * 128:(g + 1) * 128], CH[:, 0:128]))
                se(v.tensor_copy(TH[:, TOK + g * 128: TOK + (g + 1) * 128],
                                 CH[:, 128:256]))
                em.mark(f"tgev{g}", sV)

            def phev(g):
                em.wait(v, f"ph{g}")
                if g >= 2:
                    em.wait(v, f"M{g - 2}")
                se(v.tensor_copy(PHS[:, (g % 2) * L:(g % 2 + 1) * L], CH[:, 0:L]))
                em.mark(f"phev{g}", sV)

            def chunk_work(q):
                    gsl = GATE[:, q * 64:(q + 1) * 64]
                    g3 = t3(gsl)
                    io3 = t3(K_io[:])
                    S1 = SEL1[:, q * 8:(q + 1) * 8]
                    S2 = SEL2[:, q * 8:(q + 1) * 8]
                    se(v.tensor_reduce(col3(RS_A[:]), g3, AX, ALU.max))
                    se(v.tensor_tensor(t3(RS_M1[:]), g3,
                                       col3(RS_A[:]).broadcast_to([128, 8, E]),
                                       ALU.is_equal))
                    se(v.tensor_mul(RS_T[:], RS_M1[:], K_rv[:]))
                    se(v.tensor_reduce(col3(RS_B[:]), t3(RS_T[:]), AX, ALU.max))
                    se(v.tensor_scalar(S1, RS_B[:], -1.0, 7.0, ALU.mult, ALU.add))
                    se(v.tensor_tensor(t3(RS_M1[:]), io3,
                                       col3(S1).broadcast_to([128, 8, E]), ALU.is_equal))
                    se(v.scalar_tensor_tensor(RS_G2[:], RS_M1[:], -1e30, gsl,
                                              ALU.mult, ALU.add))
                    se(v.tensor_reduce(col3(RS_A[:]), t3(RS_G2[:]), AX, ALU.max))
                    se(v.tensor_reduce(col3(RS_B[:]), g3, AX, ALU.max))
                    se(v.tensor_sub(P1V[:, q * 8:(q + 1) * 8], RS_B[:], RS_A[:]))
                    em.mark(f"dl{q}", sV)
                    se(v.tensor_tensor(t3(RS_M2[:]), t3(RS_G2[:]),
                                       col3(RS_A[:]).broadcast_to([128, 8, E]),
                                       ALU.is_equal))
                    se(v.tensor_mul(RS_T[:], RS_M2[:], K_rv[:]))
                    se(v.tensor_reduce(col3(RS_B[:]), t3(RS_T[:]), AX, ALU.max))
                    se(v.tensor_scalar(S2, RS_B[:], -1.0, 7.0, ALU.mult, ALU.add))
                    se(v.tensor_tensor(t3(RS_M2[:]), io3,
                                       col3(S2).broadcast_to([128, 8, E]), ALU.is_equal))
                    se(v.tensor_add(RS_S[:], RS_M1[:], RS_M2[:]))
                    em.mark(f"S{q}", sV)
                    em.wait(v, f"cs{q}")
                    se(v.tensor_copy(RS_R[:], CH[0:1, 0:64]))
                    bufs = (RS_R, RS_R2)
                    for step, sh in enumerate((1, 2, 4)):
                        a = bufs[step % 2][:].rearrange("p (t e) -> p t e", e=E)
                        b = bufs[(step + 1) % 2][:].rearrange("p (t e) -> p t e", e=E)
                        se(v.tensor_copy(b[:, 0:sh, :], a[:, 0:sh, :]))
                        se(v.tensor_add(b[:, sh:8, :], a[:, sh:8, :], a[:, 0:8 - sh, :]))
                    PFX = bufs[1]
                    r3 = PFX[:].rearrange("p (t e) -> p t e", e=E)
                    bq3 = RS_BQ[:].rearrange("p (t e) -> p t e", e=E)
                    se(v.tensor_copy(bq3[:, 0:1, :],
                                     RBASE[:].rearrange("p (o e) -> p o e", e=E)))
                    se(v.tensor_add(bq3[:, 1:8, :], r3[:, 0:7, :],
                                    RBASE[:].rearrange("p (o e) -> p o e", e=E)
                                    .broadcast_to([1, 7, E])))
                    se(v.tensor_add(RBASE[:], RBASE[:], PFX[:, 56:64]))
                    em.mark(f"bq{q}", sV)
                    em.wait(v, f"cb{q}")
                    se(v.tensor_mul(RS_T[:], RS_M1[:], CH[:, 0:64]))
                    se(v.tensor_reduce(col3(RK1[:, q * 8:(q + 1) * 8]), t3(RS_T[:]),
                                       AX, ALU.add))
                    se(v.tensor_mul(RS_T[:], RS_M2[:], CH[:, 0:64]))
                    se(v.tensor_reduce(col3(RK2[:, q * 8:(q + 1) * 8]), t3(RS_T[:]),
                                           AX, ALU.add))
                    em.mark(f"rkread{q}", sV)
                    se(v.scalar_tensor_tensor(DSF[:], S1, 8192.0,
                                              RK1[:, q * 8:(q + 1) * 8], ALU.mult, ALU.add))
                    se(v.tensor_scalar(DSF[:], DSF[:], 16.0, None, ALU.add))
                    se(v.tensor_copy(DST1[:, q * 8:(q + 1) * 8], DSF[:]))
                    se(v.scalar_tensor_tensor(DSF[:], S2, 8192.0,
                                              RK2[:, q * 8:(q + 1) * 8], ALU.mult, ALU.add))
                    se(v.tensor_scalar(DSF[:], DSF[:], 16.0, None, ALU.add))
                    se(v.tensor_copy(DST2[:, q * 8:(q + 1) * 8], DSF[:]))
                    em.mark(f"pack{q}", sV)


            for g in range(NG):
                stats_work(g)
                if g >= 1:
                    xfev(g - 1)
                if g >= 2:
                    tgev(g - 2)
                    phev(g - 2)
                    if (g - 2) % 8 == 7:
                        chunk_work((g - 2) // 8)
            xfev(NG - 1)
            tgev(NG - 2)
            phev(NG - 2)
            tgev(NG - 1)
            phev(NG - 1)
            chunk_work(NQ - 1)

            em.wait(v, f"M{NG - 1}")
            for j in range(2):
                if j > 0:
                    em.wait(v, f"mars{j - 1}")
                se(v.tensor_copy(MAR[:, 0:512], (PM0, PM1, PM2, PM3)[j * 2 + 0][:]))
                se(v.tensor_copy(MAR[:, 512:1024], (PM0, PM1, PM2, PM3)[j * 2 + 1][:]))
                em.mark(f"marq{j}", sV)

            em.wait(v, "ctbl")
            se(v.tensor_mul(SM1[:], CTBL[:], K_em[0:8, :]))
            se(v.reduce_sum(CNTE[:], SM1[:], axis=AX))
            em.mark("cnte", sV)
            em.wait(v, "prows")
            se(v.tensor_copy(PROWS[0:1, 0:24], CH[0:1, 0:24]))
            se(v.tensor_scalar(PROWS[0:1, 24:32], PROWS[0:1, 0:8], -1.0, None,
                                   ALU.mult))
            em.mark("prows_sb", sV)
            em.wait(v, "bcasts")
            se(v.tensor_copy(PBV[:, 0:24], CH[:, 0:24]))
            em.mark("pb", sV)

            def dA(s):
                se(v.tensor_scalar(DSF[:], PBV[:, 16:24], RALL[:, s:s + 1], None,
                                   ALU.add))
                se(v.tensor_scalar(RS_A[:], DSF[:], 0.0, None, ALU.is_ge))
                se(v.tensor_reduce(STAT[:, 9:10], RS_A[:], AX, ALU.add))
                se(v.scalar_tensor_tensor(RS_B[:], RS_A[:], -BIG, DSF[:],
                                          ALU.mult, ALU.add))
                se(v.tensor_reduce(STAT[:, 10:11], RS_B[:], AX, ALU.min))
                se(v.tensor_scalar(STAT[:, 10:11], STAT[:, 10:11], BIG, None, ALU.add))
                se(v.tensor_scalar(STAT[:, 9:10], STAT[:, 9:10], -1.0, None, ALU.add))
                se(v.scalar_tensor_tensor(STAT[:, 11:12], STAT[:, 9:10], float(PACKW),
                                          STAT[:, 10:11], ALU.mult, ALU.add))
                se(v.tensor_scalar(STAT[:, 11:12], STAT[:, 11:12], K_cc[:, 1:2],
                                   None, ALU.add))
                se(v.tensor_copy(SRCU[:, s:s + 1], STAT[:, 11:12]))
                se(v.tensor_scalar(VLD[:, s:s + 1], RALL[:, s:s + 1], PBV[:, 8:9],
                                       None, ALU.is_lt))
                em.mark(f"src{s}", sV)

            def dB(s):
                em.wait(v, f"idg{s}")
                se(v.tensor_mul(STAT[:, 12:13], TIDF[:, s:s + 1], VLD[:, s:s + 1]))
                se(v.scalar_tensor_tensor(STAT[:, 13:14], VLD[:, s:s + 1], -ZROW,
                                          STAT[:, 12:13], ALU.mult, ALU.add))
                se(v.tensor_scalar(STAT[:, 13:14], STAT[:, 13:14], ZROW, None, ALU.add))
                se(v.tensor_copy(TIDU[:, s:s + 1], STAT[:, 13:14]))
                em.mark(f"tid{s}", sV)

            def dC(s):
                rb = (s % 2) * D
                em.wait(v, f"rowg{s}")
                se(v.reduce_sum(STAT[:, 0:1], XRAW[:, rb:rb + D], axis=AX))
                em.mark(f"dsum{s}", sV)
                em.wait(v, f"dssq{s}")
                se(v.tensor_scalar(STAT[:, 2:3], STAT[:, 0:1], -1.0 / D, None, ALU.mult))
                se(v.tensor_mul(STAT[:, 3:4], STAT[:, 2:3], STAT[:, 2:3]))
                se(v.scalar_tensor_tensor(STAT[:, 4:5], STAT[:, 1:2], 1.0 / D,
                                              STAT[:, 3:4], ALU.mult, ALU.subtract))
                em.mark(f"dvar{s}", sV)

            def dD(s):
                rb = (s % 2) * D
                em.wait(v, f"dsqrt{s}")
                se(v.reciprocal(STAT[:, 6:7], STAT[:, 5:6]))
                if s in (0, 1):
                    em.wait(v, f"xnb{NG - 1}")
                elif s == 2:
                    em.wait(v, f"tr{NG - 2}")
                elif s == 3:
                    em.wait(v, f"tr{NG - 1}")
                else:
                    em.wait(v, f"dtr{s - 4}")
                xb = (s % 4) * D
                se(v.tensor_scalar(XDN[:, xb:xb + D], XRAW[:, rb:rb + D],
                                       STAT[:, 2:3], STAT[:, 6:7], ALU.add, ALU.mult))
                em.mark(f"xdnv{s}", sV)

            def dE(sl):
                em.wait(v, f"dtr{sl}")
                dfb = ((sl // 4) % 2) * (8 * 512)
                off = (sl % 4) * 128
                dst = DFB[:, dfb:dfb + 8 * 512].rearrange(
                    "p (k c) -> p k c", c=512)[:, :, off:off + 128]
                src = PTR[:].rearrange("p (k c) -> p k c", c=128)
                se(v.tensor_copy(dst, src))
                em.mark(f"dfev{sl}", sV)

            def inter_one(qq, fc):
                b1 = BG1 if fc % 2 == 0 else PM1
                em.wait(v, f"sil{qq}_{fc}")
                if qq > 0 and fc == 0:
                    em.wait(v, f"omm{qq * 4 - 1}")
                se(v.tensor_tensor(INTER[:, fc * 512:(fc + 1) * 512], SIL[:],
                                   b1[:], ALU.mult))
                em.mark(f"inter{qq}_{fc}", sV)

            def eo_one(qq, s2):
                gs = qq * 4 + s2
                em.wait(v, f"omm{gs}")
                if 2 <= gs < 8:
                    em.wait(v, f"eod{gs - 2}")
                dst = EOSLOT(gs)
                b0, b1 = (PM2, PM3) if gs % 2 == 0 else (BG0, BG1)
                se(v.tensor_copy(dst[:, 0:512], b0[:]))
                se(v.tensor_copy(dst[:, 512:1024], b1[:]))
                em.mark(f"eo{gs}", sV)

            def mbp_one(lc):
                em.wait(v, f"md{lc}")
                se(v.tensor_scalar(MB[:, lc * D:(lc + 1) * D], MAR[:], 1.0 / 32.0,
                                   None, ALU.mult))
                em.mark(f"mbp{lc}", sV)
                if lc == 1:
                    em.mark("mb", sV)

            def attevx_one(u):
                g, nh = u // 2, u % 2
                em.wait(v, f"attx{u}")
                if nh == 0:
                    em.wait(v, f"xa{g}")
                bank = (CH, PM2, PM3)[u % 3]
                dst = MDPS[g % 4][:, nh * 512: nh * 512 + 512]
                se(v.tensor_tensor(dst, dst, bank[:, 0:512], ALU.add))
                em.mark(f"attevx{u}", sV)

            for f in (lambda: dA(0), lambda: dA(1), lambda: dA(2), lambda: dA(3),
                      lambda: dB(0), lambda: dB(1), lambda: dC(0), lambda: dB(2),
                      lambda: dD(0), lambda: dC(1), lambda: dB(3), lambda: dD(1),
                      lambda: dC(2), lambda: dD(2), lambda: dC(3), lambda: dD(3),
                      lambda: dE(0), lambda: dE(1), lambda: dE(2), lambda: dE(3)):
                f()
            for k, (SEL, RK, GI) in enumerate(((SEL1, RK1, GI1), (SEL2, RK2, GI2))):
                for e in range(E):
                    se(v.tensor_scalar(RS_M1[:, 0:NG], SEL[:], float(e), None,
                                       ALU.is_equal))
                    if e == 0:
                        se(v.tensor_scalar(RS_M2[:, 0:NG], RS_M1[:, 0:NG],
                                           PBV[:, 0:1], None, ALU.mult))
                    else:
                        se(v.scalar_tensor_tensor(RS_M2[:, 0:NG], RS_M1[:, 0:NG],
                                                  PBV[:, e:e + 1], RS_M2[:, 0:NG],
                                                  ALU.mult, ALU.add))
                se(v.tensor_add(RK[:], RK[:], RS_M2[:, 0:NG]))
                se(v.tensor_scalar_min(RS_M1[:, 0:NG], RK[:], float(CAP - 1)))
                se(v.tensor_scalar(RS_T[:, 0:NG], RS_M1[:, 0:NG], 768.0, None, ALU.is_ge))
                se(v.tensor_scalar(RS_S[:, 0:NG], RS_T[:, 0:NG], 768.0, None, ALU.mult))
                se(v.tensor_scalar(RS_T[:, 0:NG], RS_M1[:, 0:NG], 1536.0, None, ALU.is_ge))
                se(v.scalar_tensor_tensor(RS_S[:, 0:NG], RS_T[:, 0:NG], 768.0,
                                          RS_S[:, 0:NG], ALU.mult, ALU.add))
                se(v.tensor_scalar(RS_G2[:, 0:NG], RS_M1[:, 0:NG], 1792.0, None, ALU.is_ge))
                se(v.scalar_tensor_tensor(RS_S[:, 0:NG], RS_G2[:, 0:NG], 256.0,
                                          RS_S[:, 0:NG], ALU.mult, ALU.add))
                se(v.tensor_scalar(RS_M2[:, 0:NG], RS_T[:, 0:NG], -512.0, 768.0,
                                   ALU.mult, ALU.add))
                se(v.tensor_mul(RS_M2[:, 0:NG], RS_M2[:, 0:NG], SEL[:]))
                se(v.scalar_tensor_tensor(RS_M1[:, 0:NG], RS_S[:, 0:NG], 7.0,
                                          RS_M1[:, 0:NG], ALU.mult, ALU.add))
                se(v.tensor_add(RS_M1[:, 0:NG], RS_M1[:, 0:NG], RS_M2[:, 0:NG]))
                se(v.tensor_copy(GI[:], RS_M1[:, 0:NG]))
                em.mark(f"gidx{k}", sV)

            def units_at_v(qq, fc):
                return ATT_UNITS.get((qq, fc), ())

            for qq in range(4):
                S = [4 * (qq + 1) + j for j in range(4)] if qq < 3 else None
                sched = []
                for fc in range(16):
                    sched.append(("i", fc))
                    for u in units_at_v(qq, fc):
                        sched.append(("X", u))
                    if qq == 0 and fc == 5:
                        sched.append(("M", 0))
                    if qq == 0 and fc == 6:
                        sched.append(("M", 1))
                    if S is not None:
                        if fc == 0:
                            sched.append(("A", S[0]))
                        elif fc == 1:
                            sched.append(("A", S[1]))
                        elif fc == 3:
                            sched.append(("B", S[0]))
                        elif fc == 4:
                            sched.append(("A", S[2]))
                        elif fc == 5:
                            sched.append(("B", S[1]))
                        elif fc == 6:
                            sched.append(("C", S[0]))
                        elif fc == 7:
                            sched.append(("A", S[3]))
                        elif fc == 8:
                            sched.append(("D", S[0]))
                            sched.append(("B", S[2]))
                        elif fc == 9:
                            sched.append(("C", S[1]))
                        elif fc == 10:
                            sched.append(("D", S[1]))
                            sched.append(("B", S[3]))
                        elif fc == 11:
                            sched.append(("C", S[2]))
                        elif fc == 12:
                            sched.append(("D", S[2]))
                        elif fc == 13:
                            sched.append(("C", S[3]))
                        elif fc == 14:
                            sched.append(("D", S[3]))
                for kind, val in sched:
                    if kind == "i":
                        inter_one(qq, val)
                    elif kind == "X":
                        attevx_one(val)
                    elif kind == "M":
                        mbp_one(val)
                    elif kind == "A":
                        dA(val)
                    elif kind == "B":
                        dB(val)
                    elif kind == "C":
                        dC(val)
                    elif kind == "D":
                        dD(val)
                for s2 in range(4):
                    eo_one(qq, s2)
                if S is not None:
                    for j in range(4):
                        dE(S[j])

            em.wait(v, f"sig{NQ - 1}")
            for g in range(NG):
                ob = (g % 4) * D
                rb = (g % 8) * D
                xb = (g % 4) * D
                XCS = XC if g % 8 < 4 else XC2
                OCS = OAC if g % 8 < 4 else OAC2
                em.wait(v, f"pc{g}")
                em.wait(v, f"rbg{g}")
                if g >= 8:
                    em.wait(v, f"outd{g - 8}")
                se(v.scalar_tensor_tensor(OCS[:, ob:ob + D], RB1[:, rb:rb + D],
                                          P1V[:, g:g + 1], XCS[:, xb:xb + D],
                                          ALU.mult, ALU.add))
                em.mark(f"xcr{g}", sV)
                se(v.scalar_tensor_tensor(OCS[:, ob:ob + D], RB2[:, rb:rb + D],
                                          P2V[:, g:g + 1], OCS[:, ob:ob + D],
                                          ALU.mult, ALU.add))
                em.mark(f"oacd{g}", sV)

        def body_scalar(scalar):
            def sse(inst):
                em.se(scalar, sA, inst)
            em.wait(scalar, "k_done")
            def sig_work(q):
                em.wait(scalar, f"dl{q}")
                sse(scalar.activation(P1V[:, q * 8:(q + 1) * 8],
                                      P1V[:, q * 8:(q + 1) * 8], AF.Sigmoid))
                sse(scalar.activation(P2V[:, q * 8:(q + 1) * 8],
                                      P1V[:, q * 8:(q + 1) * 8],
                                      AF.Copy, scale=-1.0, bias=1.0))
                em.mark(f"sig{q}", sA)

            def geev(g):
                em.wait(scalar, f"ge{g}")
                if g >= 2:
                    em.wait(scalar, f"M{g - 2}")
                gb = (g % 2) * D
                sse(scalar.activation(GES[:, gb:gb + 512], BG0[:], AF.Copy))
                sse(scalar.activation(GES[:, gb + 512:gb + D], BG1[:], AF.Copy))
                em.mark(f"geev{g}", sA)

            for g in range(NG):
                gb = (g % 3) * D
                b = (g % 2) * 8
                em.wait(scalar, f"x{g}")
                if g >= 2:
                    em.wait(scalar, f"xnb{g - 2}")
                elif g == 0:
                    em.wait(scalar, "pzero")
                sse(scalar.activation(SNK[:], XT[:, gb:gb + D], AF.Square,
                                      accum_out=STAT[:, b + 1:b + 2]))
                em.mark(f"ssq{g}", sA)
                em.wait(scalar, f"var{g}")
                sse(scalar.activation(STAT[:, b + 5:b + 6], STAT[:, b + 4:b + 5], AF.Sqrt, bias=1e-5))
                em.mark(f"sqrt{g}", sA)
                if g >= 2:
                    geev(g - 2)
                if g % 8 == 2 and g >= 10:
                    sig_work((g - 10) // 8)
            geev(NG - 2)
            geev(NG - 1)
            sig_work(NQ - 1)
            def sil_one(qq, fc):
                b0 = BG0 if fc % 2 == 0 else PM0
                em.wait(scalar, f"h{qq}_{fc}")
                if fc >= 1:
                    em.wait(scalar, f"inter{qq}_{fc - 1}")
                elif qq > 0:
                    em.wait(scalar, f"inter{qq - 1}_15")
                sse(scalar.activation(SIL[:], b0[:], AF.Silu))
                em.mark(f"sil{qq}_{fc}", sA)

            def dssq_one(s):
                rb = (s % 2) * D
                em.wait(scalar, f"rowg{s}")
                if s > 0:
                    em.wait(scalar, f"dvar{s - 1}")
                sse(scalar.activation(SNK[:], XRAW[:, rb:rb + D], AF.Square,
                                      accum_out=STAT[:, 1:2]))
                em.mark(f"dssq{s}", sA)

            def dsqrt_one(s):
                em.wait(scalar, f"dvar{s}")
                sse(scalar.activation(STAT[:, 5:6], STAT[:, 4:5], AF.Sqrt, bias=1e-5))
                em.mark(f"dsqrt{s}", sA)

            for s in range(4):
                dssq_one(s)
                dsqrt_one(s)
            def eod_one(gs):
                em.wait(scalar, f"eo{gs}")
                if gs >= 4:
                    em.wait(scalar, f"eod{gs - 4}")
                esem = (sE, sE1, sE2, sE3)[gs % 4]
                em.inc(scalar.dma_start(out=eoutd.ap()[gs * 128:(gs + 1) * 128, :],
                                        in_=EOSLOT(gs)), esem, 16)
                em.mark(f"eod{gs}", esem)

            for qq in range(4):
                S = [4 * (qq + 1) + j for j in range(4)] if qq < 3 else None
                if qq >= 1:
                    for s2 in range(4):
                        eod_one(4 * (qq - 1) + s2)
                for fc in range(16):
                    sil_one(qq, fc)
                    if S is not None:
                        if fc == 6:
                            dssq_one(S[0])
                        elif fc == 7:
                            dsqrt_one(S[0])
                        elif fc == 9:
                            dssq_one(S[1])
                            dsqrt_one(S[1])
                        elif fc == 11:
                            dssq_one(S[2])
                            dsqrt_one(S[2])
                        elif fc == 13:
                            dssq_one(S[3])
                            dsqrt_one(S[3])

            for gs in range(12, 16):
                eod_one(gs)
            for g in range(NG):
                if g == 0:
                    em.wait(scalar, "xdnv14")
                elif g == 1:
                    em.wait(scalar, "xdnv15")
                elif g in (2, 3):
                    em.wait(scalar, "dfev15")
                elif g in (4, 5, 6, 7):
                    em.wait(scalar, "omm15")
                else:
                    em.wait(scalar, f"xcr{g - 8}")
                em.wait(scalar, f"pstore{g}")
                xsem = sPc[g % 8]
                dst = (XC if g % 8 < 4 else XC2)[:, (g % 4) * D:(g % 4 + 1) * D]
                em.inc(scalar.dma_start(out=dst,
                                        in_=partd.ap()[g * 128:(g + 1) * 128, :]),
                       xsem, 16)
                em.mark(f"pc{g}", xsem)

        def body_gpsimd(gp):
            em.inc(gp.memset(SNK[:], 0.0), sG)
            em.mark("zro", sG)
            em.wait(gp, "zro")
            em.inc(gp.dma_start(
                out=packsrc.ap()[16:PACKW, :].rearrange("(p b) o -> p (b o)", p=128),
                in_=SNK[:, 0:512]), sI, 16)
            em.inc(gp.dma_start(
                out=packsrc.ap().rearrange("(a b) o -> a (b o)", a=1)[0:1, 8:16],
                in_=SNK[0:1, 0:8]), sI, 16)
            em.mark("pzero", sI)
            gp.iota(GTID[:], pattern=[[128, NG]], base=0, channel_multiplier=1,
                    allow_small_or_imprecise_dtypes=True)
            em.inc(gp.iota(RALL[:], pattern=[[128, 16]], base=0, channel_multiplier=1,
                           allow_small_or_imprecise_dtypes=True), sG)
            em.mark("iotas", sG)
            em.inc(gp.dma_start(out=W_th.rearrange("p (fc l) -> p fc l", l=L),
                                in_=wthT.rearrange("(fc p) l -> p fc l", p=128)), sW, 16)
            em.inc(gp.dma_start(out=W_ph.rearrange("p (fc l) -> p fc l", l=L),
                                in_=wphT.rearrange("(fc p) l -> p fc l", p=128)), sW, 16)
            em.inc(gp.dma_start(out=W_gg.rearrange("p (fc d) -> p fc d", d=D),
                                in_=wgT.rearrange("(fc p) d -> p fc d", p=128)), sW, 16)
            em.inc(gp.dma_start(out=W_ga.rearrange("p (fc e) -> p fc e", e=E),
                                in_=wgate.rearrange("(fc p) e -> p fc e", p=128)), sW, 16)
            em.mark("wA_done", sW)
            em.inc(gp.dma_start(out=W_0[:].rearrange("p (dc f) -> p dc f", f=FF),
                                in_=w0e.rearrange("(dc p) f -> p dc f", p=128)), sWb, 16)
            em.inc(gp.dma_start(out=W_1[:].rearrange("p (dc f) -> p dc f", f=FF),
                                in_=w1e.rearrange("(dc p) f -> p dc f", p=128)), sWb, 16)
            em.inc(gp.dma_start(out=W_o[:].rearrange("p (fc d) -> p fc d", d=D),
                                in_=woe.rearrange("(fc p) d -> p fc d", p=128)), sWb, 16)
            em.mark("wB_done", sWb)
            em.wait(gp, "pzero")
            for q in range(NQ):
                em.wait(gp, f"pack{q}")
                for g in range(8 * q, 8 * q + 8):
                    em.inc(gp.indirect_dma_start(
                        out=packsrc.ap(),
                        out_offset=IndirectOffsetOnAxis(ap=DST1[:, g:g + 1], axis=0),
                        in_=GTID[:, g:g + 1], in_offset=None), sI, 16)
                    em.inc(gp.indirect_dma_start(
                        out=packsrc.ap(),
                        out_offset=IndirectOffsetOnAxis(ap=DST2[:, g:g + 1], axis=0),
                        in_=GTID[:, g:g + 1], in_offset=None), sI, 16)
            em.inc(gp.dma_start(
                out=packsrc.ap().rearrange("(a b) o -> a (b o)", a=1)[0:1, 0:8],
                in_=RBASE[:]), sI, 16)
            em.mark("packall", sI)
            if not em.dry:
                gp.wait_ge(sI.h, sI.v)
            em.inc(gp.collective_compute("AllGather", ALU.bypass, replica_groups=RGRP,
                                         ins=[packsrc.ap().opt()],
                                         outs=[agout.ap().opt()]), sC)
            em.mark("ag1", sC)
            em.wait(gp, "mars1")
            em.inc(gp.collective_compute("AllReduce", ALU.add, replica_groups=RGRP2,
                                         ins=[marsrc.ap().opt()],
                                         outs=[mardst.ap().opt()]), sC)
            em.mark("arm", sC)
            def idg_one(s):
                em.wait(gp, f"src{s}")
                em.inc(gp.indirect_dma_start(
                    out=TIDF[:, s:s + 1], out_offset=None,
                    in_=agout.ap(),
                    in_offset=IndirectOffsetOnAxis(ap=SRCU[:, s:s + 1], axis=0)),
                       sD4[s % 4], 16)
                em.mark(f"idg{s}", sD4[s % 4])

            def rowg_one(s):
                em.wait(gp, f"tid{s}")
                if s >= 2:
                    em.wait(gp, f"xdnv{s - 2}")
                else:
                    em.wait(gp, f"xnb{NG - 1}")
                em.inc(gp.indirect_dma_start(
                    out=XRAW[:, (s % 2) * D:(s % 2 + 1) * D], out_offset=None,
                    in_=xfull,
                    in_offset=IndirectOffsetOnAxis(ap=TIDU[:, s:s + 1], axis=0)),
                       sD4[s % 4], 16)
                em.mark(f"rowg{s}", sD4[s % 4])

            for t in range(4):
                S = [4 * t + j for j in range(4)]
                idg_one(S[0]); idg_one(S[1]); rowg_one(S[0]); idg_one(S[2])
                rowg_one(S[1]); idg_one(S[3]); rowg_one(S[2]); rowg_one(S[3])
            AG2B = (0, 768, 1536, 1792, 2048)
            for qc in range(4):
                lo, hi = AG2B[qc], AG2B[qc + 1]
                ng = hi // 128
                for j in range(4):
                    top = [g for g in range(ng) if g % 4 == j]
                    if top:
                        em.wait(gp, f"eod{top[-1]}")
                em.inc(gp.collective_compute(
                    "AllGather", ALU.bypass, replica_groups=RGRP,
                    ins=[eoutd.ap()[lo:hi, :].opt()],
                    outs=[ag2out.ap()[NCORE * lo:NCORE * hi, :].opt()]), sC)
                em.mark(f"ag2c{qc}", sC)
            em.wait(gp, "ag2c3")
            em.wait(gp, "gidx0")
            em.wait(gp, "gidx1")
            for g in range(NG):
                if g >= 8:
                    em.wait(gp, f"oacd{g - 8}")
                rbs = (g % 8) * D
                sR = (sR0, sR1, sR2, sR3, sR4, sR5, sR6, sR7)[g % 8]
                em.inc(gp.indirect_dma_start(
                    out=RB1[:, rbs:rbs + D], out_offset=None, in_=ag2out.ap(),
                    in_offset=IndirectOffsetOnAxis(ap=GI1[:, g:g + 1], axis=0)), sR, 16)
                em.inc(gp.indirect_dma_start(
                    out=RB2[:, rbs:rbs + D], out_offset=None, in_=ag2out.ap(),
                    in_offset=IndirectOffsetOnAxis(ap=GI2[:, g:g + 1], axis=0)), sR, 16)
                em.mark(f"rbg{g}", sR)
            if not em.dry:
                for _s in (sST, sST1, sST2, sST3, sST4, sST5, sST6, sST7):
                    gp.wait_ge(_s.h, _s.v)

        bodies = (body_sync, body_tensor, body_vector, body_scalar, body_gpsimd)
        stub = StubEng()
        for b in bodies:
            b(stub)
        _DBG["MS"] = {k: (id(v[0]), v[1]) for k, v in em.MS.items()}
        _DBG["SEMS"] = {id(s): n for s, n in zip(ALLSEMS, ("sX sX1 sX2 sX3 sK sW sWb sR0 sR1 sR2 sR3 sR4 sR5 sR6 sR7 sST1 sST2 sST3 sST4 sST5 sST6 sST7 sV sA sT sG sI sC sE sE1 sE2 sE3 sST sPd sPd1 sPd2 sPd3 sMd sD0 sD1 sD2 sD3".split() + [f"sPc{j}" for j in range(8)]))}
        for s in ALLSEMS:
            s.v = 0
        em.dry = False
        block.sync(body_sync)
        block.tensor(body_tensor)
        block.vector(body_vector)
        block.scalar(body_scalar)
        block.gpsimd(body_gpsimd)

    return nc


_CACHE = {}


def _get_nc():
    if "nc" not in _CACHE:
        _CACHE["nc"] = build_nc(debug=False)
    return _CACHE["nc"]


def make_in_maps(L0_attn_x, L0_attn_W_theta, L0_attn_W_phi, L0_attn_W_g,
                 L0_ffn_wg, L0_ffn_w0, L0_ffn_w1, L0_ffn_wo):
    x2d = np.ascontiguousarray(np.asarray(L0_attn_x, np.float32).reshape(32768, D))
    xfull = np.zeros((NROWX, D), np.float32)
    xfull[:32768] = x2d
    wthT = np.ascontiguousarray(np.asarray(L0_attn_W_theta, np.float32).T)
    wphT = np.ascontiguousarray(np.asarray(L0_attn_W_phi, np.float32).T)
    wgT = np.ascontiguousarray(np.asarray(L0_attn_W_g, np.float32).T)
    wgate = np.ascontiguousarray(np.asarray(L0_ffn_wg, np.float32))
    w0 = np.asarray(L0_ffn_w0, np.float32)
    w1 = np.asarray(L0_ffn_w1, np.float32)
    wo = np.asarray(L0_ffn_wo, np.float32)
    in_maps = []
    for c in range(NCORE):
        ccon = np.zeros((128, 8), np.float32)
        ccon[:, 0] = c * TOK
        ccon[:, 1] = c * 8192 + 16
        ccon[:, 2] = 1.0 if c < 4 else 0.0
        ccon[:, 3] = 0.0 if c < 4 else 1.0
        ccon[:, 4] = c
        ccol8 = np.zeros((8, 2), np.float32)
        ccol8[:c, 0] = 1.0
        cem = np.zeros((128, E), np.float32)
        cem[:, c] = 1.0
        in_maps.append({
            "xown": np.ascontiguousarray(x2d[c * TOK:(c + 1) * TOK]),
            "xfull": xfull,
            "wthT": wthT, "wphT": wphT, "wgT": wgT, "wgate": wgate,
            "w0e": np.ascontiguousarray(w0[c]),
            "w1e": np.ascontiguousarray(w1[c]),
            "woe": np.ascontiguousarray(wo[c]),
            "ccon": ccon, "ccol8": ccol8, "cemask": cem,
        })
    return in_maps


def kernel(L0_attn_x, L0_attn_W_theta, L0_attn_W_phi, L0_attn_W_g,
           L0_ffn_wg, L0_ffn_w0, L0_ffn_w1, L0_ffn_wo):
    nc = _get_nc()
    in_maps = make_in_maps(L0_attn_x, L0_attn_W_theta, L0_attn_W_phi, L0_attn_W_g,
                           L0_ffn_wg, L0_ffn_w0, L0_ffn_w1, L0_ffn_wo)
    res = run_bass_kernel_spmd(nc, in_maps, core_ids=list(range(NCORE)))
    outs = np.concatenate([res.results[c]["out"] for c in range(NCORE)], axis=0)
    return outs.reshape(2, 16384, D)


# version 24
# speedup vs baseline: 1.0583x; 1.0301x over previous
import contextlib
import numpy as np
import ml_dtypes
import concourse.bass as bass
import concourse.mybir as mybir
from concourse.bass_utils import run_bass_kernel_spmd
from concourse.bass import IndirectOffsetOnAxis

F32 = mybir.dt.float32
BF16 = mybir.dt.bfloat16
U32 = mybir.dt.uint32
U8 = mybir.dt.uint8
AF = mybir.ActivationFunctionType
ALU = mybir.AluOpType
AX = mybir.AxisListType.X

NCORE = 8
TOK = 4096
NG = 32
NQ = 4
D = 1024
L = 256
E = 8
FF = 2048
CAP = 2048
PACKW = 16 + E * 8192
NROWX = 32768 + 32
ZROW = 32768.0
BIG = float(2 ** 24)


class Sem:
    def __init__(self, h):
        self.h = h
        self.v = 0


class StubInst:
    def then_inc(self, *a, **k):
        return self


class StubEng:
    def __getattr__(self, name):
        def f(*a, **k):
            return StubInst()
        return f


class Em:

    def __init__(self):
        self.MS = {}
        self.dry = True

    def inc(self, inst, sem, n=1):
        sem.v += n
        if not self.dry:
            inst.then_inc(sem.h, n)
        return inst

    def mark(self, name, sem):
        if self.dry:
            assert name not in self.MS, f"dup mark {name}"
            self.MS[name] = (sem, sem.v)
        else:
            assert self.MS[name][1] == sem.v, (name, self.MS[name][1], sem.v)

    def wait(self, eng, name):
        if not self.dry:
            sem, v = self.MS[name]
            eng.wait_ge(sem.h, v)

    def se(self, eng, sem, inst):
        sem.v += 1
        if not self.dry:
            inst.then_inc(sem.h, 1)
            eng.wait_ge(sem.h, sem.v)


_DBG = {}

ATT_POINTS = [(0, fc) for fc in range(7, 16)] + \
             [(qq, fc) for qq in (1, 2, 3) for fc in range(16)]
ATT_UNITS = {}
_u = 0
for _i, _pt in enumerate(ATT_POINTS):
    _n = 2 if _i < 7 else 1
    ATT_UNITS[_pt] = tuple(range(_u, _u + _n))
    _u += _n
assert _u == 64
ATT_EO_WAIT = {16: 2, 17: 2, 32: 6, 34: 6, 49: 10, 50: 10}


def build_nc(debug=False):
    nc = bass.Bass("TRN2", target_bir_lowering=False)

    xown = nc.dram_tensor("xown", [TOK, D], F32, kind="ExternalInput").ap()
    xfull = nc.dram_tensor("xfull", [NROWX, D], F32, kind="ExternalInput").ap()
    wthT = nc.dram_tensor("wthT", [D, L], F32, kind="ExternalInput").ap()
    wphT = nc.dram_tensor("wphT", [D, L], F32, kind="ExternalInput").ap()
    wgT = nc.dram_tensor("wgT", [D, D], F32, kind="ExternalInput").ap()
    wgate = nc.dram_tensor("wgate", [D, E], F32, kind="ExternalInput").ap()
    w0e = nc.dram_tensor("w0e", [D, FF], F32, kind="ExternalInput").ap()
    w1e = nc.dram_tensor("w1e", [D, FF], F32, kind="ExternalInput").ap()
    woe = nc.dram_tensor("woe", [FF, D], F32, kind="ExternalInput").ap()
    ccon = nc.dram_tensor("ccon", [128, 8], F32, kind="ExternalInput").ap()
    ccol8 = nc.dram_tensor("ccol8", [8, 2], F32, kind="ExternalInput").ap()
    cemask = nc.dram_tensor("cemask", [128, E], F32, kind="ExternalInput").ap()
    out = nc.dram_tensor("out", [TOK, D], F32, kind="ExternalOutput").ap()

    dbg = {}
    if debug:
        for name, shape, dt in (
            ("d_gate", [128, NG * E], F32), ("d_sel", [128, 2 * NG], F32),
            ("d_rk", [128, 2 * NG], F32), ("d_prob", [128, 2 * NG], F32),
            ("d_ctbl", [8, 8], F32), ("d_gidx", [128, 2 * NG], U32),
            ("d_tid", [128, 16], U32), ("d_eout", [CAP, D], BF16),
            ("d_mb", [128, 2 * D], BF16), ("d_th", [128, 2 * TOK], BF16),
        ):
            dbg[name] = nc.dram_tensor(name, shape, dt, kind="ExternalOutput").ap()

    packsrc = nc.dram_tensor("packsrc", [PACKW, 1], F32)
    agout = nc.dram_tensor("agout", [NCORE * PACKW, 1], F32)
    marsrc = nc.dram_tensor("marsrc", [L, D], F32)
    mardst = nc.dram_tensor("mardst", [L, D], F32)
    eoutd = nc.dram_tensor("eoutd", [CAP, D], BF16)
    ag2out = nc.dram_tensor("ag2out", [4 * NCORE * 512, D], BF16)
    partd = nc.dram_tensor("partd", [TOK, D], F32)
    RGRP = [list(range(NCORE))]
    RGRP2 = [[0, 1, 2, 3], [4, 5, 6, 7]]

    ident_bf = np.eye(128, dtype=np.float32).astype(ml_dtypes.bfloat16)
    ltri_np = (np.arange(128)[:, None] < np.arange(128)[None, :]).astype(np.float32)
    ltri8_np = (np.arange(8)[:, None] < np.arange(8)[None, :]).astype(np.float32)
    onesr_np = np.ones((1, 128), np.float32)
    onesc_np = np.ones((128, 1), np.float32)
    io8_np = np.tile(np.arange(E, dtype=np.float32), (128, 8))
    rv8_np = 7.0 - io8_np
    kct_np = np.tile(np.array([[1e-5, -1.0, 1.0, 0.0, 0.5, 2.0, 0.0, 0.0]], np.float32), (128, 1))

    es = contextlib.ExitStack()
    with es:
        def sb(name, shape, dt):
            return es.enter_context(nc.sbuf_tensor(name, shape, dt))

        def psum(name, shape, dt=F32):
            return es.enter_context(nc.psum_tensor(name, shape, dt))

        def mksem(name):
            return Sem(es.enter_context(nc.semaphore(name)))

        K_cc = sb("K_cc", [128, 8], F32)
        K_ct = sb("K_ct", [128, 8], F32)
        K_c8 = sb("K_c8", [8, 2], F32)
        K_em = sb("K_em", [128, E], F32)
        K_id = sb("K_id", [128, 128], BF16)
        K_lt = sb("K_lt", [128, 128], F32)
        K_lt8 = sb("K_lt8", [8, 8], F32)
        K_or = sb("K_or", [1, 128], F32)
        K_oc = sb("K_oc", [128, 1], F32)
        K_io = sb("K_io", [128, 64], F32)
        K_rv = sb("K_rv", [128, 64], F32)
        GTID = sb("GTID", [128, NG], F32)
        RALL = sb("RALL", [128, 16], F32)
        W_th = sb("W_thP", [128, 8 * L], BF16)
        W_ph = sb("W_phP", [128, 8 * L], BF16)
        W_ga = sb("W_gaP", [128, 8 * E], BF16)
        W_0 = sb("W_0", [128, 8 * FF], BF16)
        W_1 = sb("W_1", [128, 8 * FF], BF16)
        W_o = sb("W_o", [128, 16 * D], BF16)
        TH = sb("TH", [128, 2 * TOK], BF16)
        GATE = sb("GATE", [128, NG * E], F32)
        SEL1 = sb("SEL1", [128, NG], F32)
        SEL2 = sb("SEL2", [128, NG], F32)
        P1V = sb("P1V", [128, NG], F32)
        P2V = sb("P2V", [128, NG], F32)
        RK1 = sb("RK1", [128, NG], F32)
        RK2 = sb("RK2", [128, NG], F32)
        GI1 = sb("GI1", [128, NG], U32)
        GI2 = sb("GI2", [128, NG], U32)
        DST1 = sb("DST1", [128, NG], U32)
        DST2 = sb("DST2", [128, NG], U32)
        RBASE = sb("RBASE", [1, E], F32)
        MB = sb("MB", [128, 2 * D], BF16)
        CTBL = sb("CTBL", [8, 8], F32)
        SM1 = sb("SM1", [8, 8], F32)
        CNTE = sb("CNTE", [8, 1], F32)
        PROWS = sb("PROWS", [1, 32], F32)
        PBV = sb("PBV", [128, 24], F32)
        RS_M1 = sb("RS_M1", [128, 64], F32)
        RS_M2 = sb("RS_M2", [128, 64], F32)
        RS_T = sb("RS_T", [128, 64], F32)
        RS_G2 = sb("RS_G2", [128, 64], F32)
        RS_S = sb("RS_S", [128, 64], F32)
        RS_A = sb("RS_A", [128, 8], F32)
        RS_B = sb("RS_B", [128, 8], F32)
        RS_R = sb("RS_R", [1, 64], F32)
        RS_R2 = sb("RS_R2", [1, 64], F32)
        RS_BQ = sb("RS_BQ", [1, 64], F32)
        DSF = sb("DSF", [128, 8], F32)
        STAT = sb("STAT", [128, 32], F32)
        SNK = sb("SNK", [128, D], F32)
        SRCU = sb("SRCU", [128, 16], U32)
        TIDF = sb("TIDF", [128, 16], F32)
        TIDU = sb("TIDU", [128, 16], U32)
        VLD = sb("VLD", [128, 16], F32)
        INTER = sb("INTER", [128, 16 * 512], BF16)
        SIL = sb("SIL", [128, 512], BF16)
        MAR = sb("MAR", [128, D], F32)
        T1S = sb("T1S", [128, D], F32)

        ARENA = sb("ARENA", [128, 46080], U8)
        AR_F = ARENA.bitcast(F32)
        AR_B = ARENA.bitcast(BF16)

        class Carver:
            def __init__(self):
                self.off = 0

            def take(self, cols, dt):
                sz = 4 if dt in (F32, U32) else 2
                self.off = (self.off + 63) // 64 * 64
                st = self.off // sz
                h = AR_F if dt == F32 else AR_B
                ap = h.ap()[:, st:st + cols]
                self.off += cols * sz
                assert self.off <= 46080, self.off
                return ap

        cv = Carver()
        XT = cv.take(3 * D, F32)
        XNB = cv.take(4 * D, BF16)
        XF = cv.take(2 * D, BF16)
        PHS = cv.take(2 * L, BF16)
        GES = cv.take(2 * D, BF16)
        W_gg = cv.take(8 * D, BF16)

        cv = Carver()
        XRAW = cv.take(2 * D, F32)
        XDN = cv.take(4 * D, BF16)
        DFB = cv.take(2 * 8 * 512, BF16)
        MDP = cv.take(2 * D, F32)
        cv = Carver()
        XC = cv.take(4 * D, F32)
        RB1 = W_1[:].__getitem__((slice(None), slice(0, 8 * D)))
        RB2 = W_o[:].__getitem__((slice(None), slice(0, 8 * D)))
        OAC = W_0.bitcast(F32).ap()[:, 0:4 * D]
        EO = AR_B.ap()[:, 20480:20480 + 2 * D]

        def EOSLOT(gs):
            if gs < 8:
                return EO[:, (gs % 2) * D:(gs % 2 + 1) * D]
            th_off = {8: 0, 9: 1024, 10: 4096, 11: 5120,
                      12: 2048, 13: 3072, 14: 6144, 15: 7168}[gs]
            return TH[:, th_off:th_off + D]
        OAC2 = AR_F.ap()[:, 4 * D:8 * D]
        XC2 = W_o.bitcast(F32).ap()[:, 4 * D:8 * D]
        MDPS = (MDP[:, 0:D], MDP[:, D:2 * D], T1S[:], MAR[:])

        PTR = psum("PTR", [128, 1024], BF16)
        CH = psum("CH", [128, 512], F32)
        BG0 = psum("BG0", [128, 512], F32)
        BG1 = psum("BG1", [128, 512], F32)
        PM0 = psum("PM0", [128, 512], F32)
        PM1 = psum("PM1", [128, 512], F32)
        PM2 = psum("PM2", [128, 512], F32)
        PM3 = psum("PM3", [128, 512], F32)

        sX = mksem("sX")
        sX1 = mksem("sX1")
        sX2 = mksem("sX2")
        sX3 = mksem("sX3")
        sWb = mksem("sWb")
        sR0 = mksem("sR0")
        sR1 = mksem("sR1")
        sR2 = mksem("sR2")
        sR3 = mksem("sR3")
        sR4 = mksem("sR4")
        sR5 = mksem("sR5")
        sR6 = mksem("sR6")
        sR7 = mksem("sR7")
        sST1 = mksem("sST1")
        sD4 = [mksem(f"sD{j}") for j in range(4)]
        sK = mksem("sK")
        sW = mksem("sW")
        sV = mksem("sV")
        sA = mksem("sA")
        sT = mksem("sT")
        sG = mksem("sG")
        sI = mksem("sI")
        sC = mksem("sC")
        sE = mksem("sE")
        sE1 = mksem("sE1")
        sE2 = mksem("sE2")
        sE3 = mksem("sE3")
        sST = mksem("sST")
        sPd = mksem("sPd")
        sPd1 = mksem("sPd1")
        sPd2 = mksem("sPd2")
        sPd3 = mksem("sPd3")
        sST2 = mksem("sST2")
        sST3 = mksem("sST3")
        sST4 = mksem("sST4")
        sST5 = mksem("sST5")
        sST6 = mksem("sST6")
        sST7 = mksem("sST7")
        sMd = mksem("sMd")
        sPc = [mksem(f"sPc{j}") for j in range(8)]
        ALLSEMS = [sX, sX1, sX2, sX3, sK, sW, sWb, sR0, sR1, sR2, sR3, sR4, sR5, sR6, sR7, sST1, sST2, sST3, sST4, sST5, sST6, sST7, sV, sA, sT, sG, sI, sC, sE, sE1, sE2, sE3, sST, sPd, sPd1, sPd2, sPd3, sMd] + sD4 + sPc

        block = es.enter_context(nc.Block())

        identt = nc.inline_tensor(ident_bf, "identt")
        ltrit = nc.inline_tensor(ltri_np, "ltrit")
        ltri8t = nc.inline_tensor(ltri8_np, "ltri8t")
        onesrt = nc.inline_tensor(onesr_np, "onesrt")
        onesct = nc.inline_tensor(onesc_np, "onesct")
        io8t = nc.inline_tensor(io8_np, "io8t")
        rv8t = nc.inline_tensor(rv8_np, "rv8t")
        kctt = nc.inline_tensor(kct_np, "kctt")
        for i, val in enumerate([1e-5, -1.0, 1.0, 0.0]):
            nc.const_aps.register(F32, val, K_ct[:, i:i + 1])

        em = Em()

        def t3(ap, e=E):
            return ap.rearrange("p (t e) -> p t e", e=e)

        def col3(ap):
            return ap.rearrange("p (t o) -> p t o", o=1)

        def body_sync(sync):
            for dst, src in ((K_cc[:], ccon), (K_c8[:], ccol8), (K_em[:], cemask),
                             (K_id[:], identt.ap()), (K_lt[:], ltrit.ap()),
                             (K_lt8[:], ltri8t.ap()), (K_or[:], onesrt.ap()),
                             (K_oc[:], onesct.ap()), (K_io[:], io8t.ap()),
                             (K_rv[:], rv8t.ap()), (K_ct[:], kctt.ap())):
                em.inc(sync.dma_start(out=dst, in_=src), sK, 16)
            em.mark("k_done", sK)

            def xload(g):
                if g >= 3:
                    em.wait(sync, f"xnb{g - 3}")
                xls = (sX, sX1, sX2)[g % 3]
                em.inc(sync.dma_start(out=XT[:, (g % 3) * D:(g % 3 + 1) * D],
                                      in_=xown[g * 128:(g + 1) * 128, :]),
                       xls, 16)
                em.mark(f"x{g}", xls)

            for g in range(NG):
                xload(g)
            em.wait(sync, "ag1")
            em.inc(sync.dma_start(
                out=CTBL[:],
                in_=agout.ap().rearrange("(c w) o -> c (w o)", w=PACKW)[0:8, 0:8]), sK, 16)
            em.mark("ctbl", sK)
            for j in range(2):
                em.wait(sync, f"marq{j}")
                em.inc(sync.dma_start(out=marsrc.ap()[j * 128:(j + 1) * 128, :],
                                      in_=MAR[:]), sE, 16)
                em.mark(f"mars{j}", sE)

            def xaload(g):
                if g in (0, 1):
                    em.wait(sync, f"ge{NG - 1}")
                elif g == 3:
                    em.wait(sync, "mbp1")
                elif g >= 4:
                    em.wait(sync, f"pstore{g - 4}")
                xasem = (sX, sX1, sX2, sX3)[g % 4]
                em.inc(sync.dma_start(out=MDPS[g % 4],
                                      in_=xown[g * 128:(g + 1) * 128, :]),
                       xasem, 16)
                em.mark(f"xa{g}", xasem)

            def pstore(g):
                em.wait(sync, f"attevx{2 * g + 1}")
                psem = (sPd, sPd1, sPd2, sPd3)[g % 4]
                em.inc(sync.dma_start(out=partd.ap()[g * 128:(g + 1) * 128, :],
                                      in_=MDPS[g % 4]), psem, 16)
                em.mark(f"pstore{g}", psem)

            xaload(0)
            xaload(1)
            xaload(2)
            for lc in range(2):
                em.wait(sync, "arm")
                if lc == 1:
                    em.wait(sync, "mbp0")
                em.inc(sync.dma_start(out=MAR[:],
                                      in_=mardst.ap()[lc * 128:(lc + 1) * 128, :]), sMd, 16)
                em.mark(f"md{lc}", sMd)
            xaload(3)
            for qq in range(4):
                for g in range(8 * qq, 8 * qq + 8):
                    pstore(g)
                    if g + 4 < NG:
                        xaload(g + 4)
            for g in range(NG):
                em.wait(sync, f"oacd{g}")
                osrc = (OAC if g % 8 < 4 else OAC2)[:, (g % 4) * D:(g % 4 + 1) * D]
                ssem = (sST, sST1, sST2, sST3, sST4, sST5, sST6, sST7)[g % 8]
                em.inc(sync.dma_start(out=out[g * 128:(g + 1) * 128, :], in_=osrc),
                       ssem, 16)
                em.mark(f"outd{g}", ssem)
            if debug:
                em.wait(sync, "oacd31")
                em.inc(sync.dma_start(out=dbg["d_gate"], in_=GATE[:]), sST, 16)
                em.inc(sync.dma_start(out=dbg["d_sel"].rearrange("p (k g) -> p k g", k=2)[:, 0, :], in_=SEL1[:]), sST, 16)
                em.inc(sync.dma_start(out=dbg["d_sel"].rearrange("p (k g) -> p k g", k=2)[:, 1, :], in_=SEL2[:]), sST, 16)
                em.inc(sync.dma_start(out=dbg["d_rk"].rearrange("p (k g) -> p k g", k=2)[:, 0, :], in_=RK1[:]), sST, 16)
                em.inc(sync.dma_start(out=dbg["d_rk"].rearrange("p (k g) -> p k g", k=2)[:, 1, :], in_=RK2[:]), sST, 16)
                em.inc(sync.dma_start(out=dbg["d_prob"].rearrange("p (k g) -> p k g", k=2)[:, 0, :], in_=P1V[:]), sST, 16)
                em.inc(sync.dma_start(out=dbg["d_prob"].rearrange("p (k g) -> p k g", k=2)[:, 1, :], in_=P2V[:]), sST, 16)
                em.inc(sync.dma_start(out=dbg["d_ctbl"], in_=CTBL[:]), sST, 16)
                em.inc(sync.dma_start(out=dbg["d_gidx"].rearrange("p (k g) -> p k g", k=2)[:, 0, :], in_=GI1[:]), sST, 16)
                em.inc(sync.dma_start(out=dbg["d_gidx"].rearrange("p (k g) -> p k g", k=2)[:, 1, :], in_=GI2[:]), sST, 16)
                em.inc(sync.dma_start(out=dbg["d_tid"], in_=TIDU[:]), sST, 16)
                em.inc(sync.dma_start(out=dbg["d_eout"], in_=eoutd.ap()), sST, 16)
                em.inc(sync.dma_start(out=dbg["d_mb"], in_=MB[:]), sST, 16)
                em.inc(sync.dma_start(out=dbg["d_th"], in_=TH[:]), sST, 16)
            if not em.dry:
                for _s in (sST, sST1, sST2, sST3, sST4, sST5, sST6, sST7):
                    sync.wait_ge(_s.h, _s.v)

        def body_tensor(tensor):
            em.wait(tensor, "k_done")
            em.wait(tensor, "wA_done")

            def TR(g):
                em.wait(tensor, f"xnb{g}")
                if g >= 1:
                    em.wait(tensor, f"xf{g - 1}")
                xb = (g % 4) * D
                ti = None
                for k in range(8):
                    ti = tensor.transpose(PTR[:, k * 128:(k + 1) * 128],
                                          XNB[:, xb + k * 128: xb + (k + 1) * 128],
                                          K_id[:])
                em.inc(ti, sT)
                em.mark(f"tr{g}", sT)

            def TG(g):
                em.wait(tensor, f"xf{g}")
                if g >= 1:
                    em.wait(tensor, f"phev{g - 1}")
                if g >= 8 and g % 8 == 0:
                    em.wait(tensor, f"rkread{g // 8 - 1}")
                xf = XF[:, (g % 2) * D:(g % 2 + 1) * D]
                first = True
                ti = None
                for fc in range(8):
                    for lc in range(2):
                        ti = tensor.matmul(CH[:, lc * 128:(lc + 1) * 128],
                                           W_th[:, fc * L + lc * 128: fc * L + lc * 128 + 128],
                                           xf[:, fc * 128:(fc + 1) * 128],
                                           start=first, stop=False)
                        first = False
                    ti = tensor.matmul(CH[:, 256:256 + E],
                                       xf[:, fc * 128:(fc + 1) * 128],
                                       W_ga[:, fc * E:(fc + 1) * E],
                                       start=False, stop=(fc == 7))
                em.inc(ti, sT)
                em.mark(f"tg{g}", sT)

            def PHI(g):
                em.wait(tensor, f"tgev{g}")
                xf = XF[:, (g % 2) * D:(g % 2 + 1) * D]
                ti = None
                for fc in range(8):
                    ti = tensor.matmul(CH[:, 0:L], xf[:, fc * 128:(fc + 1) * 128],
                                       W_ph[:, fc * L:(fc + 1) * L],
                                       start=(fc == 0), stop=(fc == 7))
                em.inc(ti, sT)
                em.mark(f"ph{g}", sT)

            def GP(g):
                if g >= 1:
                    em.wait(tensor, f"geev{g - 1}")
                xf = XF[:, (g % 2) * D:(g % 2 + 1) * D]
                ti = None
                for nh in range(2):
                    bank = BG0 if nh == 0 else BG1
                    for fc in range(8):
                        ti = tensor.matmul(bank[:], xf[:, fc * 128:(fc + 1) * 128],
                                           W_gg[:, fc * D + nh * 512: fc * D + nh * 512 + 512],
                                           start=(fc == 0), stop=(fc == 7))
                em.inc(ti, sT)
                em.mark(f"ge{g}", sT)

            def MM(g):
                em.wait(tensor, f"phev{g}")
                em.wait(tensor, f"geev{g}")
                pb = (g % 2) * L
                gb = (g % 2) * D
                ti = None
                for lc in range(2):
                    for nh in range(2):
                        pm = (PM0, PM1, PM2, PM3)[lc * 2 + nh]
                        ti = tensor.matmul(pm[:], PHS[:, pb + lc * 128: pb + (lc + 1) * 128],
                                           GES[:, gb + nh * 512: gb + (nh + 1) * 512],
                                           start=(g == 0), stop=(g == NG - 1))
                em.inc(ti, sT)
                em.mark(f"M{g}", sT)

            def CHUNK(q):
                em.wait(tensor, f"S{q}")
                ti = tensor.matmul(CH[0:1, 0:64], K_oc[:], RS_S[:], start=True, stop=True)
                em.inc(ti, sT)
                em.mark(f"cs{q}", sT)
                em.wait(tensor, f"bq{q}")
                tensor.matmul(CH[:, 0:64], K_or[:], RS_BQ[:], start=True, stop=False)
                ti = tensor.matmul(CH[:, 0:64], K_lt[:], RS_S[:], start=False, stop=True)
                em.inc(ti, sT)
                em.mark(f"cb{q}", sT)

            for g in range(NG + 2):
                if 2 <= g:
                    GP(g - 2)
                if g < NG:
                    TR(g)
                if 2 <= g:
                    PHI(g - 2)
                    if (g - 2) % 8 == 7:
                        CHUNK((g - 2) // 8)
                if 1 <= g < NG + 1:
                    TG(g - 1)
                if 3 <= g:
                    MM(g - 3)
            MM(NG - 1)

            em.wait(tensor, f"rkread{NQ - 1}")
            em.wait(tensor, "cnte")
            tensor.matmul(CH[0:1, 0:8], CNTE[:], K_lt8[:], start=True, stop=False)
            tensor.matmul(CH[0:1, 8:9], CNTE[:], K_oc[0:8, 0:1], start=False, stop=False)
            ti = tensor.matmul(CH[0:1, 16:24], K_c8[:, 0:1], CTBL[:],
                               start=False, stop=True)
            em.inc(ti, sT)
            em.mark("prows", sT)
            em.wait(tensor, "prows_sb")
            tensor.matmul(CH[:, 0:8], K_or[:], PROWS[0:1, 16:24], start=True, stop=False)
            tensor.matmul(CH[:, 8:9], K_or[:], PROWS[0:1, 8:9], start=False, stop=False)
            ti = tensor.matmul(CH[:, 16:24], K_or[:], PROWS[0:1, 24:32],
                               start=False, stop=True)
            em.inc(ti, sT)
            em.mark("bcasts", sT)

            def dtr_one(sl):
                em.wait(tensor, f"xdnv{sl}")
                em.wait(tensor, f"dfev{sl - 1}" if sl > 0 else f"xf{NG - 1}")
                xb = (sl % 4) * D
                ti = None
                for k in range(8):
                    ti = tensor.transpose(PTR[:, k * 128:(k + 1) * 128],
                                          XDN[:, xb + k * 128: xb + (k + 1) * 128],
                                          K_id[:])
                em.inc(ti, sT)
                em.mark(f"dtr{sl}", sT)

            def att_unit(u):
                g, nh = u // 2, u % 2
                if u == 0:
                    em.wait(tensor, "pb")
                    em.wait(tensor, "mb")
                elif u >= 3:
                    em.wait(tensor, f"attevx{u - 3}")
                if u in ATT_EO_WAIT:
                    em.wait(tensor, f"eo{ATT_EO_WAIT[u]}")
                bank = (CH, PM2, PM3)[u % 3]
                ti = None
                for lc in range(2):
                    ti = tensor.matmul(bank[:, 0:512],
                                       TH[:, lc * TOK + g * 128: lc * TOK + (g + 1) * 128],
                                       MB[:, lc * D + nh * 512: lc * D + nh * 512 + 512],
                                       start=(lc == 0), stop=(lc == 1))
                em.inc(ti, sT)
                em.mark(f"attx{u}", sT)

            def units_at(qq, fc):
                return ATT_UNITS.get((qq, fc), ())

            for sl in range(4):
                dtr_one(sl)
            for qq in range(4):
                for j in range(4):
                    em.wait(tensor, f"dfev{4 * qq + j}")
                if qq == 0:
                    em.wait(tensor, "wB_done")
                dfb = (qq % 2) * (8 * 512)
                for fc in range(16):
                    b0, b1 = ((BG0, BG1) if fc % 2 == 0 else (PM0, PM1))
                    if fc >= 2:
                        em.wait(tensor, f"inter{qq}_{fc - 2}")
                    elif qq > 0:
                        em.wait(tensor, f"inter{qq - 1}_{14 + fc}")
                        if fc == 0:
                            em.wait(tensor, f"eo{4 * qq - 1}")
                    elif qq == 0 and fc < 2:
                        em.wait(tensor, "marq1")
                        em.wait(tensor, f"geev{NG - 1}")
                    for dc in range(8):
                        tensor.matmul(b0[:],
                                      W_0[:, dc * FF + fc * 128: dc * FF + fc * 128 + 128],
                                      DFB[:, dfb + dc * 512: dfb + dc * 512 + 512],
                                      start=(dc == 0), stop=(dc == 7))
                    ti = None
                    for dc in range(8):
                        ti = tensor.matmul(b1[:],
                                           W_1[:, dc * FF + fc * 128: dc * FF + fc * 128 + 128],
                                           DFB[:, dfb + dc * 512: dfb + dc * 512 + 512],
                                           start=(dc == 0), stop=(dc == 7))
                    em.inc(ti, sT)
                    em.mark(f"h{qq}_{fc}", sT)
                    for u in units_at(qq, fc):
                        att_unit(u)
                em.wait(tensor, f"inter{qq}_15")
                for s2 in range(4):
                    gs = qq * 4 + s2
                    if gs > 1:
                        em.wait(tensor, f"eo{gs - 2}")
                    if s2 == 0:
                        em.wait(tensor, f"attevx{16 * qq + 15}")
                    ti = None
                    for nh in range(2):
                        bank = (PM2, PM3)[nh] if gs % 2 == 0 else (BG0, BG1)[nh]
                        for fc in range(16):
                            ti = tensor.matmul(
                                bank[:],
                                INTER[:, fc * 512 + s2 * 128: fc * 512 + s2 * 128 + 128],
                                W_o[:, fc * D + nh * 512: fc * D + nh * 512 + 512],
                                start=(fc == 0), stop=(fc == 15))
                    em.inc(ti, sT)
                    em.mark(f"omm{gs}", sT)
                if qq < 3:
                    for sl in range(4 * (qq + 1), 4 * (qq + 1) + 4):
                        dtr_one(sl)

        def body_vector(v):
            def se(inst):
                em.se(v, sV, inst)
            em.wait(v, "k_done")
            em.wait(v, "iotas")
            se(v.memset(RBASE[:], 0.0))
            se(v.tensor_scalar(GTID[:], GTID[:], K_cc[:, 0:1], None, ALU.add))
            em.mark("gtid", sV)

            def stats_work(g):
                gb = (g % 3) * D
                b = (g % 4) * 8
                em.wait(v, f"x{g}")
                se(v.reduce_sum(STAT[:, b:b + 1], XT[:, gb:gb + D], axis=AX))
                em.mark(f"sum{g}", sV)
                em.wait(v, f"ssq{g}")
                se(v.tensor_scalar(STAT[:, b + 2:b + 3], STAT[:, b:b + 1], -1.0 / D, None, ALU.mult))
                se(v.tensor_mul(STAT[:, b + 3:b + 4], STAT[:, b + 2:b + 3], STAT[:, b + 2:b + 3]))
                se(v.scalar_tensor_tensor(STAT[:, b + 4:b + 5], STAT[:, b + 1:b + 2], 1.0 / D,
                                          STAT[:, b + 3:b + 4], ALU.mult, ALU.subtract))
                em.mark(f"var{g}", sV)
                em.wait(v, f"sqrt{g}")
                se(v.reciprocal(STAT[:, b + 6:b + 7], STAT[:, b + 5:b + 6]))
                if g >= 4:
                    em.wait(v, f"tr{g - 4}")
                se(v.tensor_scalar(XNB[:, (g % 4) * D:(g % 4 + 1) * D], XT[:, gb:gb + D],
                                   STAT[:, b + 2:b + 3],
                                   STAT[:, b + 6:b + 7], ALU.add, ALU.mult))
                em.mark(f"xnb{g}", sV)

            def xfev(g):
                em.wait(v, f"tr{g}")
                if g >= 2:
                    em.wait(v, f"ge{g - 2}")
                se(v.tensor_copy(XF[:, (g % 2) * D:(g % 2 + 1) * D], PTR[:]))
                em.mark(f"xf{g}", sV)

            def tgev(g):
                em.wait(v, f"tg{g}")
                se(v.tensor_copy(GATE[:, g * E:(g + 1) * E], CH[:, 256:256 + E]))
                se(v.tensor_copy(TH[:, g * 128:(g + 1) * 128], CH[:, 0:128]))
                se(v.tensor_copy(TH[:, TOK + g * 128: TOK + (g + 1) * 128],
                                 CH[:, 128:256]))
                em.mark(f"tgev{g}", sV)

            def phev(g):
                em.wait(v, f"ph{g}")
                if g >= 2:
                    em.wait(v, f"M{g - 2}")
                se(v.tensor_copy(PHS[:, (g % 2) * L:(g % 2 + 1) * L], CH[:, 0:L]))
                em.mark(f"phev{g}", sV)

            def chunk_work(q):
                    gsl = GATE[:, q * 64:(q + 1) * 64]
                    g3 = t3(gsl)
                    io3 = t3(K_io[:])
                    S1 = SEL1[:, q * 8:(q + 1) * 8]
                    S2 = SEL2[:, q * 8:(q + 1) * 8]
                    se(v.tensor_reduce(col3(RS_A[:]), g3, AX, ALU.max))
                    se(v.tensor_tensor(t3(RS_M1[:]), g3,
                                       col3(RS_A[:]).broadcast_to([128, 8, E]),
                                       ALU.is_equal))
                    se(v.tensor_mul(RS_T[:], RS_M1[:], K_rv[:]))
                    se(v.tensor_reduce(col3(RS_B[:]), t3(RS_T[:]), AX, ALU.max))
                    se(v.tensor_scalar(S1, RS_B[:], -1.0, 7.0, ALU.mult, ALU.add))
                    se(v.tensor_tensor(t3(RS_M1[:]), io3,
                                       col3(S1).broadcast_to([128, 8, E]), ALU.is_equal))
                    se(v.scalar_tensor_tensor(RS_G2[:], RS_M1[:], -1e30, gsl,
                                              ALU.mult, ALU.add))
                    se(v.tensor_reduce(col3(RS_A[:]), t3(RS_G2[:]), AX, ALU.max))
                    se(v.tensor_reduce(col3(RS_B[:]), g3, AX, ALU.max))
                    se(v.tensor_sub(P1V[:, q * 8:(q + 1) * 8], RS_B[:], RS_A[:]))
                    em.mark(f"dl{q}", sV)
                    se(v.tensor_tensor(t3(RS_M2[:]), t3(RS_G2[:]),
                                       col3(RS_A[:]).broadcast_to([128, 8, E]),
                                       ALU.is_equal))
                    se(v.tensor_mul(RS_T[:], RS_M2[:], K_rv[:]))
                    se(v.tensor_reduce(col3(RS_B[:]), t3(RS_T[:]), AX, ALU.max))
                    se(v.tensor_scalar(S2, RS_B[:], -1.0, 7.0, ALU.mult, ALU.add))
                    se(v.tensor_tensor(t3(RS_M2[:]), io3,
                                       col3(S2).broadcast_to([128, 8, E]), ALU.is_equal))
                    se(v.tensor_add(RS_S[:], RS_M1[:], RS_M2[:]))
                    em.mark(f"S{q}", sV)
                    em.wait(v, f"cs{q}")
                    se(v.tensor_copy(RS_R[:], CH[0:1, 0:64]))
                    bufs = (RS_R, RS_R2)
                    for step, sh in enumerate((1, 2, 4)):
                        a = bufs[step % 2][:].rearrange("p (t e) -> p t e", e=E)
                        b = bufs[(step + 1) % 2][:].rearrange("p (t e) -> p t e", e=E)
                        se(v.tensor_copy(b[:, 0:sh, :], a[:, 0:sh, :]))
                        se(v.tensor_add(b[:, sh:8, :], a[:, sh:8, :], a[:, 0:8 - sh, :]))
                    PFX = bufs[1]
                    r3 = PFX[:].rearrange("p (t e) -> p t e", e=E)
                    bq3 = RS_BQ[:].rearrange("p (t e) -> p t e", e=E)
                    se(v.tensor_copy(bq3[:, 0:1, :],
                                     RBASE[:].rearrange("p (o e) -> p o e", e=E)))
                    se(v.tensor_add(bq3[:, 1:8, :], r3[:, 0:7, :],
                                    RBASE[:].rearrange("p (o e) -> p o e", e=E)
                                    .broadcast_to([1, 7, E])))
                    se(v.tensor_add(RBASE[:], RBASE[:], PFX[:, 56:64]))
                    em.mark(f"bq{q}", sV)
                    em.wait(v, f"cb{q}")
                    se(v.tensor_mul(RS_T[:], RS_M1[:], CH[:, 0:64]))
                    se(v.tensor_reduce(col3(RK1[:, q * 8:(q + 1) * 8]), t3(RS_T[:]),
                                       AX, ALU.add))
                    se(v.tensor_mul(RS_T[:], RS_M2[:], CH[:, 0:64]))
                    se(v.tensor_reduce(col3(RK2[:, q * 8:(q + 1) * 8]), t3(RS_T[:]),
                                           AX, ALU.add))
                    em.mark(f"rkread{q}", sV)
                    se(v.scalar_tensor_tensor(DSF[:], S1, 8192.0,
                                              RK1[:, q * 8:(q + 1) * 8], ALU.mult, ALU.add))
                    se(v.tensor_scalar(DSF[:], DSF[:], 16.0, None, ALU.add))
                    se(v.tensor_copy(DST1[:, q * 8:(q + 1) * 8], DSF[:]))
                    se(v.scalar_tensor_tensor(DSF[:], S2, 8192.0,
                                              RK2[:, q * 8:(q + 1) * 8], ALU.mult, ALU.add))
                    se(v.tensor_scalar(DSF[:], DSF[:], 16.0, None, ALU.add))
                    se(v.tensor_copy(DST2[:, q * 8:(q + 1) * 8], DSF[:]))
                    em.mark(f"pack{q}", sV)


            for g in range(NG):
                stats_work(g)
                if g >= 1:
                    xfev(g - 1)
                if g >= 2:
                    tgev(g - 2)
                    phev(g - 2)
                    if (g - 2) % 8 == 7:
                        chunk_work((g - 2) // 8)
            xfev(NG - 1)
            tgev(NG - 2)
            phev(NG - 2)
            tgev(NG - 1)
            phev(NG - 1)
            chunk_work(NQ - 1)

            em.wait(v, f"M{NG - 1}")
            for j in range(2):
                if j > 0:
                    em.wait(v, f"mars{j - 1}")
                se(v.tensor_copy(MAR[:, 0:512], (PM0, PM1, PM2, PM3)[j * 2 + 0][:]))
                se(v.tensor_copy(MAR[:, 512:1024], (PM0, PM1, PM2, PM3)[j * 2 + 1][:]))
                em.mark(f"marq{j}", sV)

            em.wait(v, "ctbl")
            se(v.tensor_mul(SM1[:], CTBL[:], K_em[0:8, :]))
            se(v.reduce_sum(CNTE[:], SM1[:], axis=AX))
            em.mark("cnte", sV)
            em.wait(v, "prows")
            se(v.tensor_copy(PROWS[0:1, 0:24], CH[0:1, 0:24]))
            se(v.tensor_scalar(PROWS[0:1, 24:32], PROWS[0:1, 0:8], -1.0, None,
                                   ALU.mult))
            em.mark("prows_sb", sV)
            em.wait(v, "bcasts")
            se(v.tensor_copy(PBV[:, 0:24], CH[:, 0:24]))
            em.mark("pb", sV)

            def dA(s):
                se(v.tensor_scalar(DSF[:], PBV[:, 16:24], RALL[:, s:s + 1], None,
                                   ALU.add))
                se(v.tensor_scalar(RS_A[:], DSF[:], 0.0, None, ALU.is_ge))
                se(v.tensor_reduce(STAT[:, 9:10], RS_A[:], AX, ALU.add))
                se(v.scalar_tensor_tensor(RS_B[:], RS_A[:], -BIG, DSF[:],
                                          ALU.mult, ALU.add))
                se(v.tensor_reduce(STAT[:, 10:11], RS_B[:], AX, ALU.min))
                se(v.tensor_scalar(STAT[:, 10:11], STAT[:, 10:11], BIG, None, ALU.add))
                se(v.tensor_scalar(STAT[:, 9:10], STAT[:, 9:10], -1.0, None, ALU.add))
                se(v.scalar_tensor_tensor(STAT[:, 11:12], STAT[:, 9:10], float(PACKW),
                                          STAT[:, 10:11], ALU.mult, ALU.add))
                se(v.tensor_scalar(STAT[:, 11:12], STAT[:, 11:12], K_cc[:, 1:2],
                                   None, ALU.add))
                se(v.tensor_copy(SRCU[:, s:s + 1], STAT[:, 11:12]))
                se(v.tensor_scalar(VLD[:, s:s + 1], RALL[:, s:s + 1], PBV[:, 8:9],
                                       None, ALU.is_lt))
                em.mark(f"src{s}", sV)

            def dB(s):
                em.wait(v, f"idg{s}")
                se(v.tensor_mul(STAT[:, 12:13], TIDF[:, s:s + 1], VLD[:, s:s + 1]))
                se(v.scalar_tensor_tensor(STAT[:, 13:14], VLD[:, s:s + 1], -ZROW,
                                          STAT[:, 12:13], ALU.mult, ALU.add))
                se(v.tensor_scalar(STAT[:, 13:14], STAT[:, 13:14], ZROW, None, ALU.add))
                se(v.tensor_copy(TIDU[:, s:s + 1], STAT[:, 13:14]))
                em.mark(f"tid{s}", sV)

            def dC(s):
                rb = (s % 2) * D
                em.wait(v, f"rowg{s}")
                se(v.reduce_sum(STAT[:, 0:1], XRAW[:, rb:rb + D], axis=AX))
                em.mark(f"dsum{s}", sV)
                em.wait(v, f"dssq{s}")
                se(v.tensor_scalar(STAT[:, 2:3], STAT[:, 0:1], -1.0 / D, None, ALU.mult))
                se(v.tensor_mul(STAT[:, 3:4], STAT[:, 2:3], STAT[:, 2:3]))
                se(v.scalar_tensor_tensor(STAT[:, 4:5], STAT[:, 1:2], 1.0 / D,
                                              STAT[:, 3:4], ALU.mult, ALU.subtract))
                em.mark(f"dvar{s}", sV)

            def dD(s):
                rb = (s % 2) * D
                em.wait(v, f"dsqrt{s}")
                se(v.reciprocal(STAT[:, 6:7], STAT[:, 5:6]))
                if s in (0, 1):
                    em.wait(v, f"xnb{NG - 1}")
                elif s == 2:
                    em.wait(v, f"tr{NG - 2}")
                elif s == 3:
                    em.wait(v, f"tr{NG - 1}")
                else:
                    em.wait(v, f"dtr{s - 4}")
                xb = (s % 4) * D
                se(v.tensor_scalar(XDN[:, xb:xb + D], XRAW[:, rb:rb + D],
                                       STAT[:, 2:3], STAT[:, 6:7], ALU.add, ALU.mult))
                em.mark(f"xdnv{s}", sV)

            def dE(sl):
                em.wait(v, f"dtr{sl}")
                dfb = ((sl // 4) % 2) * (8 * 512)
                off = (sl % 4) * 128
                dst = DFB[:, dfb:dfb + 8 * 512].rearrange(
                    "p (k c) -> p k c", c=512)[:, :, off:off + 128]
                src = PTR[:].rearrange("p (k c) -> p k c", c=128)
                se(v.tensor_copy(dst, src))
                em.mark(f"dfev{sl}", sV)

            def inter_one(qq, fc):
                b1 = BG1 if fc % 2 == 0 else PM1
                em.wait(v, f"sil{qq}_{fc}")
                if qq > 0 and fc == 0:
                    em.wait(v, f"omm{qq * 4 - 1}")
                se(v.tensor_tensor(INTER[:, fc * 512:(fc + 1) * 512], SIL[:],
                                   b1[:], ALU.mult))
                em.mark(f"inter{qq}_{fc}", sV)

            def eo_one(qq, s2):
                gs = qq * 4 + s2
                em.wait(v, f"omm{gs}")
                if 2 <= gs < 8:
                    em.wait(v, f"eod{gs - 2}")
                dst = EOSLOT(gs)
                b0, b1 = (PM2, PM3) if gs % 2 == 0 else (BG0, BG1)
                se(v.tensor_copy(dst[:, 0:512], b0[:]))
                se(v.tensor_copy(dst[:, 512:1024], b1[:]))
                em.mark(f"eo{gs}", sV)

            def mbp_one(lc):
                em.wait(v, f"md{lc}")
                se(v.tensor_scalar(MB[:, lc * D:(lc + 1) * D], MAR[:], 1.0 / 32.0,
                                   None, ALU.mult))
                em.mark(f"mbp{lc}", sV)
                if lc == 1:
                    em.mark("mb", sV)

            def attevx_one(u):
                g, nh = u // 2, u % 2
                em.wait(v, f"attx{u}")
                if nh == 0:
                    em.wait(v, f"xa{g}")
                bank = (CH, PM2, PM3)[u % 3]
                dst = MDPS[g % 4][:, nh * 512: nh * 512 + 512]
                se(v.tensor_tensor(dst, dst, bank[:, 0:512], ALU.add))
                em.mark(f"attevx{u}", sV)

            for f in (lambda: dA(0), lambda: dA(1), lambda: dA(2), lambda: dA(3),
                      lambda: dB(0), lambda: dB(1), lambda: dC(0), lambda: dB(2),
                      lambda: dD(0), lambda: dC(1), lambda: dB(3), lambda: dD(1),
                      lambda: dC(2), lambda: dD(2), lambda: dC(3), lambda: dD(3),
                      lambda: dE(0), lambda: dE(1), lambda: dE(2), lambda: dE(3)):
                f()
            for k, (SEL, RK, GI) in enumerate(((SEL1, RK1, GI1), (SEL2, RK2, GI2))):
                for e in range(E):
                    se(v.tensor_scalar(RS_M1[:, 0:NG], SEL[:], float(e), None,
                                       ALU.is_equal))
                    if e == 0:
                        se(v.tensor_scalar(RS_M2[:, 0:NG], RS_M1[:, 0:NG],
                                           PBV[:, 0:1], None, ALU.mult))
                    else:
                        se(v.scalar_tensor_tensor(RS_M2[:, 0:NG], RS_M1[:, 0:NG],
                                                  PBV[:, e:e + 1], RS_M2[:, 0:NG],
                                                  ALU.mult, ALU.add))
                se(v.tensor_add(RK[:], RK[:], RS_M2[:, 0:NG]))
                se(v.tensor_scalar_min(RS_M1[:, 0:NG], RK[:], float(CAP - 1)))
                se(v.tensor_scalar(RS_T[:, 0:NG], RS_M1[:, 0:NG], 640.0, None, ALU.is_ge))
                se(v.tensor_scalar(RS_S[:, 0:NG], RS_T[:, 0:NG], 640.0, None, ALU.mult))
                se(v.tensor_scalar(RS_T[:, 0:NG], RS_M1[:, 0:NG], 1280.0, None, ALU.is_ge))
                se(v.scalar_tensor_tensor(RS_S[:, 0:NG], RS_T[:, 0:NG], 640.0,
                                          RS_S[:, 0:NG], ALU.mult, ALU.add))
                se(v.tensor_scalar(RS_G2[:, 0:NG], RS_M1[:, 0:NG], 1792.0, None, ALU.is_ge))
                se(v.scalar_tensor_tensor(RS_S[:, 0:NG], RS_G2[:, 0:NG], 512.0,
                                          RS_S[:, 0:NG], ALU.mult, ALU.add))
                se(v.tensor_scalar(RS_M2[:, 0:NG], RS_T[:, 0:NG], -128.0, 640.0,
                                   ALU.mult, ALU.add))
                se(v.scalar_tensor_tensor(RS_M2[:, 0:NG], RS_G2[:, 0:NG], -256.0,
                                          RS_M2[:, 0:NG], ALU.mult, ALU.add))
                se(v.tensor_mul(RS_M2[:, 0:NG], RS_M2[:, 0:NG], SEL[:]))
                se(v.scalar_tensor_tensor(RS_M1[:, 0:NG], RS_S[:, 0:NG], 7.0,
                                          RS_M1[:, 0:NG], ALU.mult, ALU.add))
                se(v.tensor_add(RS_M1[:, 0:NG], RS_M1[:, 0:NG], RS_M2[:, 0:NG]))
                se(v.tensor_copy(GI[:], RS_M1[:, 0:NG]))
                em.mark(f"gidx{k}", sV)

            def units_at_v(qq, fc):
                return ATT_UNITS.get((qq, fc), ())

            for qq in range(4):
                S = [4 * (qq + 1) + j for j in range(4)] if qq < 3 else None
                sched = []
                for fc in range(16):
                    sched.append(("i", fc))
                    for u in units_at_v(qq, fc):
                        sched.append(("X", u))
                    if qq == 0 and fc == 5:
                        sched.append(("M", 0))
                    if qq == 0 and fc == 6:
                        sched.append(("M", 1))
                    if S is not None:
                        if fc == 0:
                            sched.append(("A", S[0]))
                        elif fc == 1:
                            sched.append(("A", S[1]))
                        elif fc == 3:
                            sched.append(("B", S[0]))
                        elif fc == 4:
                            sched.append(("A", S[2]))
                        elif fc == 5:
                            sched.append(("B", S[1]))
                        elif fc == 6:
                            sched.append(("C", S[0]))
                        elif fc == 7:
                            sched.append(("A", S[3]))
                        elif fc == 8:
                            sched.append(("D", S[0]))
                            sched.append(("B", S[2]))
                        elif fc == 9:
                            sched.append(("C", S[1]))
                        elif fc == 10:
                            sched.append(("D", S[1]))
                            sched.append(("B", S[3]))
                        elif fc == 11:
                            sched.append(("C", S[2]))
                        elif fc == 12:
                            sched.append(("D", S[2]))
                        elif fc == 13:
                            sched.append(("C", S[3]))
                        elif fc == 14:
                            sched.append(("D", S[3]))
                for kind, val in sched:
                    if kind == "i":
                        inter_one(qq, val)
                    elif kind == "X":
                        attevx_one(val)
                    elif kind == "M":
                        mbp_one(val)
                    elif kind == "A":
                        dA(val)
                    elif kind == "B":
                        dB(val)
                    elif kind == "C":
                        dC(val)
                    elif kind == "D":
                        dD(val)
                for s2 in range(4):
                    eo_one(qq, s2)
                if S is not None:
                    for j in range(4):
                        dE(S[j])

            em.wait(v, f"sig{NQ - 1}")
            for g in range(NG):
                ob = (g % 4) * D
                rb = (g % 8) * D
                xb = (g % 4) * D
                XCS = XC if g % 8 < 4 else XC2
                OCS = OAC if g % 8 < 4 else OAC2
                em.wait(v, f"pc{g}")
                em.wait(v, f"rbg{g}")
                if g >= 8:
                    em.wait(v, f"outd{g - 8}")
                se(v.scalar_tensor_tensor(OCS[:, ob:ob + D], RB1[:, rb:rb + D],
                                          P1V[:, g:g + 1], XCS[:, xb:xb + D],
                                          ALU.mult, ALU.add))
                em.mark(f"xcr{g}", sV)
                se(v.scalar_tensor_tensor(OCS[:, ob:ob + D], RB2[:, rb:rb + D],
                                          P2V[:, g:g + 1], OCS[:, ob:ob + D],
                                          ALU.mult, ALU.add))
                em.mark(f"oacd{g}", sV)

        def body_scalar(scalar):
            def sse(inst):
                em.se(scalar, sA, inst)
            em.wait(scalar, "k_done")
            def sig_work(q):
                em.wait(scalar, f"dl{q}")
                sse(scalar.activation(P1V[:, q * 8:(q + 1) * 8],
                                      P1V[:, q * 8:(q + 1) * 8], AF.Sigmoid))
                sse(scalar.activation(P2V[:, q * 8:(q + 1) * 8],
                                      P1V[:, q * 8:(q + 1) * 8],
                                      AF.Copy, scale=-1.0, bias=1.0))
                em.mark(f"sig{q}", sA)

            def geev(g):
                em.wait(scalar, f"ge{g}")
                if g >= 2:
                    em.wait(scalar, f"M{g - 2}")
                gb = (g % 2) * D
                sse(scalar.activation(GES[:, gb:gb + 512], BG0[:], AF.Copy))
                sse(scalar.activation(GES[:, gb + 512:gb + D], BG1[:], AF.Copy))
                em.mark(f"geev{g}", sA)

            for g in range(NG):
                gb = (g % 3) * D
                b = (g % 4) * 8
                em.wait(scalar, f"x{g}")
                if g >= 3:
                    em.wait(scalar, f"xnb{g - 3}")
                elif g == 0:
                    em.wait(scalar, "pzero")
                sse(scalar.activation(SNK[:], XT[:, gb:gb + D], AF.Square,
                                      accum_out=STAT[:, b + 1:b + 2]))
                em.mark(f"ssq{g}", sA)
                em.wait(scalar, f"var{g}")
                sse(scalar.activation(STAT[:, b + 5:b + 6], STAT[:, b + 4:b + 5], AF.Sqrt, bias=1e-5))
                em.mark(f"sqrt{g}", sA)
                if g >= 2:
                    geev(g - 2)
                if g % 8 == 2 and g >= 10:
                    sig_work((g - 10) // 8)
            geev(NG - 2)
            geev(NG - 1)
            sig_work(NQ - 1)
            def sil_one(qq, fc):
                b0 = BG0 if fc % 2 == 0 else PM0
                em.wait(scalar, f"h{qq}_{fc}")
                if fc >= 1:
                    em.wait(scalar, f"inter{qq}_{fc - 1}")
                elif qq > 0:
                    em.wait(scalar, f"inter{qq - 1}_15")
                sse(scalar.activation(SIL[:], b0[:], AF.Silu))
                em.mark(f"sil{qq}_{fc}", sA)

            def dssq_one(s):
                rb = (s % 2) * D
                em.wait(scalar, f"rowg{s}")
                if s > 0:
                    em.wait(scalar, f"dvar{s - 1}")
                sse(scalar.activation(SNK[:], XRAW[:, rb:rb + D], AF.Square,
                                      accum_out=STAT[:, 1:2]))
                em.mark(f"dssq{s}", sA)

            def dsqrt_one(s):
                em.wait(scalar, f"dvar{s}")
                sse(scalar.activation(STAT[:, 5:6], STAT[:, 4:5], AF.Sqrt, bias=1e-5))
                em.mark(f"dsqrt{s}", sA)

            for s in range(4):
                dssq_one(s)
                dsqrt_one(s)
            def eod_one(gs):
                em.wait(scalar, f"eo{gs}")
                if gs >= 4:
                    em.wait(scalar, f"eod{gs - 4}")
                esem = (sE, sE1, sE2, sE3)[gs % 4]
                em.inc(scalar.dma_start(out=eoutd.ap()[gs * 128:(gs + 1) * 128, :],
                                        in_=EOSLOT(gs)), esem, 16)
                em.mark(f"eod{gs}", esem)

            for qq in range(4):
                S = [4 * (qq + 1) + j for j in range(4)] if qq < 3 else None
                if qq >= 1:
                    for s2 in range(4):
                        eod_one(4 * (qq - 1) + s2)
                for fc in range(16):
                    sil_one(qq, fc)
                    if S is not None:
                        if fc == 6:
                            dssq_one(S[0])
                        elif fc == 7:
                            dsqrt_one(S[0])
                        elif fc == 9:
                            dssq_one(S[1])
                            dsqrt_one(S[1])
                        elif fc == 11:
                            dssq_one(S[2])
                            dsqrt_one(S[2])
                        elif fc == 13:
                            dssq_one(S[3])
                            dsqrt_one(S[3])

            for gs in range(12, 16):
                eod_one(gs)
            for g in range(NG):
                if g == 0:
                    em.wait(scalar, "xdnv14")
                elif g == 1:
                    em.wait(scalar, "xdnv15")
                elif g in (2, 3):
                    em.wait(scalar, "dfev15")
                elif g in (4, 5, 6, 7):
                    em.wait(scalar, "omm15")
                else:
                    em.wait(scalar, f"xcr{g - 8}")
                em.wait(scalar, f"pstore{g}")
                xsem = sPc[g % 8]
                dst = (XC if g % 8 < 4 else XC2)[:, (g % 4) * D:(g % 4 + 1) * D]
                em.inc(scalar.dma_start(out=dst,
                                        in_=partd.ap()[g * 128:(g + 1) * 128, :]),
                       xsem, 16)
                em.mark(f"pc{g}", xsem)

        def body_gpsimd(gp):
            em.inc(gp.memset(SNK[:], 0.0), sG)
            em.mark("zro", sG)
            em.wait(gp, "zro")
            em.inc(gp.dma_start(
                out=packsrc.ap()[16:PACKW, :].rearrange("(p b) o -> p (b o)", p=128),
                in_=SNK[:, 0:512]), sI, 16)
            em.inc(gp.dma_start(
                out=packsrc.ap().rearrange("(a b) o -> a (b o)", a=1)[0:1, 8:16],
                in_=SNK[0:1, 0:8]), sI, 16)
            em.mark("pzero", sI)
            gp.iota(GTID[:], pattern=[[128, NG]], base=0, channel_multiplier=1,
                    allow_small_or_imprecise_dtypes=True)
            em.inc(gp.iota(RALL[:], pattern=[[128, 16]], base=0, channel_multiplier=1,
                           allow_small_or_imprecise_dtypes=True), sG)
            em.mark("iotas", sG)
            em.inc(gp.dma_start(out=W_th.rearrange("p (fc l) -> p fc l", l=L),
                                in_=wthT.rearrange("(fc p) l -> p fc l", p=128)), sW, 16)
            em.inc(gp.dma_start(out=W_ph.rearrange("p (fc l) -> p fc l", l=L),
                                in_=wphT.rearrange("(fc p) l -> p fc l", p=128)), sW, 16)
            em.inc(gp.dma_start(out=W_gg.rearrange("p (fc d) -> p fc d", d=D),
                                in_=wgT.rearrange("(fc p) d -> p fc d", p=128)), sW, 16)
            em.inc(gp.dma_start(out=W_ga.rearrange("p (fc e) -> p fc e", e=E),
                                in_=wgate.rearrange("(fc p) e -> p fc e", p=128)), sW, 16)
            em.mark("wA_done", sW)
            em.inc(gp.dma_start(out=W_0[:].rearrange("p (dc f) -> p dc f", f=FF),
                                in_=w0e.rearrange("(dc p) f -> p dc f", p=128)), sWb, 16)
            em.inc(gp.dma_start(out=W_1[:].rearrange("p (dc f) -> p dc f", f=FF),
                                in_=w1e.rearrange("(dc p) f -> p dc f", p=128)), sWb, 16)
            em.inc(gp.dma_start(out=W_o[:].rearrange("p (fc d) -> p fc d", d=D),
                                in_=woe.rearrange("(fc p) d -> p fc d", p=128)), sWb, 16)
            em.mark("wB_done", sWb)
            em.wait(gp, "pzero")
            for q in range(NQ):
                em.wait(gp, f"pack{q}")
                for g in range(8 * q, 8 * q + 8):
                    em.inc(gp.indirect_dma_start(
                        out=packsrc.ap(),
                        out_offset=IndirectOffsetOnAxis(ap=DST1[:, g:g + 1], axis=0),
                        in_=GTID[:, g:g + 1], in_offset=None), sI, 16)
                    em.inc(gp.indirect_dma_start(
                        out=packsrc.ap(),
                        out_offset=IndirectOffsetOnAxis(ap=DST2[:, g:g + 1], axis=0),
                        in_=GTID[:, g:g + 1], in_offset=None), sI, 16)
            em.inc(gp.dma_start(
                out=packsrc.ap().rearrange("(a b) o -> a (b o)", a=1)[0:1, 0:8],
                in_=RBASE[:]), sI, 16)
            em.mark("packall", sI)
            if not em.dry:
                gp.wait_ge(sI.h, sI.v)
            em.inc(gp.collective_compute("AllGather", ALU.bypass, replica_groups=RGRP,
                                         ins=[packsrc.ap().opt()],
                                         outs=[agout.ap().opt()]), sC)
            em.mark("ag1", sC)
            em.wait(gp, "mars1")
            em.inc(gp.collective_compute("AllReduce", ALU.add, replica_groups=RGRP2,
                                         ins=[marsrc.ap().opt()],
                                         outs=[mardst.ap().opt()]), sC)
            em.mark("arm", sC)
            def idg_one(s):
                em.wait(gp, f"src{s}")
                em.inc(gp.indirect_dma_start(
                    out=TIDF[:, s:s + 1], out_offset=None,
                    in_=agout.ap(),
                    in_offset=IndirectOffsetOnAxis(ap=SRCU[:, s:s + 1], axis=0)),
                       sD4[s % 4], 16)
                em.mark(f"idg{s}", sD4[s % 4])

            def rowg_one(s):
                em.wait(gp, f"tid{s}")
                if s >= 2:
                    em.wait(gp, f"xdnv{s - 2}")
                else:
                    em.wait(gp, f"xnb{NG - 1}")
                em.inc(gp.indirect_dma_start(
                    out=XRAW[:, (s % 2) * D:(s % 2 + 1) * D], out_offset=None,
                    in_=xfull,
                    in_offset=IndirectOffsetOnAxis(ap=TIDU[:, s:s + 1], axis=0)),
                       sD4[s % 4], 16)
                em.mark(f"rowg{s}", sD4[s % 4])

            for t in range(4):
                S = [4 * t + j for j in range(4)]
                idg_one(S[0]); idg_one(S[1]); rowg_one(S[0]); idg_one(S[2])
                rowg_one(S[1]); idg_one(S[3]); rowg_one(S[2]); rowg_one(S[3])
            AG2B = (0, 640, 1280, 1792, 2048)
            for qc in range(4):
                lo, hi = AG2B[qc], AG2B[qc + 1]
                ng = hi // 128
                for j in range(4):
                    top = [g for g in range(ng) if g % 4 == j]
                    if top:
                        em.wait(gp, f"eod{top[-1]}")
                em.inc(gp.collective_compute(
                    "AllGather", ALU.bypass, replica_groups=RGRP,
                    ins=[eoutd.ap()[lo:hi, :].opt()],
                    outs=[ag2out.ap()[NCORE * lo:NCORE * hi, :].opt()]), sC)
                em.mark(f"ag2c{qc}", sC)
            em.wait(gp, "ag2c3")
            em.wait(gp, "gidx0")
            em.wait(gp, "gidx1")
            for g in range(NG):
                if g >= 8:
                    em.wait(gp, f"oacd{g - 8}")
                rbs = (g % 8) * D
                sR = (sR0, sR1, sR2, sR3, sR4, sR5, sR6, sR7)[g % 8]
                em.inc(gp.indirect_dma_start(
                    out=RB1[:, rbs:rbs + D], out_offset=None, in_=ag2out.ap(),
                    in_offset=IndirectOffsetOnAxis(ap=GI1[:, g:g + 1], axis=0)), sR, 16)
                em.inc(gp.indirect_dma_start(
                    out=RB2[:, rbs:rbs + D], out_offset=None, in_=ag2out.ap(),
                    in_offset=IndirectOffsetOnAxis(ap=GI2[:, g:g + 1], axis=0)), sR, 16)
                em.mark(f"rbg{g}", sR)
            if not em.dry:
                for _s in (sST, sST1, sST2, sST3, sST4, sST5, sST6, sST7):
                    gp.wait_ge(_s.h, _s.v)

        bodies = (body_sync, body_tensor, body_vector, body_scalar, body_gpsimd)
        stub = StubEng()
        for b in bodies:
            b(stub)
        _DBG["MS"] = {k: (id(v[0]), v[1]) for k, v in em.MS.items()}
        _DBG["SEMS"] = {id(s): n for s, n in zip(ALLSEMS, ("sX sX1 sX2 sX3 sK sW sWb sR0 sR1 sR2 sR3 sR4 sR5 sR6 sR7 sST1 sST2 sST3 sST4 sST5 sST6 sST7 sV sA sT sG sI sC sE sE1 sE2 sE3 sST sPd sPd1 sPd2 sPd3 sMd sD0 sD1 sD2 sD3".split() + [f"sPc{j}" for j in range(8)]))}
        for s in ALLSEMS:
            s.v = 0
        em.dry = False
        block.sync(body_sync)
        block.tensor(body_tensor)
        block.vector(body_vector)
        block.scalar(body_scalar)
        block.gpsimd(body_gpsimd)

    return nc


_CACHE = {}


def _get_nc():
    if "nc" not in _CACHE:
        _CACHE["nc"] = build_nc(debug=False)
    return _CACHE["nc"]


def make_in_maps(L0_attn_x, L0_attn_W_theta, L0_attn_W_phi, L0_attn_W_g,
                 L0_ffn_wg, L0_ffn_w0, L0_ffn_w1, L0_ffn_wo):
    x2d = np.ascontiguousarray(np.asarray(L0_attn_x, np.float32).reshape(32768, D))
    xfull = np.zeros((NROWX, D), np.float32)
    xfull[:32768] = x2d
    wthT = np.ascontiguousarray(np.asarray(L0_attn_W_theta, np.float32).T)
    wphT = np.ascontiguousarray(np.asarray(L0_attn_W_phi, np.float32).T)
    wgT = np.ascontiguousarray(np.asarray(L0_attn_W_g, np.float32).T)
    wgate = np.ascontiguousarray(np.asarray(L0_ffn_wg, np.float32))
    w0 = np.asarray(L0_ffn_w0, np.float32)
    w1 = np.asarray(L0_ffn_w1, np.float32)
    wo = np.asarray(L0_ffn_wo, np.float32)
    in_maps = []
    for c in range(NCORE):
        ccon = np.zeros((128, 8), np.float32)
        ccon[:, 0] = c * TOK
        ccon[:, 1] = c * 8192 + 16
        ccon[:, 2] = 1.0 if c < 4 else 0.0
        ccon[:, 3] = 0.0 if c < 4 else 1.0
        ccon[:, 4] = c
        ccol8 = np.zeros((8, 2), np.float32)
        ccol8[:c, 0] = 1.0
        cem = np.zeros((128, E), np.float32)
        cem[:, c] = 1.0
        in_maps.append({
            "xown": np.ascontiguousarray(x2d[c * TOK:(c + 1) * TOK]),
            "xfull": xfull,
            "wthT": wthT, "wphT": wphT, "wgT": wgT, "wgate": wgate,
            "w0e": np.ascontiguousarray(w0[c]),
            "w1e": np.ascontiguousarray(w1[c]),
            "woe": np.ascontiguousarray(wo[c]),
            "ccon": ccon, "ccol8": ccol8, "cemask": cem,
        })
    return in_maps


def kernel(L0_attn_x, L0_attn_W_theta, L0_attn_W_phi, L0_attn_W_g,
           L0_ffn_wg, L0_ffn_w0, L0_ffn_w1, L0_ffn_wo):
    nc = _get_nc()
    in_maps = make_in_maps(L0_attn_x, L0_attn_W_theta, L0_attn_W_phi, L0_attn_W_g,
                           L0_ffn_wg, L0_ffn_w0, L0_ffn_w1, L0_ffn_wo)
    res = run_bass_kernel_spmd(nc, in_maps, core_ids=list(range(NCORE)))
    outs = np.concatenate([res.results[c]["out"] for c in range(NCORE)], axis=0)
    return outs.reshape(2, 16384, D)


# version 25
# speedup vs baseline: 1.0692x; 1.0103x over previous
import contextlib
import numpy as np
import ml_dtypes
import concourse.bass as bass
import concourse.mybir as mybir
from concourse.bass_utils import run_bass_kernel_spmd
from concourse.bass import IndirectOffsetOnAxis

F32 = mybir.dt.float32
BF16 = mybir.dt.bfloat16
U32 = mybir.dt.uint32
U8 = mybir.dt.uint8
AF = mybir.ActivationFunctionType
ALU = mybir.AluOpType
AX = mybir.AxisListType.X

NCORE = 8
TOK = 4096
NG = 32
NQ = 4
D = 1024
L = 256
E = 8
FF = 2048
CAP = 2048
PACKW = 16 + E * 8192
NROWX = 32768 + 32
ZROW = 32768.0
BIG = float(2 ** 24)


class Sem:
    def __init__(self, h):
        self.h = h
        self.v = 0


class StubInst:
    def then_inc(self, *a, **k):
        return self


class StubEng:
    def __getattr__(self, name):
        def f(*a, **k):
            return StubInst()
        return f


class Em:

    def __init__(self):
        self.MS = {}
        self.dry = True

    def inc(self, inst, sem, n=1):
        sem.v += n
        if not self.dry:
            inst.then_inc(sem.h, n)
        return inst

    def mark(self, name, sem):
        if self.dry:
            assert name not in self.MS, f"dup mark {name}"
            self.MS[name] = (sem, sem.v)
        else:
            assert self.MS[name][1] == sem.v, (name, self.MS[name][1], sem.v)

    def wait(self, eng, name):
        if not self.dry:
            sem, v = self.MS[name]
            eng.wait_ge(sem.h, v)

    def se(self, eng, sem, inst):
        sem.v += 1
        if not self.dry:
            inst.then_inc(sem.h, 1)
            eng.wait_ge(sem.h, sem.v)


_DBG = {}

ATT_POINTS = [(0, fc) for fc in range(7, 16)] + \
             [(qq, fc) for qq in (1, 2, 3) for fc in range(16)]
ATT_UNITS = {}
_u = 0
for _i, _pt in enumerate(ATT_POINTS):
    _n = 2 if _i < 7 else 1
    ATT_UNITS[_pt] = tuple(range(_u, _u + _n))
    _u += _n
assert _u == 64
ATT_EO_WAIT = {16: 2, 17: 2, 32: 6, 34: 6, 49: 10, 50: 10}


def build_nc(debug=False):
    nc = bass.Bass("TRN2", target_bir_lowering=False)

    xown = nc.dram_tensor("xown", [TOK, D], F32, kind="ExternalInput").ap()
    xfull = nc.dram_tensor("xfull", [NROWX, D], F32, kind="ExternalInput").ap()
    wthT = nc.dram_tensor("wthT", [D, L], F32, kind="ExternalInput").ap()
    wphT = nc.dram_tensor("wphT", [D, L], F32, kind="ExternalInput").ap()
    wgT = nc.dram_tensor("wgT", [D, D], F32, kind="ExternalInput").ap()
    wgate = nc.dram_tensor("wgate", [D, E], F32, kind="ExternalInput").ap()
    w0e = nc.dram_tensor("w0e", [D, FF], F32, kind="ExternalInput").ap()
    w1e = nc.dram_tensor("w1e", [D, FF], F32, kind="ExternalInput").ap()
    woe = nc.dram_tensor("woe", [FF, D], F32, kind="ExternalInput").ap()
    ccon = nc.dram_tensor("ccon", [128, 8], F32, kind="ExternalInput").ap()
    ccol8 = nc.dram_tensor("ccol8", [8, 2], F32, kind="ExternalInput").ap()
    cemask = nc.dram_tensor("cemask", [128, E], F32, kind="ExternalInput").ap()
    out = nc.dram_tensor("out", [TOK, D], F32, kind="ExternalOutput").ap()

    dbg = {}
    if debug:
        for name, shape, dt in (
            ("d_gate", [128, NG * E], F32), ("d_sel", [128, 2 * NG], F32),
            ("d_rk", [128, 2 * NG], F32), ("d_prob", [128, 2 * NG], F32),
            ("d_ctbl", [8, 8], F32), ("d_gidx", [128, 2 * NG], U32),
            ("d_tid", [128, 16], U32), ("d_eout", [CAP, D], BF16),
            ("d_mb", [128, 2 * D], BF16), ("d_th", [128, 2 * TOK], BF16),
        ):
            dbg[name] = nc.dram_tensor(name, shape, dt, kind="ExternalOutput").ap()

    packsrc = nc.dram_tensor("packsrc", [PACKW, 1], F32)
    agout = nc.dram_tensor("agout", [NCORE * PACKW, 1], F32)
    marsrc = nc.dram_tensor("marsrc", [L, D], F32)
    mardst = nc.dram_tensor("mardst", [L, D], F32)
    eoutd = nc.dram_tensor("eoutd", [CAP, D], BF16)
    ag2out = nc.dram_tensor("ag2out", [4 * NCORE * 512, D], BF16)
    partd = nc.dram_tensor("partd", [TOK, D], F32)
    RGRP = [list(range(NCORE))]
    RGRP2 = [[0, 1, 2, 3], [4, 5, 6, 7]]

    ident_bf = np.eye(128, dtype=np.float32).astype(ml_dtypes.bfloat16)
    ltri_np = (np.arange(128)[:, None] < np.arange(128)[None, :]).astype(np.float32)
    ltri8_np = (np.arange(8)[:, None] < np.arange(8)[None, :]).astype(np.float32)
    onesr_np = np.ones((1, 128), np.float32)
    onesc_np = np.ones((128, 1), np.float32)
    io8_np = np.tile(np.arange(E, dtype=np.float32), (128, 8))
    rv8_np = 7.0 - io8_np
    kct_np = np.tile(np.array([[1e-5, -1.0, 1.0, 0.0, 0.5, 2.0, 0.0, 0.0]], np.float32), (128, 1))

    es = contextlib.ExitStack()
    with es:
        def sb(name, shape, dt):
            return es.enter_context(nc.sbuf_tensor(name, shape, dt))

        def psum(name, shape, dt=F32):
            return es.enter_context(nc.psum_tensor(name, shape, dt))

        def mksem(name):
            return Sem(es.enter_context(nc.semaphore(name)))

        K_cc = sb("K_cc", [128, 8], F32)
        K_ct = sb("K_ct", [128, 8], F32)
        K_c8 = sb("K_c8", [8, 2], F32)
        K_em = sb("K_em", [128, E], F32)
        K_id = sb("K_id", [128, 128], BF16)
        K_lt = sb("K_lt", [128, 128], F32)
        K_lt8 = sb("K_lt8", [8, 8], F32)
        K_or = sb("K_or", [1, 128], F32)
        K_oc = sb("K_oc", [128, 1], F32)
        K_io = sb("K_io", [128, 64], F32)
        K_rv = sb("K_rv", [128, 64], F32)
        GTID = sb("GTID", [128, NG], F32)
        RALL = sb("RALL", [128, 16], F32)
        W_th = sb("W_thP", [128, 8 * L], BF16)
        W_ph = sb("W_phP", [128, 8 * L], BF16)
        W_ga = sb("W_gaP", [128, 8 * E], BF16)
        W_0 = sb("W_0", [128, 8 * FF], BF16)
        W_1 = sb("W_1", [128, 8 * FF], BF16)
        W_o = sb("W_o", [128, 16 * D], BF16)
        TH = sb("TH", [128, 2 * TOK], BF16)
        GATE = sb("GATE", [128, NG * E], F32)
        SEL1 = sb("SEL1", [128, NG], F32)
        SEL2 = sb("SEL2", [128, NG], F32)
        P1V = sb("P1V", [128, NG], F32)
        P2V = sb("P2V", [128, NG], F32)
        RK1 = sb("RK1", [128, NG], F32)
        RK2 = sb("RK2", [128, NG], F32)
        GI1 = sb("GI1", [128, NG], U32)
        GI2 = sb("GI2", [128, NG], U32)
        DST1 = sb("DST1", [128, NG], U32)
        DST2 = sb("DST2", [128, NG], U32)
        RBASE = sb("RBASE", [1, E], F32)
        MB = sb("MB", [128, 2 * D], BF16)
        CTBL = sb("CTBL", [8, 8], F32)
        SM1 = sb("SM1", [8, 8], F32)
        CNTE = sb("CNTE", [8, 1], F32)
        PROWS = sb("PROWS", [1, 32], F32)
        PBV = sb("PBV", [128, 24], F32)
        RS_M1 = sb("RS_M1", [128, 64], F32)
        RS_M2 = sb("RS_M2", [128, 64], F32)
        RS_T = sb("RS_T", [128, 64], F32)
        RS_G2 = sb("RS_G2", [128, 64], F32)
        RS_S = sb("RS_S", [128, 64], F32)
        RS_A = sb("RS_A", [128, 8], F32)
        RS_B = sb("RS_B", [128, 8], F32)
        RS_R = sb("RS_R", [1, 64], F32)
        RS_R2 = sb("RS_R2", [1, 64], F32)
        RS_BQ = sb("RS_BQ", [1, 64], F32)
        DSF = sb("DSF", [128, 8], F32)
        STAT = sb("STAT", [128, 32], F32)
        SNK = sb("SNK", [128, D], F32)
        SRCU = sb("SRCU", [128, 16], U32)
        TIDF = sb("TIDF", [128, 16], F32)
        TIDU = sb("TIDU", [128, 16], U32)
        VLD = sb("VLD", [128, 16], F32)
        INTER = sb("INTER", [128, 16 * 512], BF16)
        SIL = sb("SIL", [128, 512], BF16)
        MAR = sb("MAR", [128, D], F32)
        T1S = sb("T1S", [128, D], F32)

        ARENA = sb("ARENA", [128, 46080], U8)
        AR_F = ARENA.bitcast(F32)
        AR_B = ARENA.bitcast(BF16)

        class Carver:
            def __init__(self):
                self.off = 0

            def take(self, cols, dt):
                sz = 4 if dt in (F32, U32) else 2
                self.off = (self.off + 63) // 64 * 64
                st = self.off // sz
                h = AR_F if dt == F32 else AR_B
                ap = h.ap()[:, st:st + cols]
                self.off += cols * sz
                assert self.off <= 46080, self.off
                return ap

        cv = Carver()
        XT = cv.take(4 * D, F32)
        XNB = cv.take(2 * D, BF16)
        XF = cv.take(2 * D, BF16)
        PHS = cv.take(2 * L, BF16)
        GES = cv.take(2 * D, BF16)
        W_gg = cv.take(8 * D, BF16)

        cv = Carver()
        XRAW = cv.take(2 * D, F32)
        XDN = cv.take(4 * D, BF16)
        DFB = cv.take(2 * 8 * 512, BF16)
        MDP = cv.take(2 * D, F32)
        cv = Carver()
        XC = cv.take(4 * D, F32)
        RB1 = W_1[:].__getitem__((slice(None), slice(0, 8 * D)))
        RB2 = W_o[:].__getitem__((slice(None), slice(0, 8 * D)))
        OAC = W_0.bitcast(F32).ap()[:, 0:4 * D]
        EO = AR_B.ap()[:, 20480:20480 + 2 * D]

        def EOSLOT(gs):
            if gs < 8:
                return EO[:, (gs % 2) * D:(gs % 2 + 1) * D]
            th_off = {8: 0, 9: 1024, 10: 4096, 11: 5120,
                      12: 2048, 13: 3072, 14: 6144, 15: 7168}[gs]
            return TH[:, th_off:th_off + D]
        OAC2 = AR_F.ap()[:, 4 * D:8 * D]
        XC2 = W_o.bitcast(F32).ap()[:, 4 * D:8 * D]
        MDPS = (MDP[:, 0:D], MDP[:, D:2 * D], T1S[:], MAR[:])

        PTR = psum("PTR", [128, 1024], BF16)
        CH = psum("CH", [128, 512], F32)
        BG0 = psum("BG0", [128, 512], F32)
        BG1 = psum("BG1", [128, 512], F32)
        PM0 = psum("PM0", [128, 512], F32)
        PM1 = psum("PM1", [128, 512], F32)
        PM2 = psum("PM2", [128, 512], F32)
        PM3 = psum("PM3", [128, 512], F32)

        sX = mksem("sX")
        sX1 = mksem("sX1")
        sX2 = mksem("sX2")
        sX3 = mksem("sX3")
        sWb = mksem("sWb")
        sR0 = mksem("sR0")
        sR1 = mksem("sR1")
        sR2 = mksem("sR2")
        sR3 = mksem("sR3")
        sR4 = mksem("sR4")
        sR5 = mksem("sR5")
        sR6 = mksem("sR6")
        sR7 = mksem("sR7")
        sST1 = mksem("sST1")
        sD4 = [mksem(f"sD{j}") for j in range(4)]
        sK = mksem("sK")
        sW = mksem("sW")
        sV = mksem("sV")
        sA = mksem("sA")
        sT = mksem("sT")
        sG = mksem("sG")
        sI = mksem("sI")
        sC = mksem("sC")
        sE = mksem("sE")
        sE1 = mksem("sE1")
        sE2 = mksem("sE2")
        sE3 = mksem("sE3")
        sST = mksem("sST")
        sPd = mksem("sPd")
        sPd1 = mksem("sPd1")
        sPd2 = mksem("sPd2")
        sPd3 = mksem("sPd3")
        sST2 = mksem("sST2")
        sST3 = mksem("sST3")
        sST4 = mksem("sST4")
        sST5 = mksem("sST5")
        sST6 = mksem("sST6")
        sST7 = mksem("sST7")
        sMd = mksem("sMd")
        sPc = [mksem(f"sPc{j}") for j in range(8)]
        ALLSEMS = [sX, sX1, sX2, sX3, sK, sW, sWb, sR0, sR1, sR2, sR3, sR4, sR5, sR6, sR7, sST1, sST2, sST3, sST4, sST5, sST6, sST7, sV, sA, sT, sG, sI, sC, sE, sE1, sE2, sE3, sST, sPd, sPd1, sPd2, sPd3, sMd] + sD4 + sPc

        block = es.enter_context(nc.Block())

        identt = nc.inline_tensor(ident_bf, "identt")
        ltrit = nc.inline_tensor(ltri_np, "ltrit")
        ltri8t = nc.inline_tensor(ltri8_np, "ltri8t")
        onesrt = nc.inline_tensor(onesr_np, "onesrt")
        onesct = nc.inline_tensor(onesc_np, "onesct")
        io8t = nc.inline_tensor(io8_np, "io8t")
        rv8t = nc.inline_tensor(rv8_np, "rv8t")
        kctt = nc.inline_tensor(kct_np, "kctt")
        for i, val in enumerate([1e-5, -1.0, 1.0, 0.0]):
            nc.const_aps.register(F32, val, K_ct[:, i:i + 1])

        em = Em()

        def t3(ap, e=E):
            return ap.rearrange("p (t e) -> p t e", e=e)

        def col3(ap):
            return ap.rearrange("p (t o) -> p t o", o=1)

        def body_sync(sync):
            for dst, src in ((K_cc[:], ccon), (K_c8[:], ccol8), (K_em[:], cemask),
                             (K_id[:], identt.ap()), (K_lt[:], ltrit.ap()),
                             (K_lt8[:], ltri8t.ap()), (K_or[:], onesrt.ap()),
                             (K_oc[:], onesct.ap()), (K_io[:], io8t.ap()),
                             (K_rv[:], rv8t.ap()), (K_ct[:], kctt.ap())):
                em.inc(sync.dma_start(out=dst, in_=src), sK, 16)
            em.mark("k_done", sK)

            def xload(g):
                if g >= 4:
                    em.wait(sync, f"xnb{g - 4}")
                xls = (sX, sX1, sX2, sX3)[g % 4]
                em.inc(sync.dma_start(out=XT[:, (g % 4) * D:(g % 4 + 1) * D],
                                      in_=xown[g * 128:(g + 1) * 128, :]),
                       xls, 16)
                em.mark(f"x{g}", xls)

            for g in range(NG):
                xload(g)
            em.wait(sync, "ag1")
            em.inc(sync.dma_start(
                out=CTBL[:],
                in_=agout.ap().rearrange("(c w) o -> c (w o)", w=PACKW)[0:8, 0:8]), sK, 16)
            em.mark("ctbl", sK)
            for j in range(2):
                em.wait(sync, f"marq{j}")
                em.inc(sync.dma_start(out=marsrc.ap()[j * 128:(j + 1) * 128, :],
                                      in_=MAR[:]), sE, 16)
                em.mark(f"mars{j}", sE)

            def xaload(g):
                if g in (0, 1):
                    em.wait(sync, f"ge{NG - 1}")
                elif g == 3:
                    em.wait(sync, "mbp1")
                elif g >= 4:
                    em.wait(sync, f"pstore{g - 4}")
                xasem = (sX, sX1, sX2, sX3)[g % 4]
                em.inc(sync.dma_start(out=MDPS[g % 4],
                                      in_=xown[g * 128:(g + 1) * 128, :]),
                       xasem, 16)
                em.mark(f"xa{g}", xasem)

            def pstore(g):
                em.wait(sync, f"attevx{2 * g + 1}")
                psem = (sPd, sPd1, sPd2, sPd3)[g % 4]
                em.inc(sync.dma_start(out=partd.ap()[g * 128:(g + 1) * 128, :],
                                      in_=MDPS[g % 4]), psem, 16)
                em.mark(f"pstore{g}", psem)

            xaload(0)
            xaload(1)
            xaload(2)
            for lc in range(2):
                em.wait(sync, "arm")
                if lc == 1:
                    em.wait(sync, "mbp0")
                em.inc(sync.dma_start(out=MAR[:],
                                      in_=mardst.ap()[lc * 128:(lc + 1) * 128, :]), sMd, 16)
                em.mark(f"md{lc}", sMd)
            xaload(3)
            for qq in range(4):
                for g in range(8 * qq, 8 * qq + 8):
                    pstore(g)
                    if g + 4 < NG:
                        xaload(g + 4)
            for g in range(NG):
                em.wait(sync, f"oacd{g}")
                osrc = (OAC if g % 8 < 4 else OAC2)[:, (g % 4) * D:(g % 4 + 1) * D]
                ssem = (sST, sST1, sST2, sST3, sST4, sST5, sST6, sST7)[g % 8]
                em.inc(sync.dma_start(out=out[g * 128:(g + 1) * 128, :], in_=osrc),
                       ssem, 16)
                em.mark(f"outd{g}", ssem)
            if debug:
                em.wait(sync, "oacd31")
                em.inc(sync.dma_start(out=dbg["d_gate"], in_=GATE[:]), sST, 16)
                em.inc(sync.dma_start(out=dbg["d_sel"].rearrange("p (k g) -> p k g", k=2)[:, 0, :], in_=SEL1[:]), sST, 16)
                em.inc(sync.dma_start(out=dbg["d_sel"].rearrange("p (k g) -> p k g", k=2)[:, 1, :], in_=SEL2[:]), sST, 16)
                em.inc(sync.dma_start(out=dbg["d_rk"].rearrange("p (k g) -> p k g", k=2)[:, 0, :], in_=RK1[:]), sST, 16)
                em.inc(sync.dma_start(out=dbg["d_rk"].rearrange("p (k g) -> p k g", k=2)[:, 1, :], in_=RK2[:]), sST, 16)
                em.inc(sync.dma_start(out=dbg["d_prob"].rearrange("p (k g) -> p k g", k=2)[:, 0, :], in_=P1V[:]), sST, 16)
                em.inc(sync.dma_start(out=dbg["d_prob"].rearrange("p (k g) -> p k g", k=2)[:, 1, :], in_=P2V[:]), sST, 16)
                em.inc(sync.dma_start(out=dbg["d_ctbl"], in_=CTBL[:]), sST, 16)
                em.inc(sync.dma_start(out=dbg["d_gidx"].rearrange("p (k g) -> p k g", k=2)[:, 0, :], in_=GI1[:]), sST, 16)
                em.inc(sync.dma_start(out=dbg["d_gidx"].rearrange("p (k g) -> p k g", k=2)[:, 1, :], in_=GI2[:]), sST, 16)
                em.inc(sync.dma_start(out=dbg["d_tid"], in_=TIDU[:]), sST, 16)
                em.inc(sync.dma_start(out=dbg["d_eout"], in_=eoutd.ap()), sST, 16)
                em.inc(sync.dma_start(out=dbg["d_mb"], in_=MB[:]), sST, 16)
                em.inc(sync.dma_start(out=dbg["d_th"], in_=TH[:]), sST, 16)
            if not em.dry:
                for _s in (sST, sST1, sST2, sST3, sST4, sST5, sST6, sST7):
                    sync.wait_ge(_s.h, _s.v)

        def body_tensor(tensor):
            em.wait(tensor, "k_done")
            em.wait(tensor, "wA_done")

            def TR(g):
                em.wait(tensor, f"xnb{g}")
                if g >= 1:
                    em.wait(tensor, f"xf{g - 1}")
                xb = (g % 2) * D
                ti = None
                for k in range(8):
                    ti = tensor.transpose(PTR[:, k * 128:(k + 1) * 128],
                                          XNB[:, xb + k * 128: xb + (k + 1) * 128],
                                          K_id[:])
                em.inc(ti, sT)
                em.mark(f"tr{g}", sT)

            def TG(g):
                em.wait(tensor, f"xf{g}")
                if g >= 1:
                    em.wait(tensor, f"phev{g - 1}")
                if g >= 8 and g % 8 == 0:
                    em.wait(tensor, f"rkread{g // 8 - 1}")
                xf = XF[:, (g % 2) * D:(g % 2 + 1) * D]
                first = True
                ti = None
                for fc in range(8):
                    for lc in range(2):
                        ti = tensor.matmul(CH[:, lc * 128:(lc + 1) * 128],
                                           W_th[:, fc * L + lc * 128: fc * L + lc * 128 + 128],
                                           xf[:, fc * 128:(fc + 1) * 128],
                                           start=first, stop=False)
                        first = False
                    ti = tensor.matmul(CH[:, 256:256 + E],
                                       xf[:, fc * 128:(fc + 1) * 128],
                                       W_ga[:, fc * E:(fc + 1) * E],
                                       start=False, stop=(fc == 7))
                em.inc(ti, sT)
                em.mark(f"tg{g}", sT)

            def PHI(g):
                em.wait(tensor, f"tgev{g}")
                xf = XF[:, (g % 2) * D:(g % 2 + 1) * D]
                ti = None
                for fc in range(8):
                    ti = tensor.matmul(CH[:, 0:L], xf[:, fc * 128:(fc + 1) * 128],
                                       W_ph[:, fc * L:(fc + 1) * L],
                                       start=(fc == 0), stop=(fc == 7))
                em.inc(ti, sT)
                em.mark(f"ph{g}", sT)

            def GP(g):
                if g >= 1:
                    em.wait(tensor, f"geev{g - 1}")
                xf = XF[:, (g % 2) * D:(g % 2 + 1) * D]
                ti = None
                for nh in range(2):
                    bank = BG0 if nh == 0 else BG1
                    for fc in range(8):
                        ti = tensor.matmul(bank[:], xf[:, fc * 128:(fc + 1) * 128],
                                           W_gg[:, fc * D + nh * 512: fc * D + nh * 512 + 512],
                                           start=(fc == 0), stop=(fc == 7))
                em.inc(ti, sT)
                em.mark(f"ge{g}", sT)

            def MM(g):
                em.wait(tensor, f"phev{g}")
                em.wait(tensor, f"geev{g}")
                pb = (g % 2) * L
                gb = (g % 2) * D
                ti = None
                for lc in range(2):
                    for nh in range(2):
                        pm = (PM0, PM1, PM2, PM3)[lc * 2 + nh]
                        ti = tensor.matmul(pm[:], PHS[:, pb + lc * 128: pb + (lc + 1) * 128],
                                           GES[:, gb + nh * 512: gb + (nh + 1) * 512],
                                           start=(g == 0), stop=(g == NG - 1))
                em.inc(ti, sT)
                em.mark(f"M{g}", sT)

            def CHUNK(q):
                em.wait(tensor, f"S{q}")
                ti = tensor.matmul(CH[0:1, 0:64], K_oc[:], RS_S[:], start=True, stop=True)
                em.inc(ti, sT)
                em.mark(f"cs{q}", sT)
                em.wait(tensor, f"bq{q}")
                tensor.matmul(CH[:, 0:64], K_or[:], RS_BQ[:], start=True, stop=False)
                ti = tensor.matmul(CH[:, 0:64], K_lt[:], RS_S[:], start=False, stop=True)
                em.inc(ti, sT)
                em.mark(f"cb{q}", sT)

            for g in range(NG + 2):
                if 2 <= g:
                    GP(g - 2)
                if g < NG:
                    TR(g)
                if 2 <= g:
                    PHI(g - 2)
                    if (g - 2) % 8 == 7:
                        CHUNK((g - 2) // 8)
                if 1 <= g < NG + 1:
                    TG(g - 1)
                if 3 <= g:
                    MM(g - 3)
            MM(NG - 1)

            em.wait(tensor, f"rkread{NQ - 1}")
            em.wait(tensor, "cnte")
            tensor.matmul(CH[0:1, 0:8], CNTE[:], K_lt8[:], start=True, stop=False)
            tensor.matmul(CH[0:1, 8:9], CNTE[:], K_oc[0:8, 0:1], start=False, stop=False)
            ti = tensor.matmul(CH[0:1, 16:24], K_c8[:, 0:1], CTBL[:],
                               start=False, stop=True)
            em.inc(ti, sT)
            em.mark("prows", sT)
            em.wait(tensor, "prows_sb")
            tensor.matmul(CH[:, 0:8], K_or[:], PROWS[0:1, 16:24], start=True, stop=False)
            tensor.matmul(CH[:, 8:9], K_or[:], PROWS[0:1, 8:9], start=False, stop=False)
            ti = tensor.matmul(CH[:, 16:24], K_or[:], PROWS[0:1, 24:32],
                               start=False, stop=True)
            em.inc(ti, sT)
            em.mark("bcasts", sT)

            def dtr_one(sl):
                em.wait(tensor, f"xdnv{sl}")
                em.wait(tensor, f"dfev{sl - 1}" if sl > 0 else f"xf{NG - 1}")
                xb = (sl % 4) * D
                ti = None
                for k in range(8):
                    ti = tensor.transpose(PTR[:, k * 128:(k + 1) * 128],
                                          XDN[:, xb + k * 128: xb + (k + 1) * 128],
                                          K_id[:])
                em.inc(ti, sT)
                em.mark(f"dtr{sl}", sT)

            def att_unit(u):
                g, nh = u // 2, u % 2
                if u == 0:
                    em.wait(tensor, "pb")
                    em.wait(tensor, "mb")
                elif u >= 3:
                    em.wait(tensor, f"attevx{u - 3}")
                if u in ATT_EO_WAIT:
                    em.wait(tensor, f"eo{ATT_EO_WAIT[u]}")
                bank = (CH, PM2, PM3)[u % 3]
                ti = None
                for lc in range(2):
                    ti = tensor.matmul(bank[:, 0:512],
                                       TH[:, lc * TOK + g * 128: lc * TOK + (g + 1) * 128],
                                       MB[:, lc * D + nh * 512: lc * D + nh * 512 + 512],
                                       start=(lc == 0), stop=(lc == 1))
                em.inc(ti, sT)
                em.mark(f"attx{u}", sT)

            def units_at(qq, fc):
                return ATT_UNITS.get((qq, fc), ())

            for sl in range(4):
                dtr_one(sl)
            for qq in range(4):
                for j in range(4):
                    em.wait(tensor, f"dfev{4 * qq + j}")
                if qq == 0:
                    em.wait(tensor, "wB_done")
                dfb = (qq % 2) * (8 * 512)
                for fc in range(16):
                    b0, b1 = ((BG0, BG1) if fc % 2 == 0 else (PM0, PM1))
                    if fc >= 2:
                        em.wait(tensor, f"inter{qq}_{fc - 2}")
                    elif qq > 0:
                        em.wait(tensor, f"inter{qq - 1}_{14 + fc}")
                        if fc == 0:
                            em.wait(tensor, f"eo{4 * qq - 1}")
                    elif qq == 0 and fc < 2:
                        em.wait(tensor, "marq1")
                        em.wait(tensor, f"geev{NG - 1}")
                    for dc in range(8):
                        tensor.matmul(b0[:],
                                      W_0[:, dc * FF + fc * 128: dc * FF + fc * 128 + 128],
                                      DFB[:, dfb + dc * 512: dfb + dc * 512 + 512],
                                      start=(dc == 0), stop=(dc == 7))
                    ti = None
                    for dc in range(8):
                        ti = tensor.matmul(b1[:],
                                           W_1[:, dc * FF + fc * 128: dc * FF + fc * 128 + 128],
                                           DFB[:, dfb + dc * 512: dfb + dc * 512 + 512],
                                           start=(dc == 0), stop=(dc == 7))
                    em.inc(ti, sT)
                    em.mark(f"h{qq}_{fc}", sT)
                    for u in units_at(qq, fc):
                        att_unit(u)
                em.wait(tensor, f"inter{qq}_15")
                for s2 in range(4):
                    gs = qq * 4 + s2
                    if gs > 1:
                        em.wait(tensor, f"eo{gs - 2}")
                    if s2 == 0:
                        em.wait(tensor, f"attevx{16 * qq + 15}")
                    ti = None
                    for nh in range(2):
                        bank = (PM2, PM3)[nh] if gs % 2 == 0 else (BG0, BG1)[nh]
                        for fc in range(16):
                            ti = tensor.matmul(
                                bank[:],
                                INTER[:, fc * 512 + s2 * 128: fc * 512 + s2 * 128 + 128],
                                W_o[:, fc * D + nh * 512: fc * D + nh * 512 + 512],
                                start=(fc == 0), stop=(fc == 15))
                    em.inc(ti, sT)
                    em.mark(f"omm{gs}", sT)
                if qq < 3:
                    for sl in range(4 * (qq + 1), 4 * (qq + 1) + 4):
                        dtr_one(sl)

        def body_vector(v):
            def se(inst):
                em.se(v, sV, inst)
            em.wait(v, "k_done")
            em.wait(v, "iotas")
            se(v.memset(RBASE[:], 0.0))
            se(v.tensor_scalar(GTID[:], GTID[:], K_cc[:, 0:1], None, ALU.add))
            em.mark("gtid", sV)

            def stats_work(g):
                gb = (g % 4) * D
                b = (g % 4) * 8
                em.wait(v, f"x{g}")
                se(v.reduce_sum(STAT[:, b:b + 1], XT[:, gb:gb + D], axis=AX))
                em.mark(f"sum{g}", sV)
                em.wait(v, f"ssq{g}")
                se(v.tensor_scalar(STAT[:, b + 2:b + 3], STAT[:, b:b + 1], -1.0 / D, None, ALU.mult))
                se(v.tensor_mul(STAT[:, b + 3:b + 4], STAT[:, b + 2:b + 3], STAT[:, b + 2:b + 3]))
                se(v.scalar_tensor_tensor(STAT[:, b + 4:b + 5], STAT[:, b + 1:b + 2], 1.0 / D,
                                          STAT[:, b + 3:b + 4], ALU.mult, ALU.subtract))
                em.mark(f"var{g}", sV)
                em.wait(v, f"sqrt{g}")
                se(v.reciprocal(STAT[:, b + 6:b + 7], STAT[:, b + 5:b + 6]))
                if g >= 2:
                    em.wait(v, f"tr{g - 2}")
                se(v.tensor_scalar(XNB[:, (g % 2) * D:(g % 2 + 1) * D], XT[:, gb:gb + D],
                                   STAT[:, b + 2:b + 3],
                                   STAT[:, b + 6:b + 7], ALU.add, ALU.mult))
                em.mark(f"xnb{g}", sV)

            def xfev(g):
                em.wait(v, f"tr{g}")
                if g >= 2:
                    em.wait(v, f"ge{g - 2}")
                se(v.tensor_copy(XF[:, (g % 2) * D:(g % 2 + 1) * D], PTR[:]))
                em.mark(f"xf{g}", sV)

            def tgev(g):
                em.wait(v, f"tg{g}")
                se(v.tensor_copy(GATE[:, g * E:(g + 1) * E], CH[:, 256:256 + E]))
                se(v.tensor_copy(TH[:, g * 128:(g + 1) * 128], CH[:, 0:128]))
                se(v.tensor_copy(TH[:, TOK + g * 128: TOK + (g + 1) * 128],
                                 CH[:, 128:256]))
                em.mark(f"tgev{g}", sV)

            def phev(g):
                em.wait(v, f"ph{g}")
                if g >= 2:
                    em.wait(v, f"M{g - 2}")
                se(v.tensor_copy(PHS[:, (g % 2) * L:(g % 2 + 1) * L], CH[:, 0:L]))
                em.mark(f"phev{g}", sV)

            def chunk_work(q):
                    gsl = GATE[:, q * 64:(q + 1) * 64]
                    g3 = t3(gsl)
                    io3 = t3(K_io[:])
                    S1 = SEL1[:, q * 8:(q + 1) * 8]
                    S2 = SEL2[:, q * 8:(q + 1) * 8]
                    se(v.tensor_reduce(col3(RS_A[:]), g3, AX, ALU.max))
                    se(v.tensor_tensor(t3(RS_M1[:]), g3,
                                       col3(RS_A[:]).broadcast_to([128, 8, E]),
                                       ALU.is_equal))
                    se(v.tensor_mul(RS_T[:], RS_M1[:], K_rv[:]))
                    se(v.tensor_reduce(col3(RS_B[:]), t3(RS_T[:]), AX, ALU.max))
                    se(v.tensor_scalar(S1, RS_B[:], -1.0, 7.0, ALU.mult, ALU.add))
                    se(v.tensor_tensor(t3(RS_M1[:]), io3,
                                       col3(S1).broadcast_to([128, 8, E]), ALU.is_equal))
                    se(v.scalar_tensor_tensor(RS_G2[:], RS_M1[:], -1e30, gsl,
                                              ALU.mult, ALU.add))
                    se(v.tensor_reduce(col3(RS_A[:]), t3(RS_G2[:]), AX, ALU.max))
                    se(v.tensor_reduce(col3(RS_B[:]), g3, AX, ALU.max))
                    se(v.tensor_sub(P1V[:, q * 8:(q + 1) * 8], RS_B[:], RS_A[:]))
                    em.mark(f"dl{q}", sV)
                    se(v.tensor_tensor(t3(RS_M2[:]), t3(RS_G2[:]),
                                       col3(RS_A[:]).broadcast_to([128, 8, E]),
                                       ALU.is_equal))
                    se(v.tensor_mul(RS_T[:], RS_M2[:], K_rv[:]))
                    se(v.tensor_reduce(col3(RS_B[:]), t3(RS_T[:]), AX, ALU.max))
                    se(v.tensor_scalar(S2, RS_B[:], -1.0, 7.0, ALU.mult, ALU.add))
                    se(v.tensor_tensor(t3(RS_M2[:]), io3,
                                       col3(S2).broadcast_to([128, 8, E]), ALU.is_equal))
                    se(v.tensor_add(RS_S[:], RS_M1[:], RS_M2[:]))
                    em.mark(f"S{q}", sV)
                    em.wait(v, f"cs{q}")
                    se(v.tensor_copy(RS_R[:], CH[0:1, 0:64]))
                    bufs = (RS_R, RS_R2)
                    for step, sh in enumerate((1, 2, 4)):
                        a = bufs[step % 2][:].rearrange("p (t e) -> p t e", e=E)
                        b = bufs[(step + 1) % 2][:].rearrange("p (t e) -> p t e", e=E)
                        se(v.tensor_copy(b[:, 0:sh, :], a[:, 0:sh, :]))
                        se(v.tensor_add(b[:, sh:8, :], a[:, sh:8, :], a[:, 0:8 - sh, :]))
                    PFX = bufs[1]
                    r3 = PFX[:].rearrange("p (t e) -> p t e", e=E)
                    bq3 = RS_BQ[:].rearrange("p (t e) -> p t e", e=E)
                    se(v.tensor_copy(bq3[:, 0:1, :],
                                     RBASE[:].rearrange("p (o e) -> p o e", e=E)))
                    se(v.tensor_add(bq3[:, 1:8, :], r3[:, 0:7, :],
                                    RBASE[:].rearrange("p (o e) -> p o e", e=E)
                                    .broadcast_to([1, 7, E])))
                    se(v.tensor_add(RBASE[:], RBASE[:], PFX[:, 56:64]))
                    em.mark(f"bq{q}", sV)
                    em.wait(v, f"cb{q}")
                    se(v.tensor_mul(RS_T[:], RS_M1[:], CH[:, 0:64]))
                    se(v.tensor_reduce(col3(RK1[:, q * 8:(q + 1) * 8]), t3(RS_T[:]),
                                       AX, ALU.add))
                    se(v.tensor_mul(RS_T[:], RS_M2[:], CH[:, 0:64]))
                    se(v.tensor_reduce(col3(RK2[:, q * 8:(q + 1) * 8]), t3(RS_T[:]),
                                           AX, ALU.add))
                    em.mark(f"rkread{q}", sV)
                    se(v.scalar_tensor_tensor(DSF[:], S1, 8192.0,
                                              RK1[:, q * 8:(q + 1) * 8], ALU.mult, ALU.add))
                    se(v.tensor_scalar(DSF[:], DSF[:], 16.0, None, ALU.add))
                    se(v.tensor_copy(DST1[:, q * 8:(q + 1) * 8], DSF[:]))
                    se(v.scalar_tensor_tensor(DSF[:], S2, 8192.0,
                                              RK2[:, q * 8:(q + 1) * 8], ALU.mult, ALU.add))
                    se(v.tensor_scalar(DSF[:], DSF[:], 16.0, None, ALU.add))
                    se(v.tensor_copy(DST2[:, q * 8:(q + 1) * 8], DSF[:]))
                    em.mark(f"pack{q}", sV)


            for g in range(NG):
                stats_work(g)
                if g >= 1:
                    xfev(g - 1)
                if g >= 2:
                    tgev(g - 2)
                    phev(g - 2)
                    if (g - 2) % 8 == 7:
                        chunk_work((g - 2) // 8)
            xfev(NG - 1)
            tgev(NG - 2)
            phev(NG - 2)
            tgev(NG - 1)
            phev(NG - 1)
            chunk_work(NQ - 1)

            em.wait(v, f"M{NG - 1}")
            for j in range(2):
                if j > 0:
                    em.wait(v, f"mars{j - 1}")
                se(v.tensor_copy(MAR[:, 0:512], (PM0, PM1, PM2, PM3)[j * 2 + 0][:]))
                se(v.tensor_copy(MAR[:, 512:1024], (PM0, PM1, PM2, PM3)[j * 2 + 1][:]))
                em.mark(f"marq{j}", sV)

            em.wait(v, "ctbl")
            se(v.tensor_mul(SM1[:], CTBL[:], K_em[0:8, :]))
            se(v.reduce_sum(CNTE[:], SM1[:], axis=AX))
            em.mark("cnte", sV)
            em.wait(v, "prows")
            se(v.tensor_copy(PROWS[0:1, 0:24], CH[0:1, 0:24]))
            se(v.tensor_scalar(PROWS[0:1, 24:32], PROWS[0:1, 0:8], -1.0, None,
                                   ALU.mult))
            em.mark("prows_sb", sV)
            em.wait(v, "bcasts")
            se(v.tensor_copy(PBV[:, 0:24], CH[:, 0:24]))
            em.mark("pb", sV)

            def dA(s):
                se(v.tensor_scalar(DSF[:], PBV[:, 16:24], RALL[:, s:s + 1], None,
                                   ALU.add))
                se(v.tensor_scalar(RS_A[:], DSF[:], 0.0, None, ALU.is_ge))
                se(v.tensor_reduce(STAT[:, 9:10], RS_A[:], AX, ALU.add))
                se(v.scalar_tensor_tensor(RS_B[:], RS_A[:], -BIG, DSF[:],
                                          ALU.mult, ALU.add))
                se(v.tensor_reduce(STAT[:, 10:11], RS_B[:], AX, ALU.min))
                se(v.tensor_scalar(STAT[:, 10:11], STAT[:, 10:11], BIG, None, ALU.add))
                se(v.tensor_scalar(STAT[:, 9:10], STAT[:, 9:10], -1.0, None, ALU.add))
                se(v.scalar_tensor_tensor(STAT[:, 11:12], STAT[:, 9:10], float(PACKW),
                                          STAT[:, 10:11], ALU.mult, ALU.add))
                se(v.tensor_scalar(STAT[:, 11:12], STAT[:, 11:12], K_cc[:, 1:2],
                                   None, ALU.add))
                se(v.tensor_copy(SRCU[:, s:s + 1], STAT[:, 11:12]))
                se(v.tensor_scalar(VLD[:, s:s + 1], RALL[:, s:s + 1], PBV[:, 8:9],
                                       None, ALU.is_lt))
                em.mark(f"src{s}", sV)

            def dB(s):
                em.wait(v, f"idg{s}")
                se(v.tensor_mul(STAT[:, 12:13], TIDF[:, s:s + 1], VLD[:, s:s + 1]))
                se(v.scalar_tensor_tensor(STAT[:, 13:14], VLD[:, s:s + 1], -ZROW,
                                          STAT[:, 12:13], ALU.mult, ALU.add))
                se(v.tensor_scalar(STAT[:, 13:14], STAT[:, 13:14], ZROW, None, ALU.add))
                se(v.tensor_copy(TIDU[:, s:s + 1], STAT[:, 13:14]))
                em.mark(f"tid{s}", sV)

            def dC(s):
                rb = (s % 2) * D
                em.wait(v, f"rowg{s}")
                se(v.reduce_sum(STAT[:, 0:1], XRAW[:, rb:rb + D], axis=AX))
                em.mark(f"dsum{s}", sV)
                em.wait(v, f"dssq{s}")
                se(v.tensor_scalar(STAT[:, 2:3], STAT[:, 0:1], -1.0 / D, None, ALU.mult))
                se(v.tensor_mul(STAT[:, 3:4], STAT[:, 2:3], STAT[:, 2:3]))
                se(v.scalar_tensor_tensor(STAT[:, 4:5], STAT[:, 1:2], 1.0 / D,
                                              STAT[:, 3:4], ALU.mult, ALU.subtract))
                em.mark(f"dvar{s}", sV)

            def dD(s):
                rb = (s % 2) * D
                em.wait(v, f"dsqrt{s}")
                se(v.reciprocal(STAT[:, 6:7], STAT[:, 5:6]))
                if s in (0, 1):
                    em.wait(v, f"xnb{NG - 1}")
                elif s == 2:
                    em.wait(v, f"tr{NG - 1}")
                elif s == 3:
                    em.wait(v, f"tr{NG - 1}")
                else:
                    em.wait(v, f"dtr{s - 4}")
                xb = (s % 4) * D
                se(v.tensor_scalar(XDN[:, xb:xb + D], XRAW[:, rb:rb + D],
                                       STAT[:, 2:3], STAT[:, 6:7], ALU.add, ALU.mult))
                em.mark(f"xdnv{s}", sV)

            def dE(sl):
                em.wait(v, f"dtr{sl}")
                dfb = ((sl // 4) % 2) * (8 * 512)
                off = (sl % 4) * 128
                dst = DFB[:, dfb:dfb + 8 * 512].rearrange(
                    "p (k c) -> p k c", c=512)[:, :, off:off + 128]
                src = PTR[:].rearrange("p (k c) -> p k c", c=128)
                se(v.tensor_copy(dst, src))
                em.mark(f"dfev{sl}", sV)

            def inter_one(qq, fc):
                b1 = BG1 if fc % 2 == 0 else PM1
                em.wait(v, f"sil{qq}_{fc}")
                if qq > 0 and fc == 0:
                    em.wait(v, f"omm{qq * 4 - 1}")
                se(v.tensor_tensor(INTER[:, fc * 512:(fc + 1) * 512], SIL[:],
                                   b1[:], ALU.mult))
                em.mark(f"inter{qq}_{fc}", sV)

            def eo_one(qq, s2):
                gs = qq * 4 + s2
                em.wait(v, f"omm{gs}")
                if 2 <= gs < 8:
                    em.wait(v, f"eod{gs - 2}")
                dst = EOSLOT(gs)
                b0, b1 = (PM2, PM3) if gs % 2 == 0 else (BG0, BG1)
                se(v.tensor_copy(dst[:, 0:512], b0[:]))
                se(v.tensor_copy(dst[:, 512:1024], b1[:]))
                em.mark(f"eo{gs}", sV)

            def mbp_one(lc):
                em.wait(v, f"md{lc}")
                se(v.tensor_scalar(MB[:, lc * D:(lc + 1) * D], MAR[:], 1.0 / 32.0,
                                   None, ALU.mult))
                em.mark(f"mbp{lc}", sV)
                if lc == 1:
                    em.mark("mb", sV)

            def attevx_one(u):
                g, nh = u // 2, u % 2
                em.wait(v, f"attx{u}")
                if nh == 0:
                    em.wait(v, f"xa{g}")
                bank = (CH, PM2, PM3)[u % 3]
                dst = MDPS[g % 4][:, nh * 512: nh * 512 + 512]
                se(v.tensor_tensor(dst, dst, bank[:, 0:512], ALU.add))
                em.mark(f"attevx{u}", sV)

            for f in (lambda: dA(0), lambda: dA(1), lambda: dA(2), lambda: dA(3),
                      lambda: dB(0), lambda: dB(1), lambda: dC(0), lambda: dB(2),
                      lambda: dD(0), lambda: dC(1), lambda: dB(3), lambda: dD(1),
                      lambda: dC(2), lambda: dD(2), lambda: dC(3), lambda: dD(3),
                      lambda: dE(0), lambda: dE(1), lambda: dE(2), lambda: dE(3)):
                f()
            for k, (SEL, RK, GI) in enumerate(((SEL1, RK1, GI1), (SEL2, RK2, GI2))):
                for e in range(E):
                    se(v.tensor_scalar(RS_M1[:, 0:NG], SEL[:], float(e), None,
                                       ALU.is_equal))
                    if e == 0:
                        se(v.tensor_scalar(RS_M2[:, 0:NG], RS_M1[:, 0:NG],
                                           PBV[:, 0:1], None, ALU.mult))
                    else:
                        se(v.scalar_tensor_tensor(RS_M2[:, 0:NG], RS_M1[:, 0:NG],
                                                  PBV[:, e:e + 1], RS_M2[:, 0:NG],
                                                  ALU.mult, ALU.add))
                se(v.tensor_add(RK[:], RK[:], RS_M2[:, 0:NG]))
                se(v.tensor_scalar_min(RS_M1[:, 0:NG], RK[:], float(CAP - 1)))
                se(v.tensor_scalar(RS_T[:, 0:NG], RS_M1[:, 0:NG], 640.0, None, ALU.is_ge))
                se(v.tensor_scalar(RS_S[:, 0:NG], RS_T[:, 0:NG], 640.0, None, ALU.mult))
                se(v.tensor_scalar(RS_T[:, 0:NG], RS_M1[:, 0:NG], 1280.0, None, ALU.is_ge))
                se(v.scalar_tensor_tensor(RS_S[:, 0:NG], RS_T[:, 0:NG], 640.0,
                                          RS_S[:, 0:NG], ALU.mult, ALU.add))
                se(v.tensor_scalar(RS_G2[:, 0:NG], RS_M1[:, 0:NG], 1792.0, None, ALU.is_ge))
                se(v.scalar_tensor_tensor(RS_S[:, 0:NG], RS_G2[:, 0:NG], 512.0,
                                          RS_S[:, 0:NG], ALU.mult, ALU.add))
                se(v.tensor_scalar(RS_M2[:, 0:NG], RS_T[:, 0:NG], -128.0, 640.0,
                                   ALU.mult, ALU.add))
                se(v.scalar_tensor_tensor(RS_M2[:, 0:NG], RS_G2[:, 0:NG], -256.0,
                                          RS_M2[:, 0:NG], ALU.mult, ALU.add))
                se(v.tensor_mul(RS_M2[:, 0:NG], RS_M2[:, 0:NG], SEL[:]))
                se(v.scalar_tensor_tensor(RS_M1[:, 0:NG], RS_S[:, 0:NG], 7.0,
                                          RS_M1[:, 0:NG], ALU.mult, ALU.add))
                se(v.tensor_add(RS_M1[:, 0:NG], RS_M1[:, 0:NG], RS_M2[:, 0:NG]))
                se(v.tensor_copy(GI[:], RS_M1[:, 0:NG]))
                em.mark(f"gidx{k}", sV)

            def units_at_v(qq, fc):
                return ATT_UNITS.get((qq, fc), ())

            for qq in range(4):
                S = [4 * (qq + 1) + j for j in range(4)] if qq < 3 else None
                sched = []
                for fc in range(16):
                    sched.append(("i", fc))
                    for u in units_at_v(qq, fc):
                        sched.append(("X", u))
                    if qq == 0 and fc == 5:
                        sched.append(("M", 0))
                    if qq == 0 and fc == 6:
                        sched.append(("M", 1))
                    if S is not None:
                        if fc == 0:
                            sched.append(("A", S[0]))
                        elif fc == 1:
                            sched.append(("A", S[1]))
                        elif fc == 3:
                            sched.append(("B", S[0]))
                        elif fc == 4:
                            sched.append(("A", S[2]))
                        elif fc == 5:
                            sched.append(("B", S[1]))
                        elif fc == 6:
                            sched.append(("C", S[0]))
                        elif fc == 7:
                            sched.append(("A", S[3]))
                        elif fc == 8:
                            sched.append(("D", S[0]))
                            sched.append(("B", S[2]))
                        elif fc == 9:
                            sched.append(("C", S[1]))
                        elif fc == 10:
                            sched.append(("D", S[1]))
                            sched.append(("B", S[3]))
                        elif fc == 11:
                            sched.append(("C", S[2]))
                        elif fc == 12:
                            sched.append(("D", S[2]))
                        elif fc == 13:
                            sched.append(("C", S[3]))
                        elif fc == 14:
                            sched.append(("D", S[3]))
                for kind, val in sched:
                    if kind == "i":
                        inter_one(qq, val)
                    elif kind == "X":
                        attevx_one(val)
                    elif kind == "M":
                        mbp_one(val)
                    elif kind == "A":
                        dA(val)
                    elif kind == "B":
                        dB(val)
                    elif kind == "C":
                        dC(val)
                    elif kind == "D":
                        dD(val)
                for s2 in range(4):
                    eo_one(qq, s2)
                if S is not None:
                    for j in range(4):
                        dE(S[j])

            em.wait(v, f"sig{NQ - 1}")
            for g in range(NG):
                ob = (g % 4) * D
                rb = (g % 8) * D
                xb = (g % 4) * D
                XCS = XC if g % 8 < 4 else XC2
                OCS = OAC if g % 8 < 4 else OAC2
                em.wait(v, f"pc{g}")
                em.wait(v, f"rbg{g}")
                if g >= 8:
                    em.wait(v, f"outd{g - 8}")
                se(v.scalar_tensor_tensor(OCS[:, ob:ob + D], RB1[:, rb:rb + D],
                                          P1V[:, g:g + 1], XCS[:, xb:xb + D],
                                          ALU.mult, ALU.add))
                em.mark(f"xcr{g}", sV)
                se(v.scalar_tensor_tensor(OCS[:, ob:ob + D], RB2[:, rb:rb + D],
                                          P2V[:, g:g + 1], OCS[:, ob:ob + D],
                                          ALU.mult, ALU.add))
                em.mark(f"oacd{g}", sV)

        def body_scalar(scalar):
            def sse(inst):
                em.se(scalar, sA, inst)
            em.wait(scalar, "k_done")
            def sig_work(q):
                em.wait(scalar, f"dl{q}")
                sse(scalar.activation(P1V[:, q * 8:(q + 1) * 8],
                                      P1V[:, q * 8:(q + 1) * 8], AF.Sigmoid))
                sse(scalar.activation(P2V[:, q * 8:(q + 1) * 8],
                                      P1V[:, q * 8:(q + 1) * 8],
                                      AF.Copy, scale=-1.0, bias=1.0))
                em.mark(f"sig{q}", sA)

            def geev(g):
                em.wait(scalar, f"ge{g}")
                if g >= 2:
                    em.wait(scalar, f"M{g - 2}")
                gb = (g % 2) * D
                sse(scalar.activation(GES[:, gb:gb + 512], BG0[:], AF.Copy))
                sse(scalar.activation(GES[:, gb + 512:gb + D], BG1[:], AF.Copy))
                em.mark(f"geev{g}", sA)

            for g in range(NG):
                gb = (g % 4) * D
                b = (g % 4) * 8
                em.wait(scalar, f"x{g}")
                if g >= 3:
                    em.wait(scalar, f"xnb{g - 3}")
                elif g == 0:
                    em.wait(scalar, "pzero")
                sse(scalar.activation(SNK[:], XT[:, gb:gb + D], AF.Square,
                                      accum_out=STAT[:, b + 1:b + 2]))
                em.mark(f"ssq{g}", sA)
                em.wait(scalar, f"var{g}")
                sse(scalar.activation(STAT[:, b + 5:b + 6], STAT[:, b + 4:b + 5], AF.Sqrt, bias=1e-5))
                em.mark(f"sqrt{g}", sA)
                if g >= 2:
                    geev(g - 2)
                if g % 8 == 2 and g >= 10:
                    sig_work((g - 10) // 8)
            geev(NG - 2)
            geev(NG - 1)
            sig_work(NQ - 1)
            def sil_one(qq, fc):
                b0 = BG0 if fc % 2 == 0 else PM0
                em.wait(scalar, f"h{qq}_{fc}")
                if fc >= 1:
                    em.wait(scalar, f"inter{qq}_{fc - 1}")
                elif qq > 0:
                    em.wait(scalar, f"inter{qq - 1}_15")
                sse(scalar.activation(SIL[:], b0[:], AF.Silu))
                em.mark(f"sil{qq}_{fc}", sA)

            def dssq_one(s):
                rb = (s % 2) * D
                em.wait(scalar, f"rowg{s}")
                if s > 0:
                    em.wait(scalar, f"dvar{s - 1}")
                sse(scalar.activation(SNK[:], XRAW[:, rb:rb + D], AF.Square,
                                      accum_out=STAT[:, 1:2]))
                em.mark(f"dssq{s}", sA)

            def dsqrt_one(s):
                em.wait(scalar, f"dvar{s}")
                sse(scalar.activation(STAT[:, 5:6], STAT[:, 4:5], AF.Sqrt, bias=1e-5))
                em.mark(f"dsqrt{s}", sA)

            for s in range(4):
                dssq_one(s)
                dsqrt_one(s)
            def eod_one(gs):
                em.wait(scalar, f"eo{gs}")
                if gs >= 4:
                    em.wait(scalar, f"eod{gs - 4}")
                esem = (sE, sE1, sE2, sE3)[gs % 4]
                em.inc(scalar.dma_start(out=eoutd.ap()[gs * 128:(gs + 1) * 128, :],
                                        in_=EOSLOT(gs)), esem, 16)
                em.mark(f"eod{gs}", esem)

            for qq in range(4):
                S = [4 * (qq + 1) + j for j in range(4)] if qq < 3 else None
                if qq >= 1:
                    for s2 in range(4):
                        eod_one(4 * (qq - 1) + s2)
                for fc in range(16):
                    sil_one(qq, fc)
                    if S is not None:
                        if fc == 6:
                            dssq_one(S[0])
                        elif fc == 7:
                            dsqrt_one(S[0])
                        elif fc == 9:
                            dssq_one(S[1])
                            dsqrt_one(S[1])
                        elif fc == 11:
                            dssq_one(S[2])
                            dsqrt_one(S[2])
                        elif fc == 13:
                            dssq_one(S[3])
                            dsqrt_one(S[3])

            for gs in range(12, 16):
                eod_one(gs)
            for g in range(NG):
                if g == 0:
                    em.wait(scalar, "xdnv14")
                elif g == 1:
                    em.wait(scalar, "xdnv15")
                elif g in (2, 3):
                    em.wait(scalar, "dfev15")
                elif g in (4, 5, 6, 7):
                    em.wait(scalar, "omm15")
                else:
                    em.wait(scalar, f"xcr{g - 8}")
                em.wait(scalar, f"pstore{g}")
                xsem = sPc[g % 8]
                dst = (XC if g % 8 < 4 else XC2)[:, (g % 4) * D:(g % 4 + 1) * D]
                em.inc(scalar.dma_start(out=dst,
                                        in_=partd.ap()[g * 128:(g + 1) * 128, :]),
                       xsem, 16)
                em.mark(f"pc{g}", xsem)

        def body_gpsimd(gp):
            em.inc(gp.memset(SNK[:], 0.0), sG)
            em.mark("zro", sG)
            em.wait(gp, "zro")
            em.inc(gp.dma_start(
                out=packsrc.ap()[16:PACKW, :].rearrange("(p b) o -> p (b o)", p=128),
                in_=SNK[:, 0:512]), sI, 16)
            em.inc(gp.dma_start(
                out=packsrc.ap().rearrange("(a b) o -> a (b o)", a=1)[0:1, 8:16],
                in_=SNK[0:1, 0:8]), sI, 16)
            em.mark("pzero", sI)
            gp.iota(GTID[:], pattern=[[128, NG]], base=0, channel_multiplier=1,
                    allow_small_or_imprecise_dtypes=True)
            em.inc(gp.iota(RALL[:], pattern=[[128, 16]], base=0, channel_multiplier=1,
                           allow_small_or_imprecise_dtypes=True), sG)
            em.mark("iotas", sG)
            em.inc(gp.dma_start(out=W_th.rearrange("p (fc l) -> p fc l", l=L),
                                in_=wthT.rearrange("(fc p) l -> p fc l", p=128)), sW, 16)
            em.inc(gp.dma_start(out=W_ph.rearrange("p (fc l) -> p fc l", l=L),
                                in_=wphT.rearrange("(fc p) l -> p fc l", p=128)), sW, 16)
            em.inc(gp.dma_start(out=W_gg.rearrange("p (fc d) -> p fc d", d=D),
                                in_=wgT.rearrange("(fc p) d -> p fc d", p=128)), sW, 16)
            em.inc(gp.dma_start(out=W_ga.rearrange("p (fc e) -> p fc e", e=E),
                                in_=wgate.rearrange("(fc p) e -> p fc e", p=128)), sW, 16)
            em.mark("wA_done", sW)
            em.inc(gp.dma_start(out=W_0[:].rearrange("p (dc f) -> p dc f", f=FF),
                                in_=w0e.rearrange("(dc p) f -> p dc f", p=128)), sWb, 16)
            em.inc(gp.dma_start(out=W_1[:].rearrange("p (dc f) -> p dc f", f=FF),
                                in_=w1e.rearrange("(dc p) f -> p dc f", p=128)), sWb, 16)
            em.inc(gp.dma_start(out=W_o[:].rearrange("p (fc d) -> p fc d", d=D),
                                in_=woe.rearrange("(fc p) d -> p fc d", p=128)), sWb, 16)
            em.mark("wB_done", sWb)
            em.wait(gp, "pzero")
            for q in range(NQ):
                em.wait(gp, f"pack{q}")
                for g in range(8 * q, 8 * q + 8):
                    em.inc(gp.indirect_dma_start(
                        out=packsrc.ap(),
                        out_offset=IndirectOffsetOnAxis(ap=DST1[:, g:g + 1], axis=0),
                        in_=GTID[:, g:g + 1], in_offset=None), sI, 16)
                    em.inc(gp.indirect_dma_start(
                        out=packsrc.ap(),
                        out_offset=IndirectOffsetOnAxis(ap=DST2[:, g:g + 1], axis=0),
                        in_=GTID[:, g:g + 1], in_offset=None), sI, 16)
            em.inc(gp.dma_start(
                out=packsrc.ap().rearrange("(a b) o -> a (b o)", a=1)[0:1, 0:8],
                in_=RBASE[:]), sI, 16)
            em.mark("packall", sI)
            if not em.dry:
                gp.wait_ge(sI.h, sI.v)
            em.inc(gp.collective_compute("AllGather", ALU.bypass, replica_groups=RGRP,
                                         ins=[packsrc.ap().opt()],
                                         outs=[agout.ap().opt()]), sC)
            em.mark("ag1", sC)
            em.wait(gp, "mars1")
            em.inc(gp.collective_compute("AllReduce", ALU.add, replica_groups=RGRP2,
                                         ins=[marsrc.ap().opt()],
                                         outs=[mardst.ap().opt()]), sC)
            em.mark("arm", sC)
            def idg_one(s):
                em.wait(gp, f"src{s}")
                em.inc(gp.indirect_dma_start(
                    out=TIDF[:, s:s + 1], out_offset=None,
                    in_=agout.ap(),
                    in_offset=IndirectOffsetOnAxis(ap=SRCU[:, s:s + 1], axis=0)),
                       sD4[s % 4], 16)
                em.mark(f"idg{s}", sD4[s % 4])

            def rowg_one(s):
                em.wait(gp, f"tid{s}")
                if s >= 2:
                    em.wait(gp, f"xdnv{s - 2}")
                else:
                    em.wait(gp, f"xnb{NG - 1}")
                em.inc(gp.indirect_dma_start(
                    out=XRAW[:, (s % 2) * D:(s % 2 + 1) * D], out_offset=None,
                    in_=xfull,
                    in_offset=IndirectOffsetOnAxis(ap=TIDU[:, s:s + 1], axis=0)),
                       sD4[s % 4], 16)
                em.mark(f"rowg{s}", sD4[s % 4])

            for t in range(4):
                S = [4 * t + j for j in range(4)]
                idg_one(S[0]); idg_one(S[1]); rowg_one(S[0]); idg_one(S[2])
                rowg_one(S[1]); idg_one(S[3]); rowg_one(S[2]); rowg_one(S[3])
            AG2B = (0, 640, 1280, 1792, 2048)
            for qc in range(4):
                lo, hi = AG2B[qc], AG2B[qc + 1]
                ng = hi // 128
                for j in range(4):
                    top = [g for g in range(ng) if g % 4 == j]
                    if top:
                        em.wait(gp, f"eod{top[-1]}")
                em.inc(gp.collective_compute(
                    "AllGather", ALU.bypass, replica_groups=RGRP,
                    ins=[eoutd.ap()[lo:hi, :].opt()],
                    outs=[ag2out.ap()[NCORE * lo:NCORE * hi, :].opt()]), sC)
                em.mark(f"ag2c{qc}", sC)
            em.wait(gp, "ag2c3")
            em.wait(gp, "gidx0")
            em.wait(gp, "gidx1")
            for g in range(NG):
                if g >= 8:
                    em.wait(gp, f"oacd{g - 8}")
                rbs = (g % 8) * D
                sR = (sR0, sR1, sR2, sR3, sR4, sR5, sR6, sR7)[g % 8]
                em.inc(gp.indirect_dma_start(
                    out=RB1[:, rbs:rbs + D], out_offset=None, in_=ag2out.ap(),
                    in_offset=IndirectOffsetOnAxis(ap=GI1[:, g:g + 1], axis=0)), sR, 16)
                em.inc(gp.indirect_dma_start(
                    out=RB2[:, rbs:rbs + D], out_offset=None, in_=ag2out.ap(),
                    in_offset=IndirectOffsetOnAxis(ap=GI2[:, g:g + 1], axis=0)), sR, 16)
                em.mark(f"rbg{g}", sR)
            if not em.dry:
                for _s in (sST, sST1, sST2, sST3, sST4, sST5, sST6, sST7):
                    gp.wait_ge(_s.h, _s.v)

        bodies = (body_sync, body_tensor, body_vector, body_scalar, body_gpsimd)
        stub = StubEng()
        for b in bodies:
            b(stub)
        _DBG["MS"] = {k: (id(v[0]), v[1]) for k, v in em.MS.items()}
        _DBG["SEMS"] = {id(s): n for s, n in zip(ALLSEMS, ("sX sX1 sX2 sX3 sK sW sWb sR0 sR1 sR2 sR3 sR4 sR5 sR6 sR7 sST1 sST2 sST3 sST4 sST5 sST6 sST7 sV sA sT sG sI sC sE sE1 sE2 sE3 sST sPd sPd1 sPd2 sPd3 sMd sD0 sD1 sD2 sD3".split() + [f"sPc{j}" for j in range(8)]))}
        for s in ALLSEMS:
            s.v = 0
        em.dry = False
        block.sync(body_sync)
        block.tensor(body_tensor)
        block.vector(body_vector)
        block.scalar(body_scalar)
        block.gpsimd(body_gpsimd)

    return nc


_CACHE = {}


def _get_nc():
    if "nc" not in _CACHE:
        _CACHE["nc"] = build_nc(debug=False)
    return _CACHE["nc"]


def make_in_maps(L0_attn_x, L0_attn_W_theta, L0_attn_W_phi, L0_attn_W_g,
                 L0_ffn_wg, L0_ffn_w0, L0_ffn_w1, L0_ffn_wo):
    x2d = np.ascontiguousarray(np.asarray(L0_attn_x, np.float32).reshape(32768, D))
    xfull = np.zeros((NROWX, D), np.float32)
    xfull[:32768] = x2d
    wthT = np.ascontiguousarray(np.asarray(L0_attn_W_theta, np.float32).T)
    wphT = np.ascontiguousarray(np.asarray(L0_attn_W_phi, np.float32).T)
    wgT = np.ascontiguousarray(np.asarray(L0_attn_W_g, np.float32).T)
    wgate = np.ascontiguousarray(np.asarray(L0_ffn_wg, np.float32))
    w0 = np.asarray(L0_ffn_w0, np.float32)
    w1 = np.asarray(L0_ffn_w1, np.float32)
    wo = np.asarray(L0_ffn_wo, np.float32)
    in_maps = []
    for c in range(NCORE):
        ccon = np.zeros((128, 8), np.float32)
        ccon[:, 0] = c * TOK
        ccon[:, 1] = c * 8192 + 16
        ccon[:, 2] = 1.0 if c < 4 else 0.0
        ccon[:, 3] = 0.0 if c < 4 else 1.0
        ccon[:, 4] = c
        ccol8 = np.zeros((8, 2), np.float32)
        ccol8[:c, 0] = 1.0
        cem = np.zeros((128, E), np.float32)
        cem[:, c] = 1.0
        in_maps.append({
            "xown": np.ascontiguousarray(x2d[c * TOK:(c + 1) * TOK]),
            "xfull": xfull,
            "wthT": wthT, "wphT": wphT, "wgT": wgT, "wgate": wgate,
            "w0e": np.ascontiguousarray(w0[c]),
            "w1e": np.ascontiguousarray(w1[c]),
            "woe": np.ascontiguousarray(wo[c]),
            "ccon": ccon, "ccol8": ccol8, "cemask": cem,
        })
    return in_maps


def kernel(L0_attn_x, L0_attn_W_theta, L0_attn_W_phi, L0_attn_W_g,
           L0_ffn_wg, L0_ffn_w0, L0_ffn_w1, L0_ffn_wo):
    nc = _get_nc()
    in_maps = make_in_maps(L0_attn_x, L0_attn_W_theta, L0_attn_W_phi, L0_attn_W_g,
                           L0_ffn_wg, L0_ffn_w0, L0_ffn_w1, L0_ffn_wo)
    res = run_bass_kernel_spmd(nc, in_maps, core_ids=list(range(NCORE)))
    outs = np.concatenate([res.results[c]["out"] for c in range(NCORE)], axis=0)
    return outs.reshape(2, 16384, D)
